# version 05
# baseline (speedup 1.0000x reference)
import numpy as np
import ml_dtypes
import concourse.bass as bass
import concourse.mybir as mybir
from concourse.bass_utils import run_bass_kernel_spmd

F32 = mybir.dt.float32
BF16 = mybir.dt.bfloat16
I32 = mybir.dt.int32
AF = mybir.ActivationFunctionType
ALU = mybir.AluOpType
AX = mybir.AxisListType
bfloat16 = ml_dtypes.bfloat16

TPC = 2048
NTT = 16
DM = 1024
DI = 2048
DF = 4096
DFI = 2816
NH0, P0H, N0 = 8, 256, 64
NH1, DH1 = 16, 64
CH = 128
NCH = TPC // CH
EPS0 = float(np.finfo(np.float32).eps)

ENGINES = ("tensor", "vector", "scalar", "gpsimd", "sync")


class Prog:

    def __init__(self, nc):
        self.nc = nc
        self.streams = {e: [] for e in ENGINES}
        self.sems = {}
        self.semcnt = {}
        self.ctxs = []

    def sem(self, name):
        cm = self.nc.semaphore(name)
        h = cm.__enter__()
        self.ctxs.append(cm)
        self.sems[name] = h
        self.semcnt[name] = 0
        return h

    def ev(self, name):
        return (name, self.semcnt[name])

    DMA_SEMS = ["dC", "dR", "dW", "dP", "dK", "dO", "dS", "dV", "dM"]

    def step(self, engine, fn, wait=(), sig=None, n=1):
        wait = [w for w in wait if w is not None]
        byname = {}
        for (s, v) in wait:
            byname[s] = max(byname.get(s, 0), v)
        wait = list(byname.items())
        for (s, v) in wait:
            assert v <= self.semcnt[s], f"wait on future event {s}:{v} > {self.semcnt[s]}"
            if s in self.DMA_SEMS:
                assert v == self.semcnt[s], (
                    f"partial DMA wait {s}:{v} != {self.semcnt[s]} (completions are "
                    f"unordered; wait must cover all issued DMAs on this sem)")
        ev = None
        if sig is not None:
            self.semcnt[sig] += n
            ev = (sig, self.semcnt[sig])
        sems = self.sems
        waits = tuple(wait)

        def emit(eng, fn=fn, waits=waits, signame=sig, n=n):
            for (s, v) in waits:
                if v > 0:
                    eng.wait_ge(sems[s], v)
            inst = fn(eng)
            if signame is not None:
                inst.then_inc(sems[signame], n)
        self.streams[engine].append(emit)
        return ev

    def dma(self, engine, out, in_, wait=(), sig=None):
        return self.step(engine, lambda e: e.dma_start(out=out, in_=in_),
                         wait=wait, sig=sig, n=16)

    def barrier(self, engines=("tensor", "vector", "scalar", "gpsimd", "sync")):
        evs = []
        for e in engines:
            cw = [self.const_ev, ("bar_" + e, self.semcnt["bar_" + e])]
            if e == "vector":
                evs.append(self.step(e, lambda eng: eng.tensor_scalar(
                    out=self.dummy_v[0:1, 0:1], in0=self.const_src, scalar1=0.0, scalar2=None,
                    op0=ALU.mult), wait=cw, sig="bar_" + e))
            elif e == "gpsimd":
                evs.append(self.step(e, lambda eng: eng.tensor_scalar(
                    out=self.dummy_g[0:1, 0:1], in0=self.const_src, scalar1=0.0, scalar2=None,
                    op0=ALU.mult), wait=cw, sig="bar_" + e))
            elif e == "scalar":
                evs.append(self.step(e, lambda eng: eng.mul(
                    out=self.dummy_s[0:1, 0:1], in_=self.const_src, mul=0.0),
                    wait=cw, sig="bar_" + e))
            else:
                evs.append(self.step(e, lambda eng: eng.nop(), wait=cw[1:], sig="bar_" + e))
        dwaits = [(s, self.semcnt[s]) for s in self.DMA_SEMS
                  if s in self.semcnt and self.semcnt[s] > 0]
        for e in engines:
            self.step(e, lambda eng: eng.nop(),
                      wait=[x for x in evs if x[0] != "bar_" + e] + dwaits)

    def finalize(self, block):
        for ename in ENGINES:
            stream = self.streams[ename]
            if not stream:
                continue
            dec = getattr(block, ename)

            def make(stream):
                def _(eng):
                    for f in stream:
                        f(eng)
                return _
            dec(make(stream))


class DmaPipe:

    def __init__(self, p, base, n=2, engine="sync"):
        self.p = p
        self.base = base
        self.n = n
        self.engine = engine
        self.i = 0
        for j in range(n):
            name = base + str(j)
            if name not in p.sems:
                p.sem(name)
                p.DMA_SEMS = tuple(list(p.DMA_SEMS) + [name])

    def load(self, out, in_, wait=()):
        s = self.base + str(self.i % self.n)
        self.i += 1
        return self.p.dma(self.engine, out, in_, wait=wait, sig=s)

    def load_group(self, pairs, wait=()):
        s = self.base + str(self.i % self.n)
        self.i += 1
        ev = None
        for (out, in_) in pairs:
            ev = self.p.dma(self.engine, out, in_, wait=wait, sig=s)
        return ev


class Slots:

    def __init__(self, n):
        self.n = n
        self.busy = [None] * n
        self.i = 0

    def get(self):
        i = self.i % self.n
        self.i += 1
        return i, self.busy[i]

    def done(self, i, ev):
        self.busy[i] = ev


def build_program(debug=(), stage=99, lnsub=9, lntiles=99):
    nc = bass.Bass("TRN2", target_bir_lowering=False)
    g = {}

    def din(name, shape, dt=F32):
        g[name] = nc.dram_tensor(name, list(shape), dt, kind="ExternalInput").ap()

    def dout(name, shape, dt=F32):
        g[name] = nc.dram_tensor(name, list(shape), dt, kind="ExternalOutput").ap()

    def dint(name, shape, dt=F32):
        g[name] = nc.dram_tensor(name, list(shape), dt).ap()

    din("hid", (TPC, DM))
    din("hid_halo", (3, DM))
    din("w_norm_in", (1, DM))
    din("w_in_proj", (DM, 5128))
    din("conv_w", (128, 16 * 4))
    din("conv_b", (128, 16))
    din("w_dt", (8, DI))
    din("b_dt", (128, 16))
    din("A_log", (8, 1))
    din("Dp", (8, 1))
    din("w_norm_out", (1, DI))
    din("w_out_proj", (DI, DM))
    din("w1", (DM, DF))
    din("b1", (128, 32))
    din("b1row", (1, DF))
    din("w2", (DF, DM))
    din("b2", (1, DM))
    din("Wq", (DM, DM)); din("Wk", (DM, DM)); din("Wv", (DM, DM))
    din("Wg", (DM, DM)); din("Wo", (DM, DM))
    din("gn_w", (1, DM)); din("gn_b", (1, DM))
    din("clip_x", (1, 1)); din("clip_down", (1, 1))
    din("w_gate", (DFI, DM)); din("w_up", (DFI, DM)); din("w_down", (DM, DFI))
    din("consts", (128, 8))
    din("eye_bf", (128, 128), BF16)
    din("eye_f32", (128, 128))
    din("ones_f32", (1, 128))
    din("Dp_row", (1, 8))
    din("ssd_premaskT", (8, 8))
    din("ssd_maskinvT", (8, 8))
    din("trilT", (128, 128))
    din("dtmean", (DI, 8), BF16)
    din("w_norm_in_t", (128, 8))
    din("w_norm_out_t", (128, 16))
    din("onesK", (1, 512), BF16)
    din("ropeP", (128, 128), BF16)
    din("cosq_bf", (128, TPC), BF16); din("sinq_bf", (128, TPC), BF16)
    din("cosk_bf", (128, TPC), BF16); din("sink_bf", (128, TPC), BF16)
    din("dmaskT", (128, NH1 * CH), BF16)
    din("gfacT", (128, NH1)); din("gkT", (128, NH1)); din("gCT", (64, NH1))
    din("ret_coef", (64, 8 * NH1))
    dout("out", (TPC, DM))
    dint("w1bf", (DM + 1, DF), BF16)
    dint("w2bf", (DF + 1, DM), BF16)
    dint("ccA_in", (65, TPC), BF16)
    dint("winbf", (DM, 5128), BF16)
    dint("woutbf", (DI, DM), BF16)
    dint("xtok_park", (TPC, DI), BF16)
    dint("wgTint", (DM, DFI), BF16)
    dint("wuTint", (DM, DFI), BF16)
    dint("wdTint", (DFI, DM), BF16)
    dint("hT_park", (DFI, TPC), BF16)
    dint("hq_park", (DFI, TPC), BF16)
    dint("kT_park", (DM, TPC), BF16)
    dint("v_park", (TPC, DM), BF16)
    dint("g_park", (TPC, DM), BF16)
    dint("ccM_in", (1, 16))
    dint("ccM_out", (1, 16))
    dint("ccS_in", (64, 1024), BF16)
    dint("ccS_out", (8 * 64, 1024), BF16)
    dint("xnT1_park", (DM, TPC), BF16)
    dint("scl_park", (128, 144))
    dint("wqbfT", (DM, DM), BF16)
    dint("wkbfT", (DM, DM), BF16)
    dint("wvbfT", (DM, DM), BF16)
    dint("wgbfT", (DM, DM), BF16)
    dint("wobfT", (DM, DM), BF16)
    dint("ccA_out", (8 * 65, TPC), BF16)

    p = Prog(nc)
    from contextlib import ExitStack
    stack = ExitStack()

    def sb(name, shape, dt):
        t = stack.enter_context(nc.sbuf_tensor(name, list(shape), dt))
        g[name] = t
        return t

    def psum(name, shape, dt=F32):
        t = stack.enter_context(nc.psum_tensor(name, list(shape), dt))
        g[name] = t
        return t

    def cvo(buf, nk, w, c0=0):
        return buf[:, c0:c0 + nk * w].rearrange("p (K j) -> p K j", j=w)

    def cvi(dram, k0, nk, c0, w):
        return dram[k0:k0 + nk * 128, c0:c0 + w].rearrange("(K p) n -> p K n", p=128)

    def cvrows(dram, r0, nt, c0, w, pknum=128):
        return dram[r0:r0 + nt * pknum, c0:c0 + w].rearrange("(T p) n -> p T n", p=pknum)

    def cvcols(buf, nt, w, c0=0):
        return buf[:, c0:c0 + nt * w].rearrange("p (T j) -> p T j", j=w)

    for s in ("dC", "dR", "dW", "dP", "dK", "dO", "dS", "dV", "dM", "pe", "act", "dve", "gps", "cc",
              "bar_tensor", "bar_vector", "bar_scalar", "bar_gpsimd", "bar_sync"):
        p.sem(s)

    CP = sb("cpool", (128, 8), F32)
    for _ci, _cv in enumerate([256.0, 0.25, 1.0, 0.5, 4.0, 64.0, 127.0, -128.0]):
        nc.const_aps.register(F32, _cv, CP[:, _ci:_ci + 1])
    p.dummy_v = sb("bardummy_v", (1, 8), F32)
    p.dummy_g = sb("bardummy_g", (1, 8), F32)
    p.dummy_s = sb("bardummy_s", (1, 8), F32)
    EYE = sb("eye", (128, 128), BF16)
    EYEF = sb("eyef", (128, 128), F32)
    ONESK = sb("onesk", (1, 512), BF16)
    ONESF = sb("onesf", (1, 128), F32)
    R = sb("R", (128, NTT * DM), F32)
    small = sb("small", (128, 15), F32)
    bias = sb("biasp", (128, 48), F32)
    aH = nc.auto_arena()
    h1T = aH.sbuf_tensor("h1T", [128, 8 * (TPC + 3)], BF16)
    aH.allocate()
    a12 = nc.auto_arena()
    sb12 = lambda name, shape, dt: a12.sbuf_tensor(name, list(shape), dt)
    xnT = sb12("xnT", (128, 8 * TPC), BF16)
    lnx0 = sb12("lnx0", (128, 1024), F32)
    lnx = [lnx0, lnx0]
    lnsq = sb12("lnsq", (128, 1024), F32)
    pass
    pw_g0 = [sb12(f"pwg0{j}", (128, 512), F32) for j in range(2)]
    pw_g1 = [sb12(f"pwg1{j}", (128, 512), F32) for j in range(2)]
    pw_d = [sb12(f"pwd{j}", (128, 512), F32) for j in range(2)]
    pw_i = [sb12(f"pwi{j}", (128, 512), I32) for j in range(2)]
    pw_j = [sb12(f"pwj{j}", (128, 512), I32) for j in range(2)]
    syT = sb12("syT", (128, 32 * 512), BF16)
    lnxb0 = syT[:, 0:1024]
    lnh0 = syT[:, 1024:2048]
    lnxb = [lnxb0, lnxb0]
    lnh = [lnh0, lnh0]
    wpan0 = sb12("wpan0", (128, 8 * 128 + 128), BF16)
    wpan = [wpan0, wpan0]
    wk2 = [sb12(f"wk2{j}", (128, 1024), BF16) for j in range(2)]
    wprep = sb12("wprep", (128, 1024), F32)
    pw_w = [wprep[:, 0:512], wprep[:, 512:1024]]
    pw_e = pw_g1
    pw_z = pw_w
    wprepb0 = sb12("wprepb0", (128, 1024), BF16)
    wprepb = [wprepb0, wprepb0]
    b2sb = wprepb0
    vout0 = lnsq[:, 0:512]
    vouti0 = sb12("vouti0", (128, 512), I32)
    voutb0 = sb12("voutb0", (128, 512), BF16)
    vout, vouti, voutb = [vout0, vout0], [vouti0, vouti0], [voutb0, voutb0]
    a12.allocate()
    p.arenas = [aH, a12]

    PA = [psum(f"PA{j}", (128, 512), F32) for j in range(2)]
    PTs = [psum(f"PT{j}", (128, 512), BF16) for j in range(2)]
    PW = [psum(f"PW{j}", (128, 512), F32) for j in range(4)]

    p.dma("sync", CP[:, :], g["consts"][:, :], sig="dC")
    p.dma("sync", EYE[:, :], g["eye_bf"][:, :], sig="dC")
    p.dma("sync", EYEF[:, :], g["eye_f32"][:, :], sig="dC")
    p.dma("sync", ONESF[:, :], g["ones_f32"][:, :], sig="dC")
    p.dma("sync", ONESK[:, :], g["onesK"][:, :], sig="dC")
    p.dma("sync", bias[:, 0:16], g["b_dt"][:, :], sig="dC")
    p.dma("sync", bias[:, 16:32], g["conv_b"][:, :], sig="dC")
    evC = p.ev("dC")
    p.const_src = CP[0:1, 0:1]
    p.const_ev = evC
    for i in range(NTT):
        p.dma("sync", R[:, i * DM:(i + 1) * DM], g["hid"][i * 128:(i + 1) * 128, :], sig="dR")
    evR = p.ev("dR")

    sl_prep = Slots(1)
    wprev = None
    for k in range(8):
        for q in range(4):
            j, wre = sl_prep.get()
            eL = p.dma("sync", wprep[:, 0:1024],
                       g["w1"][k * 128:(k + 1) * 128, q * 1024:(q + 1) * 1024],
                       wait=[wre, wprev], sig="dW")
            eC1 = p.step("vector", lambda e, j=j: e.tensor_copy(
                out=wprepb[j][:, :], in_=wprep[:, 0:1024]), wait=[eL], sig="dve")
            wprev = eC1
            eS1 = p.dma("gpsimd", g["w1bf"][k * 128:(k + 1) * 128, q * 1024:(q + 1) * 1024],
                        wprepb[j][:, :], wait=[eC1], sig="dS")
            sl_prep.done(j, eS1)
    for q in range(4):
        j, wre = sl_prep.get()
        eL = p.dma("sync", wprep[0:1, 0:1024], g["b1row"][0:1, q * 1024:(q + 1) * 1024],
                   wait=[wre, wprev], sig="dW")
        eC1 = p.step("vector", lambda e, j=j: e.tensor_copy(
            out=wprepb[j][0:1, :], in_=wprep[0:1, 0:1024]), wait=[eL], sig="dve")
        wprev = eC1
        eS1 = p.dma("gpsimd", g["w1bf"][DM:DM + 1, q * 1024:(q + 1) * 1024],
                    wprepb[j][0:1, :], wait=[eC1], sig="dS")
        sl_prep.done(j, eS1)
    lastw = None
    for k in range(32):
        j, wre = sl_prep.get()
        eL = p.dma("sync", wprep[:, 0:1024], g["w2"][k * 128:(k + 1) * 128, :],
                   wait=[wre, wprev], sig="dW")
        eC = p.step("vector", lambda e, j=j: e.tensor_scalar(
            out=wprepb[j][:, :], in0=wprep[:, 0:1024], scalar1=1.0 / 256.0, scalar2=None,
            op0=ALU.mult), wait=[eL], sig="dve")
        wprev = eC
        lastw = p.dma("gpsimd", g["w2bf"][k * 128:(k + 1) * 128, :], wprepb[j][:, :],
                      wait=[eC], sig="dS")
        sl_prep.done(j, lastw)
    eLb2 = p.dma("sync", wprep[0:1, 0:1024], g["b2"][:, :], wait=[wprev], sig="dW")
    eCb2 = p.step("vector", lambda e: e.tensor_copy(out=b2sb[0:1, :], in_=wprep[0:1, 0:1024]),
                  wait=[eLb2, lastw], sig="dve")
    evW2done = p.dma("gpsimd", g["w2bf"][DF:DF + 1, :], b2sb[0:1, :], wait=[eCb2], sig="dS")

    SL = small
    sl_ln = Slots(1)
    sl_pt = Slots(2)
    for i in range(min(lntiles, 0 if stage < 1 else (NTT if stage == 1 else NTT + 1))):
        halo = (i == NTT)
        np_ = 3 if halo else 128
        j, wre = sl_ln.get()
        X = lnx[j]
        XB = lnxb[j]
        H1B = lnh[j]
        if halo:
            evh = p.dma("sync", X[0:3, 0:DM], g["hid_halo"][:, :], wait=[wre], sig="dR")
            src = X[0:3, 0:DM]
            w0 = [evh, evC]
        else:
            src = R[:, i * DM:(i + 1) * DM]
            w0 = [evR, evC, wre]
        eS2 = p.step("scalar", lambda e, src=src, np_=np_: e.activation(
            out=lnsq[0:np_, :], in_=src, func=AF.Square, accum_out=SL[0:np_, 0:1]),
            wait=w0, sig="act")
        eS1 = p.step("vector", lambda e, src=src, np_=np_: e.reduce_sum(
            out=SL[0:np_, 1:2], in_=src, axis=AX.X), wait=w0, sig="dve")
        eM = p.step("vector", lambda e, np_=np_: e.tensor_scalar(
            out=SL[0:np_, 2:3], in0=SL[0:np_, 1:2], scalar1=1.0 / DM, scalar2=None, op0=ALU.mult),
            wait=[eS1], sig="dve")
        eV1 = p.step("vector", lambda e, np_=np_: e.tensor_scalar(
            out=SL[0:np_, 3:4], in0=SL[0:np_, 0:1], scalar1=1.0 / DM, scalar2=1e-5,
            op0=ALU.mult, op1=ALU.add), wait=[eS2], sig="dve")
        eV2 = p.step("vector", lambda e, np_=np_: e.scalar_tensor_tensor(
            out=SL[0:np_, 4:5], in0=SL[0:np_, 2:3], scalar=SL[0:np_, 2:3], in1=SL[0:np_, 3:4],
            op0=ALU.mult, op1=ALU.subtract), wait=[eM, eV1], sig="dve")
        eV3 = p.step("vector", lambda e, np_=np_: e.tensor_scalar(
            out=SL[0:np_, 4:5], in0=SL[0:np_, 4:5], scalar1=-1.0, scalar2=None, op0=ALU.mult),
            wait=[eV2], sig="dve")
        eSq = p.step("scalar", lambda e, np_=np_: e.activation(
            out=SL[0:np_, 5:6], in_=SL[0:np_, 4:5], func=AF.Sqrt), wait=[eV3], sig="act")
        eRs = p.step("vector", lambda e, np_=np_: e.reciprocal(
            out=SL[0:np_, 6:7], in_=SL[0:np_, 5:6]), wait=[eSq], sig="dve")
        eNb = p.step("vector", lambda e, np_=np_: e.scalar_tensor_tensor(
            out=SL[0:np_, 7:8], in0=SL[0:np_, 2:3], scalar=-1.0, in1=SL[0:np_, 6:7],
            op0=ALU.mult, op1=ALU.mult), wait=[eRs], sig="dve")
        if not halo:
            eXn = p.step("vector", lambda e, src=src, np_=np_, X=X: e.tensor_scalar(
                out=X[0:np_, :], in0=src, scalar1=SL[0:np_, 6:7], scalar2=SL[0:np_, 7:8],
                op0=ALU.mult, op1=ALU.add), wait=[eNb, eRs], sig="dve")
        else:
            eXn = p.step("vector", lambda e, np_=np_, X=X: e.tensor_scalar(
                out=lnsq[0:np_, 0:DM], in0=X[0:np_, 0:DM], scalar1=SL[0:np_, 6:7],
                scalar2=SL[0:np_, 7:8], op0=ALU.mult, op1=ALU.add), wait=[eNb, eRs], sig="dve")
            eXn = p.step("vector", lambda e, np_=np_, X=X: e.tensor_copy(
                out=X[0:np_, 0:DM], in_=lnsq[0:np_, 0:DM]), wait=[eXn], sig="dve")
        eRq = p.step("scalar", lambda e, np_=np_, X=X: e.activation(
            out=lnsq[0:np_, 0:DM], in_=X[0:np_, 0:DM], func=AF.Square,
            accum_out=SL[0:np_, 8:9]), wait=[eXn], sig="act")
        eRv = p.step("vector", lambda e, np_=np_: e.tensor_scalar(
            out=SL[0:np_, 9:10], in0=SL[0:np_, 8:9], scalar1=1.0 / DM, scalar2=EPS0,
            op0=ALU.mult, op1=ALU.add), wait=[eRq], sig="dve")
        eRsq = p.step("scalar", lambda e, np_=np_: e.activation(
            out=SL[0:np_, 10:11], in_=SL[0:np_, 9:10], func=AF.Sqrt), wait=[eRv], sig="act")
        eRr = p.step("vector", lambda e, np_=np_: e.reciprocal(
            out=SL[0:np_, 11:12], in_=SL[0:np_, 10:11]), wait=[eRsq], sig="dve")
        if lnsub < 2:
            sl_ln.done(j, eXn)
            continue
        eXb = p.step("vector", lambda e, np_=np_, X=X, XB=XB: e.tensor_copy(
            out=XB[0:np_, 0:DM], in_=X[0:np_, 0:DM]), wait=[eXn], sig="dve") if not halo else None
        eH1 = p.step("scalar", lambda e, np_=np_, X=X, H1B=H1B: e.activation(
            out=H1B[0:np_, 0:DM], in_=X[0:np_, 0:DM], func=AF.Copy, scale=SL[0:np_, 11:12]),
            wait=[eRr, eXn], sig="act")
        if not halo:
            eRadd = p.step("vector", lambda e, i=i, X=X: e.tensor_tensor(
                out=R[:, i * DM:(i + 1) * DM], in0=R[:, i * DM:(i + 1) * DM],
                in1=X[0:128, :], op=ALU.add), wait=[eXn], sig="dve")
        if lnsub < 3:
            sl_ln.done(j, eRadd if not halo else eH1)
            continue
        lastcp = None
        for f in range(8):
            if not halo:
                jt, wt = sl_pt.get()
                eT1 = p.step("tensor", lambda e, f=f, jt=jt, XB=XB: e.transpose(
                    PTs[jt][:, 0:128], XB[0:128, f * 128:(f + 1) * 128], EYE[:, :]),
                    wait=[eXb, evC, wt], sig="pe")
                c1 = p.step("vector", lambda e, i=i, f=f, jt=jt: e.tensor_copy(
                    out=xnT[:, f * TPC + i * 128: f * TPC + (i + 1) * 128],
                    in_=PTs[jt][:, 0:128]), wait=[eT1], sig="dve")
                sl_pt.done(jt, c1)
                jt, wt = sl_pt.get()
                eT2 = p.step("tensor", lambda e, f=f, jt=jt, H1B=H1B: e.transpose(
                    PTs[jt][:, 0:128], H1B[0:128, f * 128:(f + 1) * 128], EYE[:, :]),
                    wait=[eH1, evC, wt], sig="pe")
                c2 = p.step("vector", lambda e, i=i, f=f, jt=jt: e.tensor_copy(
                    out=h1T[:, f * (TPC + 3) + 3 + i * 128: f * (TPC + 3) + 3 + (i + 1) * 128],
                    in_=PTs[jt][:, 0:128]), wait=[eT2], sig="dve")
                sl_pt.done(jt, c2)
                lastcp = c2
            else:
                jt, wt = sl_pt.get()
                eT2 = p.step("tensor", lambda e, f=f, jt=jt, H1B=H1B: e.transpose(
                    PTs[jt][:, 0:3], H1B[0:3, f * 128:(f + 1) * 128], EYE[0:3, 0:3]),
                    wait=[eH1, evC, wt], sig="pe")
                c2 = p.step("vector", lambda e, f=f, jt=jt: e.tensor_copy(
                    out=h1T[:, f * (TPC + 3): f * (TPC + 3) + 3],
                    in_=PTs[jt][:, 0:3]), wait=[eT2], sig="dve")
                sl_pt.done(jt, c2)
                lastcp = c2
        sl_ln.done(j, lastcp)

    p.barrier()

    if "xnT" in debug:
        dout("dbg_xnT", (DM, TPC), BF16)
        for f in range(8):
            p.dma("gpsimd", g["dbg_xnT"][f * 128:(f + 1) * 128, :],
                  xnT[:, f * TPC:(f + 1) * TPC], sig="dO")
    if "h1T" in debug:
        dout("dbg_h1T", (DM, TPC + 3), BF16)
        for f in range(8):
            p.dma("gpsimd", g["dbg_h1T"][f * 128:(f + 1) * 128, :],
                  h1T[:, f * (TPC + 3):(f + 1) * (TPC + 3)], sig="dO")

    sl_pwl = Slots(2)
    sl_wpan = Slots(1)
    sl_wk2 = Slots(2)
    sl_vout = Slots(1)
    sl_pa = Slots(2)
    sl_pt2 = Slots(2)
    pipeP = DmaPipe(p, "dPp", n=2)
    pipeK = DmaPipe(p, "dKk", n=2)
    pw_free = [None] * 4
    for sc in range(4 if stage >= 2 else 0):
        c0 = sc * 512
        ev_syT_guard = p.ev("pe")
        for m in range(32):
            jw, wwp = sl_wpan.get()
            wp = wpan[jw]
            ePan = pipeP.load_group(
                [(cvo(wp, 8, 128), cvi(g["w1bf"], 0, 8, m * 128, 128)),
                 (wp[0:1, 1024:1152], g["w1bf"][DM:DM + 1, m * 128:(m + 1) * 128])],
                wait=[wwp, evW2done])
            ja, wpa = sl_pa.get()
            PAm = PA[ja]
            mm_ev = None
            for k in range(9):
                if k < 8:
                    mm_ev = p.step("tensor", lambda e, k=k, wp=wp, PAm=PAm, c0=c0: e.matmul(
                        PAm[:, :], wp[:, k * 128:(k + 1) * 128],
                        xnT[:, k * TPC + c0: k * TPC + c0 + 512],
                        start=(k == 0), stop=False),
                        wait=[ePan, wpa] if k == 0 else [],
                        sig=None)
                else:
                    mm_ev = p.step("tensor", lambda e, wp=wp, PAm=PAm: e.matmul(
                        PAm[:, :], wp[0:1, 1024:1152], ONESK[0:1, :],
                        start=False, stop=True),
                        wait=[evC], sig="pe")
            sl_wpan.done(jw, mm_ev)
            jp, wpw = sl_pwl.get()
            W_, G0, G1, D_, E_, Z_, I_, J_ = (pw_w[jp], pw_g0[jp], pw_g1[jp], pw_d[jp],
                                              pw_e[jp], pw_z[jp], pw_i[jp], pw_j[jp])
            eW = p.step("scalar", lambda e, PAm=PAm, W_=W_: e.activation(
                out=W_[:, :], in_=PAm[:, :], func=AF.Copy, scale=256.0),
                wait=[mm_ev, wpw], sig="act")
            sl_pa.done(ja, eW)
            eSi = p.step("vector", lambda e, W_=W_, I_=I_: e.tensor_copy(
                out=I_[:, :], in_=W_[:, :]), wait=[eW], sig="dve")
            eIi = p.step("vector", lambda e, I_=I_, J_=J_: e.tensor_scalar(
                out=J_[:, :], in0=I_[:, :], scalar1=6, scalar2=None,
                op0=ALU.arith_shift_right), wait=[eSi], sig="dve")
            eFi = p.step("vector", lambda e, I_=I_: e.tensor_scalar(
                out=I_[:, :], in0=I_[:, :], scalar1=63, scalar2=None,
                op0=ALU.bitwise_and), wait=[eIi], sig="dve")
            eG0 = p.step("scalar", lambda e, J_=J_, G0=G0: e.activation(
                out=G0[:, :], in_=J_[:, :], func=AF.Gelu, scale=0.25), wait=[eIi], sig="act")
            eG1 = p.step("scalar", lambda e, J_=J_, G1=G1: e.activation(
                out=G1[:, :], in_=J_[:, :], func=AF.Gelu, scale=0.25, bias=0.25),
                wait=[eIi], sig="act")
            eD = p.step("gpsimd", lambda e, G0=G0, G1=G1, D_=D_: e.tensor_tensor(
                out=D_[:, :], in0=G1[:, :], in1=G0[:, :], op=ALU.subtract),
                wait=[eG0, eG1], sig="gps")
            eE4 = p.step("vector", lambda e, I_=I_, D_=D_, E_=E_: e.scalar_tensor_tensor(
                out=E_[:, :], in0=I_[:, :], scalar=4.0, in1=D_[:, :],
                op0=ALU.mult, op1=ALU.mult), wait=[eD, eFi], sig="dve")
            eZ = p.step("vector", lambda e, G0=G0, E_=E_, Z_=Z_: e.scalar_tensor_tensor(
                out=Z_[:, :], in0=G0[:, :], scalar=256.0, in1=E_[:, :],
                op0=ALU.mult, op1=ALU.add), wait=[eE4], sig="dve")
            eZi = p.step("vector", lambda e, Z_=Z_, I_=I_: e.tensor_copy(
                out=I_[:, :], in_=Z_[:, :]), wait=[eZ], sig="dve")
            eSy = p.step("gpsimd", lambda e, m=m, I_=I_: e.tensor_copy(
                out=syT[:, m * 512:(m + 1) * 512], in_=I_[:, :]),
                wait=[eZi, ev_syT_guard], sig="gps")
            sl_pwl.done(jp, eSy)
        evPWL = p.ev("gps")
        for og in range(2):
            ev_o = [None] * 4
            for k in range(33):
                jk, wk_ = sl_wk2.get()
                wkt = wk2[jk]
                if k < 32:
                    eK = pipeK.load(wkt[:, :], g["w2bf"][k * 128:(k + 1) * 128, :],
                                    wait=[wk_])
                else:
                    eK = pipeK.load(wkt[0:1, :], g["w2bf"][DF:DF + 1, :],
                                    wait=[wk_])
                last_mm = None
                for o in range(4):
                    oo = og * 4 + o
                    if k < 32:
                        last_mm = p.step("tensor", lambda e, k=k, o=o, oo=oo, wkt=wkt: e.matmul(
                            PW[o][:, :], wkt[:, oo * 128:(oo + 1) * 128],
                            syT[:, k * 512:(k + 1) * 512],
                            start=(k == 0), stop=False),
                            wait=[eK, evPWL] + ([pw_free[o]] if k == 0 else []),
                            sig="pe" if o == 3 else None)
                    else:
                        last_mm = p.step("tensor", lambda e, o=o, oo=oo, wkt=wkt: e.matmul(
                            PW[o][:, :], wkt[0:1, oo * 128:(oo + 1) * 128],
                            ONESK[0:1, :], start=False, stop=True),
                            wait=[eK, evC], sig="pe")
                        ev_o[o] = last_mm
                sl_wk2.done(jk, last_mm)
            for o in range(4):
                oo = og * 4 + o
                jv, wv = sl_vout.get()
                V, VI, VB = vout[jv], vouti[jv], voutb[jv]
                eV = p.step("scalar", lambda e, o=o, V=V: e.activation(
                    out=V[:, :], in_=PW[o][:, :], func=AF.Copy, scale=256.0),
                    wait=[ev_o[o], wv], sig="act")
                ev_o[o] = eV
                pw_free[o] = eV
                eVi = p.step("vector", lambda e, V=V, VI=VI: e.tensor_copy(
                    out=VI[:, :], in_=V[:, :]), wait=[eV], sig="dve")
                eVb = p.step("vector", lambda e, VI=VI, VB=VB: e.tensor_copy(
                    out=VB[:, :], in_=VI[:, :]), wait=[eVi], sig="dve")
                lastr = None
                for tt in range(4):
                    jt, wt = sl_pt2.get()
                    eT = p.step("tensor", lambda e, tt=tt, jt=jt, VB=VB: e.transpose(
                        PTs[jt][:, 0:128], VB[:, tt * 128:(tt + 1) * 128], EYE[:, :]),
                        wait=[eVb, wt], sig="pe")
                    ti = sc * 4 + tt
                    lastr = p.step("vector", lambda e, ti=ti, oo=oo, jt=jt: e.scalar_tensor_tensor(
                        out=R[:, ti * DM + oo * 128: ti * DM + (oo + 1) * 128],
                        in0=PTs[jt][:, 0:128], scalar=1.0 / 256.0,
                        in1=R[:, ti * DM + oo * 128: ti * DM + (oo + 1) * 128],
                        op0=ALU.mult, op1=ALU.add), wait=[eT], sig="dve")
                    sl_pt2.done(jt, lastr)
                sl_vout.done(jv, lastr)


    if stage >= 3:
        a12.free()
        a3c = nc.auto_arena()
        p.arenas.append(a3c)
        BT = a3c.sbuf_tensor("BT", [128, 4 * TPC], BF16); g["BT"] = BT
        eTall = a3c.sbuf_tensor("eTall", [128, 16 * 24], F32)
        lcsg = a3c.sbuf_tensor("lcsg", [8, TPC], F32); g["lcsg"] = lcsg
        hall = a3c.sbuf_tensor("hall", [64, 2048], F32); g["hall"] = hall
        hbf = a3c.sbuf_tensor("hbf", [128, 2048], BF16)
        wdtb = a3c.sbuf_tensor("wdtb", [8, DI], BF16)
        deye = a3c.sbuf_tensor("deye", [128, 8 * 128], BF16)
        lambc = a3c.sbuf_tensor("lambc", [64, 384], F32)
        coefb = a3c.sbuf_tensor("coefb", [64, 64], F32)
        smal2 = a3c.sbuf_tensor("smal2", [24, 32], F32)
        lgb = a3c.sbuf_tensor("lgb", [8, 8], BF16)
        lgf = a3c.sbuf_tensor("lgf", [8, 8], F32)
        lmT = a3c.sbuf_tensor("lmT", [8, 8], F32)
        crow = a3c.sbuf_tensor("crow", [1, 96], F32)
        ones8 = a3c.sbuf_tensor("ones8", [8, 512], F32)
        dtraw = a3c.sbuf_tensor("dtraw", [8, 512], BF16)
        dtmn = a3c.sbuf_tensor("dtmn", [128, 16 * 8], BF16)
        hsb = a3c.sbuf_tensor("hsb", [64, 2048], BF16)
        wf8 = a3c.sbuf_tensor("wf8", [128, 256], F32)
        wA8 = a3c.sbuf_tensor("wA8", [128, 512], BF16)
        wB8 = a3c.sbuf_tensor("wB8", [128, 512], BF16)
        convw = a3c.sbuf_tensor("convw", [128, 64], F32)
        pmT = a3c.sbuf_tensor("pmT", [8, 8], F32)
        pmiT = a3c.sbuf_tensor("pmiT", [8, 8], F32)
        a3c.allocate()
        a3a = nc.auto_arena()
        p.arenas.append(a3a)
        xin = [a3a.sbuf_tensor(f"xin{j}", [128, 515], F32) for j in range(2)]
        cacc = [a3a.sbuf_tensor(f"cacc{j}", [128, 512], F32) for j in range(4)]
        xcb = [a3a.sbuf_tensor(f"xcb{j}", [128, 512], BF16) for j in range(2)]
        xtok = a3a.sbuf_tensor("xtok", [128, 4 * 2048], BF16)
        stash = a3a.sbuf_tensor("stash", [128, 48], F32)
        expd = [a3a.sbuf_tensor(f"expd{j}", [128, 512], F32) for j in range(2)]
        dtb = [a3a.sbuf_tensor(f"dtb{j}", [128, 512], BF16) for j in range(2)]
        wpanA = [a3a.sbuf_tensor(f"wpanA{j}", [128, 1024], BF16) for j in range(2)]
        Bs = [a3a.sbuf_tensor(f"Bs{j}", [128, 64], BF16) for j in range(2)]
        dtw = [a3a.sbuf_tensor(f"dtw{j}", [8, 512], F32) for j in range(2)]
        eraw = [a3a.sbuf_tensor(f"eraw{j}", [8, 3 * 128], F32) for j in range(2)]
        a3a.allocate()

        pipeW = DmaPipe(p, "dWi", n=2)
        sl_xin = Slots(2)
        sl_cacc = Slots(4)
        sl_xcb = Slots(2)
        sl_expd = Slots(2)
        sl_dtb = Slots(2)
        sl_bs = Slots(2)
        sl_dtw = Slots(2)
        sl_eraw = Slots(2)
        sl_pa3 = Slots(2)
        sl_pt3 = Slots(2)
        pw_free3 = [p.ev("act")] * 4

        evP = p.ev("dve")
        wnin = bias[:, 40:48]
        eWn = p.dma("sync", bias[:, 40:48], g["w_norm_in_t"][:, :], sig="dW")
        eCw = p.dma("sync", convw[:, :], g["conv_w"][:, :], sig="dW")
        sl_wp3 = Slots(1)
        wlast = None
        for kk in range(8):
            for q in range(6):
                w0 = 1024 * q
                w1_ = min(1024 * (q + 1), 5128)
                if w0 >= 5128:
                    break
                jj, wre = sl_wp3.get()
                eL = p.dma("sync", wprep[:, 0:w1_ - w0],
                           g["w_in_proj"][kk * 128:(kk + 1) * 128, w0:w1_],
                           wait=[wre, wlast], sig="dW")
                eC = p.step("vector", lambda e, jj=jj, kk=kk, n=w1_ - w0: e.tensor_scalar(
                    out=wprepb[jj][:, 0:n], in0=wprep[:, 0:n],
                    scalar1=bias[:, 40 + kk:41 + kk], scalar2=None, op0=ALU.mult),
                    wait=[eL, eWn], sig="dve")
                wlast = eC
                eS = p.dma("gpsimd", g["winbf"][kk * 128:(kk + 1) * 128, w0:w1_],
                           wprepb[jj][:, 0:w1_ - w0], wait=[eC], sig="dS")
                sl_wp3.done(jj, eS)
        eWno = p.dma("sync", bias[:, 40:48], g["w_norm_out_t"][:, 0:8], wait=[wlast], sig="dW")
        eWno2 = p.dma("sync", stash[:, 0:8], g["w_norm_out_t"][:, 8:16], sig="dW")
        for kk in range(16):
            jj, wre = sl_wp3.get()
            eL = p.dma("sync", wprep[:, 0:1024], g["w_out_proj"][kk * 128:(kk + 1) * 128, :],
                       wait=[wre, wlast], sig="dW")
            sccol = bias[:, 40 + kk:41 + kk] if kk < 8 else stash[:, kk - 8:kk - 7]
            eC = p.step("vector", lambda e, jj=jj, sccol=sccol: e.tensor_scalar(
                out=wprepb[jj][:, 0:1024], in0=wprep[:, 0:1024],
                scalar1=sccol, scalar2=None, op0=ALU.mult),
                wait=[eL, eWno, eWno2], sig="dve")
            wlast = eC
            eS = p.dma("gpsimd", g["woutbf"][kk * 128:(kk + 1) * 128, :],
                       wprepb[jj][:, 0:1024], wait=[eC], sig="dS")
            sl_wp3.done(jj, eS)
        for q in range(2):
            jj, wre = sl_wp3.get()
            eL = p.dma("sync", wprep[0:8, 0:1024], g["w_dt"][:, q * 1024:(q + 1) * 1024],
                       wait=[wre, wlast], sig="dW")
            eC = p.step("vector", lambda e, q=q: e.tensor_copy(
                out=wdtb[:, q * 1024:(q + 1) * 1024], in_=wprep[0:8, 0:1024]),
                wait=[eL], sig="dve")
            wlast = eC
            sl_wp3.done(jj, eC)
        evPrep = wlast
        for kk in range(16):
            p.dma("sync", dtmn[:, kk * 8:(kk + 1) * 8], g["dtmean"][kk * 128:(kk + 1) * 128, :],
                  sig="dK")
        evDtmn = p.ev("dK")
        eAl = p.dma("sync", smal2[0:8, 0:1], g["A_log"][:, :], sig="dK")
        eA1 = p.step("scalar", lambda e: e.activation(
            out=smal2[0:8, 1:2], in_=smal2[0:8, 0:1], func=AF.Exp), wait=[eAl], sig="act")
        eA2 = p.step("vector", lambda e: e.tensor_scalar(
            out=bias[0:8, 32:33], in0=smal2[0:8, 1:2], scalar1=-1.0, scalar2=None,
            op0=ALU.mult), wait=[eA1], sig="dve")
        eZc = p.step("vector", lambda e: e.tensor_scalar(
            out=bias[0:8, 33:34], in0=bias[0:8, 32:33], scalar1=0.0, scalar2=None,
            op0=ALU.mult), wait=[eA2], sig="dve")
        eO8 = p.step("vector", lambda e: e.tensor_scalar(
            out=ones8[:, :], in0=ones8[:, :], scalar1=0.0, scalar2=1.0,
            op0=ALU.mult, op1=ALU.add), wait=[], sig="dve") if False else None
        eO8 = p.step("gpsimd", lambda e: e.memset(ones8[:, :], 1.0), sig="gps")
        eDl = p.dma("sync", smal2[0:1, 8:16], g["Dp_row"][:, :], wait=[eA1], sig="dK")
        eDb = p.step("tensor", lambda e: e.matmul(
            PA[0][:, 0:8], ONESF[0:1, 0:128].transpose([1, 0]) if False else ONESF[0:1, :],
            smal2[0:1, 8:16], start=True, stop=True), wait=[eDl, evC], sig="pe") if False else None
        eDb = p.step("tensor", lambda e: e.matmul(
            PA[0][:, 0:8], ONESF[0:1, :], smal2[0:1, 8:16], start=True, stop=True),
            wait=[eDl, evC], sig="pe")
        eDbc = p.step("vector", lambda e: e.tensor_copy(
            out=smal2[0:24, 16:24] if False else stash[:, 8:16], in_=PA[0][:, 0:8]),
            wait=[eDb], sig="dve")
        evDeye = []
        for h in range(8):
            evDeye.append(p.step("vector", lambda e, h=h: e.tensor_scalar(
                out=deye[:, h * 128:(h + 1) * 128], in0=EYE[:, :],
                scalar1=stash[:, 8 + h:9 + h], scalar2=None, op0=ALU.mult),
                wait=[eDbc, evC], sig="dve"))
        evDeyeAll = evDeye[-1]

        sl_pa3.busy[0] = eDbc
        IMX = list(range(16, 32))
        IMB = list(range(32, 36))
        evStashInit = None
        for ii, m in enumerate(IMX):
            ja, wpa = sl_pa3.get()
            ePan = pipeW.load_group(
                [(cvo(wpanA[ja], 8, 128), cvi(g["winbf"], 0, 8, m * 128, 128))],
                wait=[wpa, evPrep])
            mmv = None
            for kk in range(8):
                mmv = p.step("tensor", lambda e, kk=kk, ja=ja: e.matmul(
                    PA[ja][:, 0:3], wpanA[ja][:, kk * 128:(kk + 1) * 128],
                    h1T[:, kk * (TPC + 3): kk * (TPC + 3) + 3],
                    start=(kk == 0), stop=(kk == 7)),
                    wait=[ePan, wpa] if kk == 0 else [], sig="pe" if kk == 7 else None)
            ev = p.step("vector", lambda e, ii=ii, ja=ja: e.tensor_copy(
                out=stash[:, 16 + ii * 2:16 + ii * 2 + 2][:, 0:2] if False
                else stash[:, 0:3], in_=PA[ja][:, 0:3]), wait=[mmv], sig="dve") if False else None
            ev = p.step("vector", lambda e, ii=ii, ja=ja: e.tensor_copy(
                out=stash[:, ii * 3:ii * 3 + 3], in_=PA[ja][:, 0:3]), wait=[mmv], sig="dve")
            sl_pa3.done(ja, ev)
            evStashInit = ev

        evHall0 = p.step("gpsimd", lambda e: e.memset(hall[:, :], 0.0), sig="gps")
        ev_xtok_dma = None
        ev_dtraw = None
        ev_scan_prev = None
        ev_lambc = [None] * 4
        ev_eT = [None] * 16
        ev_BT = [None] * 4
        evStash = [evStashInit] * 16
        hall_upd = [evHall0] * 8
        for s in range(4):
            cb3 = s * 512 + 3
            ev_xt_copies = []
            for ii, m in enumerate(IMX):
                ja, wpa = sl_pa3.get()
                ePan = pipeW.load_group(
                    [(cvo(wpanA[ja % 2], 8, 128), cvi(g["winbf"], 0, 8, m * 128, 128))],
                    wait=[wpa, evPrep])
                mmv = None
                for kk in range(8):
                    mmv = p.step("tensor", lambda e, kk=kk, ja=ja, cb3=cb3: e.matmul(
                        PA[ja][:, :], wpanA[ja % 2][:, kk * 128:(kk + 1) * 128],
                        h1T[:, kk * (TPC + 3) + cb3: kk * (TPC + 3) + cb3 + 512],
                        start=(kk == 0), stop=(kk == 7)),
                        wait=[ePan, wpa] if kk == 0 else [], sig="pe" if kk == 7 else None)
                jx, wxin = sl_xin.get()
                evXin3 = p.step("vector", lambda e, ii=ii, jx=jx: e.tensor_copy(
                    out=xin[jx][:, 0:3], in_=stash[:, ii * 3:ii * 3 + 3]),
                    wait=[evStash[ii], wxin], sig="dve")
                evXin = p.step("scalar", lambda e, jx=jx, ja=ja: e.activation(
                    out=xin[jx][:, 3:515], in_=PA[ja][:, :], func=AF.Copy),
                    wait=[mmv, wxin], sig="act")
                sl_pa3.done(ja, evXin)
                evStash[ii] = p.step("vector", lambda e, ii=ii, jx=jx: e.tensor_copy(
                    out=stash[:, ii * 3:ii * 3 + 3], in_=xin[jx][:, 512:515]),
                    wait=[evXin, evXin3], sig="dve")
                jc, wc = sl_cacc.get()
                jc2, wc2 = sl_cacc.get()
                a0 = p.step("vector", lambda e, jx=jx, jc=jc, ii=ii: e.tensor_scalar(
                    out=cacc[jc][:, :], in0=xin[jx][:, 0:512],
                    scalar1=convw[:, ii:ii + 1], scalar2=None, op0=ALU.mult),
                    wait=[evXin, evXin3, wc, evPrep], sig="dve")
                a1 = p.step("vector", lambda e, jx=jx, jc=jc, jc2=jc2, ii=ii: e.scalar_tensor_tensor(
                    out=cacc[jc2][:, :], in0=xin[jx][:, 1:513], scalar=convw[:, 16 + ii:17 + ii],
                    in1=cacc[jc][:, :], op0=ALU.mult, op1=ALU.add),
                    wait=[a0, wc2], sig="dve")
                a2 = p.step("vector", lambda e, jx=jx, jc=jc, jc2=jc2, ii=ii: e.scalar_tensor_tensor(
                    out=cacc[jc][:, :], in0=xin[jx][:, 2:514], scalar=convw[:, 32 + ii:33 + ii],
                    in1=cacc[jc2][:, :], op0=ALU.mult, op1=ALU.add),
                    wait=[a1], sig="dve")
                a3 = p.step("vector", lambda e, jx=jx, jc=jc, jc2=jc2, ii=ii: e.scalar_tensor_tensor(
                    out=cacc[jc2][:, :], in0=xin[jx][:, 3:515], scalar=convw[:, 48 + ii:49 + ii],
                    in1=cacc[jc][:, :], op0=ALU.mult, op1=ALU.add),
                    wait=[a2], sig="dve")
                sl_xin.done(jx, a3)
                sl_cacc.done(jc, a3)
                jxc, wxc = sl_xcb.get()
                exc = p.step("scalar", lambda e, jc2=jc2, jxc=jxc, ii=ii: e.activation(
                    out=xcb[jxc][:, :], in_=cacc[jc2][:, :], func=AF.Silu,
                    bias=bias[:, 16 + ii:17 + ii], scale=1.0),
                    wait=[a3, wxc, evC], sig="act")
                sl_cacc.done(jc2, exc)
                lastT = None
                for tt in range(4):
                    jt3, wt3 = sl_pt3.get()
                    eT = p.step("tensor", lambda e, tt=tt, jt3=jt3, jxc=jxc: e.transpose(
                        PTs[jt3][:, 0:128], xcb[jxc][:, tt * 128:(tt + 1) * 128], EYE[:, :]),
                        wait=[exc, wt3, evC], sig="pe")
                    ecp = p.step("vector", lambda e, tt=tt, jt3=jt3, ii=ii: e.tensor_copy(
                        out=xtok[:, tt * 2048 + ii * 128: tt * 2048 + (ii + 1) * 128],
                        in_=PTs[jt3][:, 0:128]),
                        wait=[eT, ev_xtok_dma], sig="dve")
                    sl_pt3.done(jt3, ecp)
                    ev_xt_copies.append(ecp)
                    lastT = eT
                sl_xcb.done(jxc, lastT)
            ev_xtok_ready = ev_xt_copies[-1]
            for ib, m in enumerate(IMB):
                ja, wpa = sl_pa3.get()
                ePan = pipeW.load_group(
                    [(cvo(wpanA[ja % 2], 8, 128), cvi(g["winbf"], 0, 8, m * 128, 128))],
                    wait=[wpa, evPrep])
                mmv = None
                for kk in range(8):
                    mmv = p.step("tensor", lambda e, kk=kk, ja=ja, cb3=cb3: e.matmul(
                        PA[ja][:, :], wpanA[ja % 2][:, kk * 128:(kk + 1) * 128],
                        h1T[:, kk * (TPC + 3) + cb3: kk * (TPC + 3) + cb3 + 512],
                        start=(kk == 0), stop=(kk == 7)),
                        wait=[ePan, wpa] if kk == 0 else [], sig="pe" if kk == 7 else None)
                evB = p.step("scalar", lambda e, ib=ib, ja=ja, s=s: e.activation(
                    out=BT[:, ib * TPC + s * 512: ib * TPC + (s + 1) * 512],
                    in_=PA[ja][:, :], func=AF.Copy),
                    wait=[mmv] + ([ev_BT[ib]] if ev_BT[ib] else []), sig="act")
                sl_pa3.done(ja, evB)
                ev_BT[ib] = evB
            ja, wpa = sl_pa3.get()
            ePan = pipeW.load_group(
                [(cvo(wpanA[ja % 2], 8, 8, c0=960), cvi(g["winbf"], 0, 8, 5120, 8))],
                wait=[wpa, evPrep])
            mmv = None
            for kk in range(8):
                mmv = p.step("tensor", lambda e, kk=kk, ja=ja, cb3=cb3: e.matmul(
                    PA[ja][0:8, :], wpanA[ja % 2][:, kk * 8 + 960: kk * 8 + 968],
                    h1T[:, kk * (TPC + 3) + cb3: kk * (TPC + 3) + cb3 + 512],
                    start=(kk == 0), stop=(kk == 7)),
                    wait=[ePan, wpa] if kk == 0 else [], sig="pe" if kk == 7 else None)
            evDtraw = p.step("scalar", lambda e, ja=ja: e.activation(
                out=dtraw[:, :], in_=PA[ja][0:8, :], func=AF.Copy),
                wait=[mmv, ev_dtraw], sig="act")
            sl_pa3.done(ja, evDtraw)
            lastmean = None
            for m2 in range(16):
                ja2, wpa2 = sl_pa3.get()
                mmd = p.step("tensor", lambda e, m2=m2, ja2=ja2: e.matmul(
                    PA[ja2][:, :], wdtb[:, m2 * 128:(m2 + 1) * 128], dtraw[:, :],
                    start=True, stop=True),
                    wait=[evDtraw, wpa2, evPrep], sig="pe")
                je, wex = sl_expd.get()
                eEx = p.step("scalar", lambda e, m2=m2, ja2=ja2, je=je: e.activation(
                    out=expd[je][:, :], in_=PA[ja2][:, :], func=AF.Exp,
                    bias=bias[:, m2:m2 + 1], scale=1.0),
                    wait=[mmd, wex, evC], sig="act")
                sl_pa3.done(ja2, eEx)
                jd, wdt_ = sl_dtb.get()
                eLn = p.step("scalar", lambda e, je=je, jd=jd: e.activation(
                    out=dtb[jd][:, :], in_=expd[je][:, :], func=AF.Ln, bias=1.0),
                    wait=[eEx, wdt_], sig="act")
                sl_expd.done(je, eLn)
                lastmean = p.step("tensor", lambda e, m2=m2, jd=jd: e.matmul(
                    PW[3][0:8, :], dtmn[:, m2 * 8:(m2 + 1) * 8], dtb[jd][:, :],
                    start=(m2 == 0), stop=(m2 == 15)),
                    wait=[eLn, p.ev("dK")] + ([pw_free3[3]] if m2 == 0 else []),
                    sig="pe" if m2 == 15 else None)
                sl_dtb.done(jd, lastmean if m2 == 15 else p.ev("pe"))
            jw, wdtw = sl_dtw.get()
            evdtA = p.step("vector", lambda e, jw=jw: e.tensor_scalar(
                out=dtw[jw][:, :], in0=PW[3][0:8, :], scalar1=bias[0:8, 32:33],
                scalar2=None, op0=ALU.mult),
                wait=[lastmean, wdtw, eA2], sig="dve")
            pw_free3[3] = evdtA
            evScan = p.step("vector", lambda e, s=s, jw=jw: e.tensor_tensor_scan(
                out=lcsg[:, s * 512:(s + 1) * 512], data0=ones8[:, :], data1=dtw[jw][:, :],
                initial=(0.0 if s == 0 else lcsg[0:8, s * 512 - 1:s * 512]),
                op0=ALU.mult, op1=ALU.add),
                wait=[evdtA, eO8] + ([ev_scan_prev] if ev_scan_prev else []), sig="dve")
            ev_scan_prev = evScan
            sl_dtw.done(jw, evScan)
            for tt in range(4):
                c = s * 4 + tt
                je2, wer = sl_eraw.get()
                jw2, wdtw2 = sl_dtw.get()
                base = bias[0:8, 33:34] if c == 0 else lcsg[0:8, c * 128 - 1:c * 128]
                eD1 = p.step("vector", lambda e, c=c, jw2=jw2, base=base: e.tensor_scalar(
                    out=dtw[jw2][:, 0:128], in0=lcsg[:, c * 128:(c + 1) * 128],
                    scalar1=base, scalar2=None, op0=ALU.subtract),
                    wait=[evScan, wdtw2, eZc], sig="dve")
                eD2 = p.step("vector", lambda e, c=c, jw2=jw2: e.tensor_scalar(
                    out=dtw[jw2][:, 128:256], in0=lcsg[:, c * 128:(c + 1) * 128],
                    scalar1=lcsg[0:8, (c + 1) * 128 - 1:(c + 1) * 128], scalar2=None,
                    op0=ALU.subtract),
                    wait=[evScan, eD1], sig="dve")
                eE1 = p.step("scalar", lambda e, je2=je2, jw2=jw2: e.activation(
                    out=eraw[je2][:, 0:128], in_=dtw[jw2][:, 0:128], func=AF.Exp),
                    wait=[eD1, wer], sig="act")
                eE3 = p.step("scalar", lambda e, je2=je2, jw2=jw2: e.activation(
                    out=eraw[je2][:, 128:256], in_=dtw[jw2][:, 0:128], func=AF.Exp, scale=-1.0),
                    wait=[eD1], sig="act")
                eE2 = p.step("scalar", lambda e, je2=je2, jw2=jw2: e.activation(
                    out=eraw[je2][:, 256:384], in_=dtw[jw2][:, 128:256], func=AF.Exp, scale=-1.0),
                    wait=[eD2], sig="act")
                sl_dtw.done(jw2, eE2)
                ja3, wpa3 = sl_pa3.get()
                eTr = None
                for gci, gev in ((0, eE1), (1, eE3), (2, eE2)):
                    eTr = p.step("tensor", lambda e, je2=je2, ja3=ja3, gci=gci: e.transpose(
                        PA[ja3][:, gci * 8:(gci + 1) * 8], eraw[je2][:, gci * 128:(gci + 1) * 128],
                        EYEF[0:8, 0:8]),
                        wait=[gev, evC] + ([wpa3] if gci == 0 else []),
                        sig="pe" if gci == 2 else None)
                sl_eraw.done(je2, eTr)
                ecp = p.step("vector", lambda e, c=c, ja3=ja3: e.tensor_copy(
                    out=eTall[:, c * 24:(c + 1) * 24], in_=PA[ja3][:, 0:24]),
                    wait=[eTr] + ([ev_eT[c]] if ev_eT[c] else []), sig="dve")
                sl_pa3.done(ja3, ecp)
                ev_eT[c] = ecp
            ja4, wpa4 = sl_pa3.get()
            evLrow = p.dma("gpsimd", crow[0:1, 0:96], eTall[127:128, s * 96:(s + 1) * 96],
                           wait=[ev_eT[s * 4], ev_eT[s * 4 + 1], ev_eT[s * 4 + 2],
                                 ev_eT[s * 4 + 3]], sig="dS")
            eLb = p.step("tensor", lambda e, s=s, ja4=ja4: e.matmul(
                PA[ja4][0:64, 0:96], ONESF[0:1, 0:64], crow[0:1, 0:96],
                start=True, stop=True),
                wait=[evLrow, wpa4, evC], sig="pe")
            eLbc = p.step("vector", lambda e, s=s, ja4=ja4: e.tensor_copy(
                out=lambc[:, s * 96:(s + 1) * 96], in_=PA[ja4][0:64, 0:96]),
                wait=[eLb] + ([ev_lambc[s]] if ev_lambc[s] else []), sig="dve")
            sl_pa3.done(ja4, eLbc)
            ev_lambc[s] = eLbc
            for tt in range(4):
                c = s * 4 + tt
                for h in range(8):
                    tb, ro = h // 2, (h % 2) * 64
                    jt3, wt3 = sl_pt3.get()
                    eTb = p.step("tensor", lambda e, tb=tb, ro=ro, s=s, tt=tt, jt3=jt3: e.transpose(
                        PTs[jt3][:, 0:64],
                        BT[ro:ro + 64, tb * TPC + s * 512 + tt * 128: tb * TPC + s * 512 + (tt + 1) * 128],
                        EYE[ro:ro + 64, ro:ro + 64]),
                        wait=[ev_BT[tb], wt3, evC], sig="pe")
                    jb, wb = sl_bs.get()
                    eBs = p.step("scalar", lambda e, jt3=jt3, jb=jb, c=c, h=h: e.activation(
                        out=Bs[jb][:, :], in_=PTs[jt3][:, 0:64], func=AF.Copy,
                        scale=eTall[:, c * 24 + 16 + h:c * 24 + 17 + h]),
                        wait=[eTb, wb, ev_eT[c]], sig="act")
                    sl_pt3.done(jt3, eBs)
                    mmD = p.step("tensor", lambda e, jb=jb, tt=tt, h=h: e.matmul(
                        PW[h // 2][0:64, (h % 2) * 256:((h % 2) + 1) * 256],
                        Bs[jb][:, :], xtok[:, tt * 2048 + h * 256: tt * 2048 + (h + 1) * 256],
                        start=True, stop=True),
                        wait=[eBs, ev_xtok_ready, pw_free3[h // 2], hall_upd[(h // 2) * 2],
                              hall_upd[(h // 2) * 2 + 1]],
                        sig="pe")
                    sl_bs.done(jb, mmD)
                    upd = p.step("vector", lambda e, h=h, c=c: e.scalar_tensor_tensor(
                        out=hall[:, h * 256:(h + 1) * 256], in0=hall[:, h * 256:(h + 1) * 256],
                        scalar=lambc[0:64, c * 24 + h:c * 24 + h + 1],
                        in1=PW[h // 2][0:64, (h % 2) * 256:((h % 2) + 1) * 256],
                        op0=ALU.mult, op1=ALU.add),
                        wait=[mmD, ev_lambc[s], hall_upd[h]], sig="dve")
                    hall_upd[h] = upd
                    pw_free3[h // 2] = upd
            prevdma = ev_xtok_dma
            ev_xtok_dma = p.dma("sync",
                cvrows(g["xtok_park"], s * 512, 4, 0, DI),
                cvcols(xtok, 4, DI),
                wait=[hall_upd[7], ev_xtok_ready], sig="dV")

        if stage >= 4:
            eHb = p.step("vector", lambda e: e.tensor_copy(out=hbf[0:64, :], in_=hall[:, :]),
                         wait=hall_upd, sig="dve")
            eLs1 = p.step("scalar", lambda e: e.activation(
                out=smal2[0:8, 2:3], in_=lcsg[0:8, TPC - 1:TPC], func=AF.Exp),
                wait=[ev_scan_prev], sig="act")
            ja5, wpa5 = sl_pa3.get()
            eTl = p.step("tensor", lambda e, ja5=ja5: e.transpose(
                PA[ja5][0:1, 0:8], smal2[0:8, 2:3], EYEF[0:8, 0:8]),
                wait=[eLs1, wpa5, evC], sig="pe")
            eLrow2 = p.step("vector", lambda e, ja5=ja5: e.tensor_copy(
                out=lgb[0:1, 0:8], in_=PA[ja5][0:1, 0:8]), wait=[eTl], sig="dve")
            sl_pa3.done(ja5, eLrow2)
            eS1 = p.dma("gpsimd", g["ccA_in"][0:64, :], hbf[0:64, :], wait=[eHb], sig="dS")
            eZr = p.step("gpsimd", lambda e: e.memset(hsb[0:1, :], 0.0), sig="gps")
            eZr2 = p.step("vector", lambda e: e.tensor_copy(
                out=hsb[0:1, 0:8], in_=lgb[0:1, 0:8]), wait=[eZr, eLrow2], sig="dve")
            eS2 = p.dma("gpsimd", g["ccA_in"][64:65, :], hsb[0:1, :], wait=[eZr2], sig="dS")
            evCC = p.step("gpsimd", lambda e: e.collective_compute(
                "AllGather", ALU.bypass, replica_groups=[list(range(8))],
                ins=[g["ccA_in"].opt()], outs=[g["ccA_out"].opt()]),
                wait=[eS1, eS2], sig="cc")
            def emit_compose():
                eM1 = p.dma("sync", pmT[:, :], g["ssd_premaskT"][:, :], wait=[evCC], sig="dM")
                eM2 = p.dma("sync", pmiT[:, :], g["ssd_maskinvT"][:, :], wait=[eM1], sig="dM")
                eLg = None
                for sdr in range(8):
                    eLg = p.dma("sync", lgb[sdr:sdr + 1, 0:8],
                                g["ccA_out"][sdr * 65 + 64:sdr * 65 + 65, 0:8],
                                wait=[eM2 if sdr == 0 else eLg], sig="dM")
                eLgf = p.step("vector", lambda e: e.tensor_copy(out=lgf[:, :], in_=lgb[:, :]),
                              wait=[eLg], sig="dve")
                ja6, wpa6 = sl_pa3.get()
                eTg = p.step("tensor", lambda e, ja6=ja6: e.transpose(
                    PA[ja6][0:8, 0:8], lgf[:, :], EYEF[0:8, 0:8]),
                    wait=[eLgf, wpa6], sig="pe")
                eLm = p.step("vector", lambda e, ja6=ja6: e.tensor_copy(
                    out=lmT[:, :], in_=PA[ja6][0:8, 0:8]), wait=[eTg], sig="dve")
                sl_pa3.done(ja6, eLm)
                eLp1 = p.step("vector", lambda e: e.tensor_tensor(
                    out=lmT[:, :], in0=lmT[:, :], in1=pmT[:, :], op=ALU.mult),
                    wait=[eLm, p.ev("dM")], sig="dve")
                eLp2 = p.step("vector", lambda e: e.tensor_tensor(
                    out=lmT[:, :], in0=lmT[:, :], in1=pmiT[:, :], op=ALU.add),
                    wait=[eLp1], sig="dve")
                evR0 = p.step("vector", lambda e: e.tensor_tensor(
                    out=lgf[:, :], in0=pmT[:, :], in1=pmiT[:, :], op=ALU.add),
                    wait=[eLp2, eLgf], sig="dve")
                evSuf = evR0
                for jj in range(1, 8):
                    evSuf = p.step("vector", lambda e, jj=jj: e.tensor_scalar(
                        out=lgf[:, 0:jj], in0=lgf[:, 0:jj], scalar1=lmT[0:8, jj:jj + 1],
                        scalar2=None, op0=ALU.mult), wait=[evSuf, eLp2], sig="dve")
                eCoef = p.step("vector", lambda e: e.tensor_tensor(
                    out=lgf[:, :], in0=lgf[:, :], in1=pmT[:, :], op=ALU.mult),
                    wait=[evSuf], sig="dve")
                eFlat = p.dma("gpsimd", crow[0:1, 0:64], lgf[:, :], wait=[eCoef], sig="dS")
                ja7, wpa7 = sl_pa3.get()
                eRb = p.step("tensor", lambda e, ja7=ja7: e.matmul(
                    PA[ja7][0:64, 0:64], ONESF[0:1, 0:64], crow[0:1, 0:64],
                    start=True, stop=True), wait=[eFlat, wpa7], sig="pe")
                eCb = p.step("vector", lambda e, ja7=ja7: e.tensor_copy(
                    out=coefb[:, :], in_=PA[ja7][0:64, 0:64]), wait=[eRb], sig="dve")
                sl_pa3.done(ja7, eCb)
                evHs = None
                evCompose = eHb
                for sdr in range(8):
                    evHs = p.dma("sync", hsb[:, :], g["ccA_out"][sdr * 65:sdr * 65 + 64, :],
                                 wait=[evCompose, evHs], sig="dR")
                    for h in range(8):
                        if sdr == 0:
                            evCompose = p.step("vector", lambda e, h=h, sdr=sdr: e.tensor_scalar(
                                out=hall[:, h * 256:(h + 1) * 256],
                                in0=hsb[:, h * 256:(h + 1) * 256],
                                scalar1=coefb[0:64, h * 8 + sdr:h * 8 + sdr + 1], scalar2=None,
                                op0=ALU.mult),
                                wait=[evHs, eCb, eHb], sig="dve")
                        else:
                            evCompose = p.step("vector", lambda e, h=h, sdr=sdr: e.scalar_tensor_tensor(
                                out=hall[:, h * 256:(h + 1) * 256],
                                in0=hsb[:, h * 256:(h + 1) * 256],
                                scalar=coefb[0:64, h * 8 + sdr:h * 8 + sdr + 1],
                                in1=hall[:, h * 256:(h + 1) * 256],
                                op0=ALU.mult, op1=ALU.add),
                                wait=[evHs, eCb, evCompose], sig="dve")
                evHin = evCompose
                eHb2 = p.step("vector", lambda e: e.tensor_copy(out=hbf[0:64, :], in_=hall[:, :]),
                              wait=[evHin], sig="dve")
                eHb2 = p.dma("gpsimd", hbf[64:128, :], hbf[0:64, :], wait=[eHb2], sig="dS")
                return eHb2
            if stage == 4:
                eHb2 = emit_compose()

        if stage >= 5:
            a3a.free()
            a3b = nc.auto_arena()
            p.arenas.append(a3b)
            xtokB0 = a3b.sbuf_tensor("xtokB0", [128, 2048], BF16)
            xtokB = [xtokB0, xtokB0]
            sztok = a3b.sbuf_tensor("sztok", [128, 4 * 2048], BF16)
            CTs = a3b.sbuf_tensor("CTs", [128, 4 * 512], BF16)
            dstb = [a3b.sbuf_tensor(f"dstb{j}", [128, 128], BF16) for j in range(2)]
            yrow = a3b.sbuf_tensor("yrow", [128, 2048], F32)
            ynb = a3b.sbuf_tensor("ynb", [128, 2048], BF16)
            ynT = a3b.sbuf_tensor("ynT", [128, 16 * 256], BF16)
            wko0 = a3b.sbuf_tensor("wko0", [128, 1024], BF16)
            wko = [wko0, wko0]
            wpanB = [a3b.sbuf_tensor(f"wpanB{j}", [128, 1024], BF16) for j in range(2)]
            zsil0 = a3b.sbuf_tensor("zsil0", [128, 512], BF16)
            zsil = [zsil0, zsil0]
            trilsb = a3b.sbuf_tensor("trilsb", [128, 128], F32)
            a3b.allocate()
            pipeX = DmaPipe(p, "dXt", n=2, engine="scalar")
            pipeZ = DmaPipe(p, "dZz", n=2, engine="scalar")
            pipeO = DmaPipe(p, "dWo", n=2)
            sl_xB = Slots(1)
            sl_zs = Slots(1)
            sl_dst = Slots(2)
            sl_wko = Slots(1)
            sl_wpB = Slots(2)
            eTril = p.dma("sync", trilsb[:, :], g["trilT"][:, :],
                          wait=[p.ev("dK"), p.ev("dV")], sig="dK")
            ev_yrow = None
            ev_ynb = None
            ev_ynT = [None, None]
            ev_sz = None
            ev_CT = None
            ev_hbf = None
            ev_Rw = p.ev("dve")
            for s in range(4):
                cb3 = s * 512 + 3
                ev_sz_copies = []
                for ii in range(16):
                    jw2, wpB = sl_wpB.get()
                    ePan = pipeZ.load_group(
                        [(cvo(wpanB[jw2], 8, 128), cvi(g["winbf"], 0, 8, ii * 128, 128))],
                        wait=[wpB, evPrep] + ([p.ev("dV")] if (s == 0 and ii < 2) else []))
                    ja, wpa = sl_pa3.get()
                    mmv = None
                    for kk in range(8):
                        mmv = p.step("tensor", lambda e, kk=kk, ja=ja, cb3=cb3, jw2=jw2: e.matmul(
                            PA[ja][:, :], wpanB[jw2][:, kk * 128:(kk + 1) * 128],
                            h1T[:, kk * (TPC + 3) + cb3: kk * (TPC + 3) + cb3 + 512],
                            start=(kk == 0), stop=(kk == 7)),
                            wait=[ePan, wpa] if kk == 0 else [], sig="pe" if kk == 7 else None)
                    sl_wpB.done(jw2, mmv)
                    jz, wz = sl_zs.get()
                    ez = p.step("scalar", lambda e, ja=ja, jz=jz: e.activation(
                        out=zsil[jz][:, :], in_=PA[ja][:, :], func=AF.Silu),
                        wait=[mmv, wz], sig="act")
                    sl_pa3.done(ja, ez)
                    lastT = None
                    for tt in range(4):
                        jt3, wt3 = sl_pt3.get()
                        eT = p.step("tensor", lambda e, tt=tt, jt3=jt3, jz=jz: e.transpose(
                            PTs[jt3][:, 0:128], zsil[jz][:, tt * 128:(tt + 1) * 128], EYE[:, :]),
                            wait=[ez, wt3], sig="pe")
                        ecp = p.step("vector", lambda e, tt=tt, jt3=jt3, ii=ii: e.tensor_copy(
                            out=sztok[:, tt * 2048 + ii * 128: tt * 2048 + (ii + 1) * 128],
                            in_=PTs[jt3][:, 0:128]),
                            wait=[eT, ev_sz], sig="dve")
                        sl_pt3.done(jt3, ecp)
                        ev_sz_copies.append(ecp)
                        lastT = eT
                    sl_zs.done(jz, lastT)
                ev_sz = ev_sz_copies[-1]
                evCT_new = None
                for ic in range(4):
                    m = 36 + ic
                    jw2, wpB = sl_wpB.get()
                    ePan = pipeZ.load_group(
                        [(cvo(wpanB[jw2], 8, 128), cvi(g["winbf"], 0, 8, m * 128, 128))],
                        wait=[wpB, evPrep])
                    ja, wpa = sl_pa3.get()
                    mmv = None
                    for kk in range(8):
                        mmv = p.step("tensor", lambda e, kk=kk, ja=ja, cb3=cb3, jw2=jw2: e.matmul(
                            PA[ja][:, :], wpanB[jw2][:, kk * 128:(kk + 1) * 128],
                            h1T[:, kk * (TPC + 3) + cb3: kk * (TPC + 3) + cb3 + 512],
                            start=(kk == 0), stop=(kk == 7)),
                            wait=[ePan, wpa] if kk == 0 else [], sig="pe" if kk == 7 else None)
                    sl_wpB.done(jw2, mmv)
                    evCT_new = p.step("scalar", lambda e, ja=ja, ic=ic: e.activation(
                        out=CTs[:, ic * 512:(ic + 1) * 512], in_=PA[ja][:, :], func=AF.Copy),
                        wait=[mmv, ev_CT], sig="act")
                    sl_pa3.done(ja, evCT_new)
                ev_CT = evCT_new
                if s == 0:
                    ev_hbf = emit_compose()
                    pipeW8 = DmaPipe(p, "dW8", n=2, engine="scalar")
                    pipeP8 = DmaPipe(p, "dP8", n=2, engine="scalar")
                    evprev8 = None
                    evP8st = None
                    evT8last = None
                    for wname8, dst8 in (("Wq", "wqbfT"), ("Wk", "wkbfT"), ("Wv", "wvbfT"),
                                         ("Wg", "wgbfT"), ("Wo", "wobfT")):
                        for kt8 in range(8):
                            for hh8 in range(2):
                                eCst8 = None
                                for qq8 in range(2):
                                    c08 = hh8 * 512 + qq8 * 256
                                    eL8 = pipeW8.load(
                                        wf8[:, :],
                                        g[wname8][kt8 * 128:(kt8 + 1) * 128, c08:c08 + 256],
                                        wait=[evprev8])
                                    eCst8 = p.step("vector", lambda e, qq8=qq8: e.tensor_copy(
                                        out=wA8[:, qq8 * 256:(qq8 + 1) * 256], in_=wf8[:, :]),
                                        wait=[eL8, evT8last], sig="dve")
                                    evprev8 = eCst8
                                lastD8 = None
                                for fb in range(4):
                                    jt3, wt3 = sl_pt3.get()
                                    eT8 = p.step("tensor", lambda e, fb=fb,
                                                 jt3=jt3: e.transpose(
                                        PTs[jt3][:, 0:128], wA8[:, fb * 128:(fb + 1) * 128],
                                        EYE[:, :]),
                                        wait=[eCst8, wt3], sig="pe")
                                    evT8last = eT8
                                    eC8b = p.step("vector", lambda e, jt3=jt3,
                                                  fb=fb: e.tensor_copy(
                                        out=wB8[:, fb * 128:(fb + 1) * 128],
                                        in_=PTs[jt3][:, 0:128]),
                                        wait=[eT8] + ([evP8st] if fb == 0 else []),
                                        sig="dve")
                                    sl_pt3.done(jt3, eC8b)
                                    lastD8 = eC8b
                                eSt8 = pipeP8.load(cvi(g[dst8], hh8 * 512, 4, kt8 * 128, 128),
                                                   cvo(wB8, 4, 128), wait=[lastD8])
                                evP8st = eSt8
                for tt in range(4):
                    c = s * 4 + tt
                    jx, wx = sl_xB.get()
                    evX = pipeX.load(xtokB[jx][:, :],
                                     g["xtok_park"][c * 128:(c + 1) * 128, :],
                                     wait=[wx, p.ev("dV")])
                    ev_ev = [None] * 8
                    for h in range(8):
                        tb, ro = h // 2, (h % 2) * 64
                        ja, wpa = sl_pa3.get()
                        mmG = p.step("tensor", lambda e, ja=ja, tb=tb, ro=ro, c=c, tt=tt: e.matmul(
                            PA[ja][:, 0:128],
                            BT[ro:ro + 64, tb * TPC + c * 128: tb * TPC + (c + 1) * 128],
                            CTs[ro:ro + 64, tb * 512 + tt * 128: tb * 512 + (tt + 1) * 128],
                            start=True, stop=True),
                            wait=[ev_BT[tb], ev_CT, wpa], sig="pe")
                        jd, wd = sl_dst.get()
                        eDst = p.step("vector", lambda e, ja=ja, jd=jd, c=c, h=h: e.scalar_tensor_tensor(
                            out=dstb[jd][:, :], in0=PA[ja][:, 0:128],
                            scalar=eTall[:, c * 24 + 8 + h:c * 24 + 9 + h],
                            in1=trilsb[:, :], op0=ALU.mult, op1=ALU.mult),
                            wait=[mmG, wd, ev_eT[c], p.ev("dK")], sig="dve")
                        sl_pa3.done(ja, eDst)
                        mm1 = p.step("tensor", lambda e, jd=jd, jx=jx, h=h: e.matmul(
                            PW[h // 2][:, (h % 2) * 256:((h % 2) + 1) * 256],
                            dstb[jd][:, :], xtokB[jx][:, h * 256:(h + 1) * 256],
                            start=True, stop=False),
                            wait=[eDst, evX, pw_free3[h // 2]], sig=None)
                        mm2 = p.step("tensor", lambda e, tb=tb, ro=ro, tt=tt, h=h: e.matmul(
                            PW[h // 2][:, (h % 2) * 256:((h % 2) + 1) * 256],
                            CTs[ro:ro + 64, tb * 512 + tt * 128: tb * 512 + (tt + 1) * 128],
                            hbf[ro:ro + 64, h * 256:(h + 1) * 256],
                            start=False, stop=False),
                            wait=[ev_hbf, ev_CT], sig=None)
                        mm3 = p.step("tensor", lambda e, jx=jx, h=h: e.matmul(
                            PW[h // 2][:, (h % 2) * 256:((h % 2) + 1) * 256],
                            deye[:, h * 128:(h + 1) * 128], xtokB[jx][:, h * 256:(h + 1) * 256],
                            start=False, stop=True),
                            wait=[evDeyeAll], sig="pe")
                        sl_dst.done(jd, mm1)
                        sl_xB.done(jx, mm3)
                        ev_ev[h] = mm3
                        eY = p.step("scalar", lambda e, h=h, c=c: e.activation(
                            out=yrow[:, h * 256:(h + 1) * 256],
                            in_=PW[h // 2][:, (h % 2) * 256:((h % 2) + 1) * 256],
                            func=AF.Copy, scale=eTall[:, c * 24 + h:c * 24 + 1 + h]),
                            wait=[mm3, ev_yrow, ev_eT[c]], sig="act")
                        pw_free3[h // 2] = eY
                        ev_ev[h] = eY
                    for h in range(8):
                        tb, ro = h // 2, (h % 2) * 64
                        jt3, wt3 = sl_pt3.get()
                        eTb = p.step("tensor", lambda e, tb=tb, ro=ro, c=c, jt3=jt3: e.transpose(
                            PTs[jt3][:, 0:64],
                            BT[ro:ro + 64, tb * TPC + c * 128: tb * TPC + (c + 1) * 128],
                            EYE[ro:ro + 64, ro:ro + 64]),
                            wait=[ev_BT[tb], wt3], sig="pe")
                        jb, wb = sl_bs.get()
                        eBs = p.step("scalar", lambda e, jt3=jt3, jb=jb, c=c, h=h: e.activation(
                            out=Bs[jb][:, :], in_=PTs[jt3][:, 0:64], func=AF.Copy,
                            scale=eTall[:, c * 24 + 16 + h:c * 24 + 17 + h]),
                            wait=[eTb, wb, ev_eT[c]], sig="act")
                        sl_pt3.done(jt3, eBs)
                        ja, wpa = sl_pa3.get()
                        mmD = p.step("tensor", lambda e, jb=jb, jx=jx, h=h, ja=ja: e.matmul(
                            PA[ja][0:64, 0:256], Bs[jb][:, :],
                            xtokB[jx][:, h * 256:(h + 1) * 256],
                            start=True, stop=True),
                            wait=[eBs, evX, wpa], sig="pe")
                        sl_bs.done(jb, mmD)
                        upd = p.step("vector", lambda e, h=h, c=c, ja=ja: e.scalar_tensor_tensor(
                            out=hall[:, h * 256:(h + 1) * 256], in0=hall[:, h * 256:(h + 1) * 256],
                            scalar=lambc[0:64, c * 24 + h:c * 24 + h + 1],
                            in1=PA[ja][0:64, 0:256],
                            op0=ALU.mult, op1=ALU.add),
                            wait=[mmD, ev_lambc[s], hall_upd[h], ev_hbf], sig="dve")
                        sl_pa3.done(ja, upd)
                        hall_upd[h] = upd
                        last_mmD = mmD
                    sl_xB.done(jx, last_mmD)
                    ev_hbf = p.step("vector", lambda e: e.tensor_copy(
                        out=hbf[0:64, :], in_=hall[:, :]), wait=hall_upd + [ev_hbf], sig="dve")
                    ev_hbf = p.dma("gpsimd", hbf[64:128, :], hbf[0:64, :],
                                   wait=[ev_hbf], sig="dS")
                    eGat = p.step("vector", lambda e, tt=tt: e.tensor_tensor(
                        out=yrow[:, :], in0=yrow[:, :], in1=sztok[:, tt * 2048:(tt + 1) * 2048],
                        op=ALU.mult), wait=ev_ev + [ev_sz], sig="dve")
                    eSq = p.step("scalar", lambda e: e.activation(
                        out=ynb[:, :], in_=yrow[:, :], func=AF.Square,
                        accum_out=small[0:128, 12:13]),
                        wait=[eGat, ev_ynb], sig="act")
                    eRv2 = p.step("vector", lambda e: e.tensor_scalar(
                        out=small[0:128, 13:14], in0=small[0:128, 12:13],
                        scalar1=1.0 / DI, scalar2=EPS0, op0=ALU.mult, op1=ALU.add),
                        wait=[eSq], sig="dve")
                    eSr = p.step("scalar", lambda e: e.activation(
                        out=small[0:128, 14:15], in_=small[0:128, 13:14], func=AF.Sqrt),
                        wait=[eRv2], sig="act")
                    eRc = p.step("vector", lambda e: e.reciprocal(
                        out=small[0:128, 13:14], in_=small[0:128, 14:15]),
                        wait=[eSr], sig="dve")
                    eYn = p.step("scalar", lambda e: e.activation(
                        out=ynb[:, :], in_=yrow[:, :], func=AF.Copy,
                        scale=small[0:128, 13:14]),
                        wait=[eRc, eSq], sig="act")
                    ev_yrow = eYn
                    lastTy = None
                    for ff in range(16):
                        jt3, wt3 = sl_pt3.get()
                        eT = p.step("tensor", lambda e, ff=ff, jt3=jt3: e.transpose(
                            PTs[jt3][:, 0:128], ynb[:, ff * 128:(ff + 1) * 128], EYE[:, :]),
                            wait=[eYn, wt3], sig="pe")
                        ecp = p.step("vector", lambda e, ff=ff, tt=tt, jt3=jt3: e.tensor_copy(
                            out=ynT[:, ff * 256 + (tt % 2) * 128: ff * 256 + (tt % 2) * 128 + 128],
                            in_=PTs[jt3][:, 0:128]),
                            wait=[eT, ev_ynT[tt % 2]], sig="dve")
                        sl_pt3.done(jt3, ecp)
                        lastTy = ecp
                    ev_ynb = lastTy
                    ev_ynT[tt % 2] = lastTy
                    if tt % 2 == 1:
                        ev_mm_out = [None] * 4
                        for k2 in range(16):
                            jko, wko_ = sl_wko.get()
                            eKo = pipeO.load(wko[jko][:, :],
                                             g["woutbf"][k2 * 128:(k2 + 1) * 128, :],
                                             wait=[wko_])
                            for tt2 in range(2):
                                for nh in range(2):
                                    pwidx = tt2 * 2 + nh
                                    ev_mm_out[pwidx] = p.step("tensor",
                                        lambda e, k2=k2, tt2=tt2, nh=nh, jko=jko, pwidx=pwidx: e.matmul(
                                            PW[pwidx][:, :],
                                            ynT[:, k2 * 256 + tt2 * 128: k2 * 256 + tt2 * 128 + 128],
                                            wko[jko][:, nh * 512:(nh + 1) * 512],
                                            start=(k2 == 0), stop=(k2 == 15)),
                                        wait=[eKo, ev_ynT[0], ev_ynT[1]] +
                                             ([pw_free3[pwidx]] if k2 == 0 else []),
                                        sig="pe" if (pwidx == 3 or k2 == 15) else None)
                            sl_wko.done(jko, ev_mm_out[3] or p.ev("pe"))
                        for tt2 in range(2):
                            ti = s * 4 + (tt - 1) + tt2
                            for nh in range(2):
                                pwidx = tt2 * 2 + nh
                                evR = p.step("vector", lambda e, ti=ti, nh=nh, pwidx=pwidx: e.tensor_tensor(
                                    out=R[:, ti * DM + nh * 512: ti * DM + (nh + 1) * 512],
                                    in0=R[:, ti * DM + nh * 512: ti * DM + (nh + 1) * 512],
                                    in1=PW[pwidx][:, :], op=ALU.add),
                                    wait=[ev_mm_out[pwidx], ev_Rw], sig="dve")
                                pw_free3[pwidx] = evR
                                ev_Rw = evR
                        ev_ynT = [ev_Rw, ev_Rw]

        if stage >= 6:
            p.barrier()
            a3b.free()
            a3c.free()
            aH.free()
            a5 = nc.auto_arena()
            p.arenas.append(a5)
            xnT1 = a5.sbuf_tensor("xnT1", [128, 8 * TPC], BF16); g["xnT1"] = xnT1
            xqT = a5.sbuf_tensor("xqT", [128, 8 * TPC], BF16); g["xqT"] = xqT
            stL = a5.sbuf_tensor("stL", [128, 48], F32)
            lnX = a5.sbuf_tensor("lnX", [128, 1024], F32)
            lnS = a5.sbuf_tensor("lnS", [128, 1024], F32)
            lnB = a5.sbuf_tensor("lnB", [128, 1024], BF16)
            lnI = a5.sbuf_tensor("lnI", [128, 1024], I32)
            wst = a5.sbuf_tensor("wst", [128, 1024], F32)
            wstb = [a5.sbuf_tensor(f"wstb{j}", [128, 1024], BF16) for j in range(2)]
            wstf = [wst, a5.sbuf_tensor("wstf1", [128, 1024], F32)]
            wstf3 = a5.sbuf_tensor("wstf3", [128, DFI - 2048], F32)
            lnI2 = a5.sbuf_tensor("lnI2", [128, 1024], I32)
            wstc = a5.sbuf_tensor("wstc", [128, 1024], BF16)
            scl = a5.sbuf_tensor("scl", [128, 64], F32); g["scl"] = scl
            sm1 = a5.sbuf_tensor("sm1", [8, 32], F32)
            a5.allocate()
            evR5 = p.ev("dve")
            evA5 = p.ev("act")
            sl_ln1 = Slots(1)
            sl_wb = Slots(2)
            evMax = None
            for i in range(NTT):
                src_ = R[:, i * DM:(i + 1) * DM]
                w0 = [evR5, evA5] if i == 0 else []
                eS2 = p.step("scalar", lambda e, src_=src_: e.activation(
                    out=lnS[:, :], in_=src_, func=AF.Square, accum_out=SL[0:128, 0:1]),
                    wait=w0 + [p.ev("dve")], sig="act")
                eS1 = p.step("vector", lambda e, src_=src_: e.reduce_sum(
                    out=SL[0:128, 1:2], in_=src_, axis=AX.X), wait=w0, sig="dve")
                eM = p.step("vector", lambda e: e.tensor_scalar(
                    out=SL[0:128, 2:3], in0=SL[0:128, 1:2], scalar1=1.0 / DM, scalar2=None,
                    op0=ALU.mult), wait=[eS1], sig="dve")
                eV1 = p.step("vector", lambda e: e.tensor_scalar(
                    out=SL[0:128, 3:4], in0=SL[0:128, 0:1], scalar1=1.0 / DM, scalar2=1e-5,
                    op0=ALU.mult, op1=ALU.add), wait=[eS2], sig="dve")
                eV2 = p.step("vector", lambda e: e.scalar_tensor_tensor(
                    out=SL[0:128, 4:5], in0=SL[0:128, 2:3], scalar=SL[0:128, 2:3],
                    in1=SL[0:128, 3:4], op0=ALU.mult, op1=ALU.subtract),
                    wait=[eM, eV1], sig="dve")
                eV3 = p.step("vector", lambda e: e.tensor_scalar(
                    out=SL[0:128, 4:5], in0=SL[0:128, 4:5], scalar1=-1.0, scalar2=None,
                    op0=ALU.mult), wait=[eV2], sig="dve")
                eSq = p.step("scalar", lambda e: e.activation(
                    out=SL[0:128, 5:6], in_=SL[0:128, 4:5], func=AF.Sqrt), wait=[eV3], sig="act")
                eRs = p.step("vector", lambda e, i=i: e.reciprocal(
                    out=stL[:, i:i + 1], in_=SL[0:128, 5:6]), wait=[eSq], sig="dve")
                eNb = p.step("vector", lambda e, i=i: e.scalar_tensor_tensor(
                    out=stL[:, 16 + i:17 + i], in0=SL[0:128, 2:3], scalar=-1.0,
                    in1=stL[:, i:i + 1], op0=ALU.mult, op1=ALU.mult),
                    wait=[eRs], sig="dve")
                j1, wl1 = sl_ln1.get()
                eXn = p.step("vector", lambda e, src_=src_, i=i: e.tensor_scalar(
                    out=lnX[:, :], in0=src_, scalar1=stL[:, i:i + 1],
                    scalar2=stL[:, 16 + i:17 + i], op0=ALU.mult, op1=ALU.add),
                    wait=[eNb, wl1], sig="dve")
                eAm = p.step("vector", lambda e: e.reduce_max(
                    out=SL[0:128, 6:7], in_=lnX[:, :], axis=AX.X, apply_absolute_value=True),
                    wait=[eXn], sig="dve")
                if i == 0:
                    evMax = p.step("vector", lambda e: e.tensor_copy(
                        out=SL[0:128, 7:8], in_=SL[0:128, 6:7]), wait=[eAm], sig="dve")
                else:
                    evMax = p.step("vector", lambda e: e.tensor_tensor(
                        out=SL[0:128, 7:8], in0=SL[0:128, 7:8], in1=SL[0:128, 6:7],
                        op=ALU.max), wait=[eAm, evMax], sig="dve")
                eXb = p.step("vector", lambda e: e.tensor_copy(
                    out=lnB[:, :], in_=lnX[:, :]), wait=[eXn], sig="dve")
                lastc = None
                for f in range(8):
                    jt, wt = sl_pt.get()
                    eT = p.step("tensor", lambda e, f=f, jt=jt: e.transpose(
                        PTs[jt][:, 0:128], lnB[:, f * 128:(f + 1) * 128], EYE[:, :]),
                        wait=[eXb, wt], sig="pe")
                    lastc = p.step("vector", lambda e, i=i, f=f, jt=jt: e.tensor_copy(
                        out=xnT1[:, f * TPC + i * 128: f * TPC + (i + 1) * 128],
                        in_=PTs[jt][:, 0:128]), wait=[eT], sig="dve")
                    sl_pt.done(jt, lastc)
                sl_ln1.done(j1, lastc)
            ja, wpa = sl_pa3.get()
            eTm = p.step("tensor", lambda e, ja=ja: e.transpose(
                PA[ja][0:1, 0:128], SL[0:128, 7:8], EYEF[:, :]),
                wait=[evMax, wpa], sig="pe")
            eMx = p.step("vector", lambda e, ja=ja: e.reduce_max(
                out=sm1[0:1, 0:1], in_=PA[ja][0:1, 0:128], axis=AX.X), wait=[eTm], sig="dve")
            sl_pa3.done(ja, eMx)
            eZm = p.step("gpsimd", lambda e: e.memset(sm1[0:1, 1:16], 0.0), sig="gps")
            eSm = p.dma("gpsimd", g["ccM_in"][0:1, 0:16], sm1[0:1, 0:16],
                        wait=[eMx, eZm], sig="dS")
            evCC3 = p.step("gpsimd", lambda e: e.collective_compute(
                "AllReduce", ALU.max, replica_groups=[list(range(8))],
                ins=[g["ccM_in"].opt()], outs=[g["ccM_out"].opt()]),
                wait=[eSm], sig="cc")
            eGm = p.dma("sync", sm1[0:1, 16:32], g["ccM_out"][0:1, 0:16],
                        wait=[evCC3], sig="dK")
            eCx = p.dma("sync", sm1[0:1, 4:5], g["clip_x"][:, :], wait=[eGm], sig="dK")
            eCd = p.dma("sync", sm1[0:1, 5:6], g["clip_down"][:, :], wait=[eCx], sig="dK")
            eSg = p.step("scalar", lambda e: e.activation(
                out=sm1[0:1, 6:7], in_=sm1[0:1, 4:5], func=AF.Sigmoid),
                wait=[eCx, p.ev("dK")], sig="act")
            eSx = p.step("vector", lambda e: e.scalar_tensor_tensor(
                out=sm1[0:1, 2:3], in0=sm1[0:1, 16:17], scalar=1.0 / 127.0,
                in1=sm1[1:2, 1:2].partition_broadcast(1) if False else sm1[0:1, 1:2],
                op0=ALU.mult, op1=ALU.mult), wait=[eSg, eGm], sig="dve") if False else None
            eSx = p.step("vector", lambda e: e.scalar_tensor_tensor(
                out=sm1[0:1, 2:3], in0=sm1[0:1, 16:17], scalar=1.0 / 127.0,
                in1=sm1[0:1, 6:7], op0=ALU.mult, op1=ALU.mult),
                wait=[eSg, p.ev("dK")], sig="dve")
            eIx = p.step("vector", lambda e: e.reciprocal(
                out=sm1[0:1, 3:4], in_=sm1[0:1, 2:3]), wait=[eSx], sig="dve")
            ja, wpa = sl_pa3.get()
            eB1 = p.step("tensor", lambda e, ja=ja: e.matmul(
                PA[ja][:, 0:4], ONESF[0:1, :], sm1[0:1, 0:4], start=True, stop=True),
                wait=[eIx, wpa], sig="pe")
            eB2 = p.step("vector", lambda e, ja=ja: e.tensor_copy(
                out=stL[:, 32:36], in_=PA[ja][:, 0:4]), wait=[eB1], sig="dve")
            sl_pa3.done(ja, eB2)
            evXq = None
            for i in range(NTT):
                src_ = R[:, i * DM:(i + 1) * DM]
                eR2 = p.step("vector", lambda e, i=i: e.tensor_tensor(
                    out=SL[0:128, 8:9], in0=stL[:, i:i + 1], in1=stL[:, 35:36],
                    op=ALU.mult), wait=[eB2] + ([evXq] if evXq else []), sig="dve")
                eN2 = p.step("vector", lambda e, i=i: e.tensor_tensor(
                    out=SL[0:128, 9:10], in0=stL[:, 16 + i:17 + i], in1=stL[:, 35:36],
                    op=ALU.mult), wait=[eR2], sig="dve")
                eW5 = p.step("vector", lambda e, src_=src_: e.tensor_scalar(
                    out=lnX[:, :], in0=src_, scalar1=SL[0:128, 8:9], scalar2=SL[0:128, 9:10],
                    op0=ALU.mult, op1=ALU.add), wait=[eN2], sig="dve")
                eI5 = p.step("vector", lambda e: e.tensor_copy(out=lnI[:, :], in_=lnX[:, :]),
                             wait=[eW5], sig="dve")
                eF5 = p.step("vector", lambda e: e.tensor_copy(out=lnX[:, :], in_=lnI[:, :]),
                             wait=[eI5], sig="dve")
                eC5 = p.step("vector", lambda e: e.tensor_scalar(
                    out=lnB[:, :], in0=lnX[:, :], scalar1=127.0, scalar2=-128.0,
                    op0=ALU.min, op1=ALU.max), wait=[eF5], sig="dve")
                lastc = None
                for f in range(8):
                    jt, wt = sl_pt.get()
                    eT = p.step("tensor", lambda e, f=f, jt=jt: e.transpose(
                        PTs[jt][:, 0:128], lnB[:, f * 128:(f + 1) * 128], EYE[:, :]),
                        wait=[eC5, wt], sig="pe")
                    lastc = p.step("vector", lambda e, i=i, f=f, jt=jt: e.tensor_copy(
                        out=xqT[:, f * TPC + i * 128: f * TPC + (i + 1) * 128],
                        in_=PTs[jt][:, 0:128]), wait=[eT], sig="dve")
                    sl_pt.done(jt, lastc)
                evXq = lastc

        if stage >= 7:
            wsti = lnI
            wsti2 = [lnI, lnI2]
            evb2 = [lnB, wstc]
            sl_wf = Slots(2)
            evSL = [None, None]
            evQb2 = [None, None]
            evSt2 = [None, None]
            pipeQs = DmaPipe(p, "dQs", n=2)
            pipeQl = DmaPipe(p, "dQl", n=2)
            pipeWD = DmaPipe(p, "dWD", n=3)

            def qp2_core(wname, dstname, kt, kwid, eIv, pc, par, extra_wait=()):
                jw = qp2_jw[0]
                eQ = p.step("vector", lambda e, jw=jw, kwid=kwid, pc=pc: e.tensor_scalar(
                    out=wstf[jw][:, 0:kwid], in0=wstf[jw][:, 0:kwid],
                    scalar1=SL[0:128, pc + 1:pc + 2], scalar2=127.0,
                    op0=ALU.mult, op1=ALU.mult), wait=[eIv] + list(extra_wait), sig="dve")
                evSL[par] = eQ
                eQi = p.step("vector", lambda e, jw=jw, par=par, kwid=kwid: e.tensor_copy(
                    out=wsti2[par][:, 0:kwid], in_=wstf[jw][:, 0:kwid]),
                    wait=[eQ, evQb2[par]], sig="dve")
                sl_wf.done(jw, eQi)
                jb, wb = sl_wb.get()
                eQb = p.step("gpsimd", lambda e, jb=jb, par=par, kwid=kwid: e.tensor_copy(
                    out=wstb[jb][:, 0:kwid], in_=wsti2[par][:, 0:kwid]),
                    wait=[eQi, wb], sig="gps")
                evQb2[par] = eQb
                lastC = None
                eT = None
                for fb in range(kwid // 128):
                    jt, wt = sl_pt.get()
                    eT = p.step("tensor", lambda e, jb=jb, fb=fb, jt=jt: e.transpose(
                        PTs[jt][:, 0:128], wstb[jb][:, fb * 128:(fb + 1) * 128], EYE[:, :]),
                        wait=[eQb, wt], sig="pe")
                    eC = p.step("scalar", lambda e, jt=jt, par=par, fb=fb: e.activation(
                        out=evb2[par][:, fb * 128:(fb + 1) * 128], in_=PTs[jt][:, 0:128],
                        func=AF.Copy),
                        wait=[eT] + ([evSt2[par]] if fb == 0 else []), sig="act")
                    sl_pt.done(jt, eC)
                    lastC = eC
                eSt = pipeQs.load(cvi(g[dstname], 0, kwid // 128, kt * 128, 128),
                                  cvo(evb2[par], kwid // 128, 128), wait=[lastC])
                evSt2[par] = eSt
                sl_wb.done(jb, eT)
                return eSt

            qp2_jw = [0]

            def quant_prep2(wname, dstname, sc0, ktiles, kwid, evprev):
                eSt = evprev
                for kt in range(ktiles):
                    par = kt % 2
                    jw, wwf = sl_wf.get()
                    qp2_jw[0] = jw
                    eL = pipeQl.load(wstf[jw][:, 0:kwid],
                                     g[wname][kt * 128:(kt + 1) * 128, 0:kwid],
                                     wait=[wwf] + ([evprev] if kt == 0 else []))
                    pc = 10 + 2 * par
                    eAm = p.step("vector", lambda e, jw=jw, kwid=kwid, pc=pc: e.reduce_max(
                        out=SL[0:128, pc:pc + 1], in_=wstf[jw][:, 0:kwid], axis=AX.X,
                        apply_absolute_value=True), wait=[eL, evSL[par]], sig="dve")
                    eIv = p.step("vector", lambda e, pc=pc: e.reciprocal(
                        out=SL[0:128, pc + 1:pc + 2], in_=SL[0:128, pc:pc + 1]),
                        wait=[eAm], sig="dve")
                    p.step("vector", lambda e, kt=kt, sc0=sc0, pc=pc: e.tensor_scalar(
                        out=scl[:, sc0 + kt:sc0 + kt + 1], in0=SL[0:128, pc:pc + 1],
                        scalar1=1.0 / 127.0, scalar2=None, op0=ALU.mult),
                        wait=[eIv], sig="dve")
                    eSt = qp2_core(wname, dstname, kt, kwid, eIv, pc, par)
                return eSt
            a7 = nc.auto_arena()
            p.arenas.append(a7)
            wupan = [a7.sbuf_tensor(f"wupan{j}", [128, 1024], BF16) for j in range(2)]
            wgpan = [a7.sbuf_tensor(f"wgpan{j}", [128, 1024], BF16) for j in range(2)]
            hbst = [a7.sbuf_tensor(f"hbst{j}", [128, 2048], BF16) for j in range(2)]
            sl_hb = Slots(2)
            pipeHp = DmaPipe(p, "dHp", n=2)
            a7.allocate()
            sl_g = Slots(2)
            sl_u = Slots(2)
            sl_pwU = Slots(2)
            evHm = None
            ev_hpark = None
            sg_prev = None
            evSLg = None
            evSLu = None
            for io in range(22):
                jgp, wgp = sl_g.get()
                jup, wup = sl_u.get()
                egv = [None, None]
                for wi, (wname, pans, jslot, wslot, pcA, sccol) in enumerate((
                        ("w_gate", wgpan, jgp, wgp, 10, io),
                        ("w_up", wupan, jup, wup, 12, 22 + io))):
                    jw, wwf = sl_wf.get()
                    eL = pipeQl.load(wstf[jw][:, :], g[wname][io * 128:(io + 1) * 128, :],
                                     wait=[wwf])
                    prevQ = evSLg if wi == 0 else evSLu
                    eAm = p.step("vector", lambda e, jw=jw, pcA=pcA: e.reduce_max(
                        out=SL[0:128, pcA:pcA + 1], in_=wstf[jw][:, :], axis=AX.X,
                        apply_absolute_value=True), wait=[eL, prevQ], sig="dve")
                    eIv = p.step("vector", lambda e, pcA=pcA: e.reciprocal(
                        out=SL[0:128, pcA + 1:pcA + 2], in_=SL[0:128, pcA:pcA + 1]),
                        wait=[eAm], sig="dve")
                    p.step("vector", lambda e, sccol=sccol, pcA=pcA: e.tensor_scalar(
                        out=scl[:, sccol:sccol + 1], in0=SL[0:128, pcA:pcA + 1],
                        scalar1=1.0 / 127.0, scalar2=None, op0=ALU.mult),
                        wait=[eIv], sig="dve")
                    eQ = p.step("vector", lambda e, jw=jw, pcA=pcA: e.tensor_scalar(
                        out=wstf[jw][:, :], in0=wstf[jw][:, :],
                        scalar1=SL[0:128, pcA + 1:pcA + 2], scalar2=127.0,
                        op0=ALU.mult, op1=ALU.mult), wait=[eIv], sig="dve")
                    if wi == 0:
                        evSLg = eQ
                    else:
                        evSLu = eQ
                    eQi = p.step("vector", lambda e, jw=jw, wi=wi: e.tensor_copy(
                        out=wsti2[wi][:, :], in_=wstf[jw][:, :]),
                        wait=[eQ, evQb2[wi]], sig="dve")
                    sl_wf.done(jw, eQi)
                    jb, wb = sl_wb.get()
                    eQb = p.step("gpsimd", lambda e, jb=jb, wi=wi: e.tensor_copy(
                        out=wstb[jb][:, :], in_=wsti2[wi][:, :]), wait=[eQi, wb], sig="gps")
                    evQb2[wi] = eQb
                    lastT = None
                    ev8 = None
                    for fb in range(8):
                        jt, wt = sl_pt.get()
                        eT = p.step("tensor", lambda e, jb=jb, fb=fb, jt=jt: e.transpose(
                            PTs[jt][:, 0:128], wstb[jb][:, fb * 128:(fb + 1) * 128], EYE[:, :]),
                            wait=[eQb, wt], sig="pe")
                        lastT = eT
                        ev8 = p.step("scalar", lambda e, jt=jt, pans=pans, jslot=jslot,
                                     fb=fb: e.activation(
                            out=pans[jslot][:, fb * 128:(fb + 1) * 128], in_=PTs[jt][:, 0:128],
                            func=AF.Copy),
                            wait=[eT] + ([wslot] if fb == 0 else []), sig="act")
                        sl_pt.done(jt, ev8)
                    egv[wi] = ev8
                    sl_wb.done(jb, lastT)
                ePg, ePu = egv
                eSg2 = p.step("vector", lambda e, io=io: e.tensor_tensor(
                    out=stL[:, 38:39], in0=scl[:, io:io + 1], in1=stL[:, 34:35],
                    op=ALU.mult), wait=[eB2, p.ev("dve"), sg_prev], sig="dve")
                eSu2 = p.step("vector", lambda e, io=io: e.tensor_tensor(
                    out=stL[:, 39:40], in0=scl[:, 22 + io:23 + io], in1=stL[:, 34:35],
                    op=ALU.mult), wait=[eSg2], sig="dve")
                lastg = None
                for nn in range(4):
                    ja, wpa = sl_pa3.get()
                    mmg = None
                    for kk in range(8):
                        mmg = p.step("tensor", lambda e, kk=kk, ja=ja, nn=nn, jgp=jgp: e.matmul(
                            PA[ja][:, :], wgpan[jgp][:, kk * 128:(kk + 1) * 128],
                            xqT[:, kk * TPC + nn * 512: kk * TPC + (nn + 1) * 512],
                            start=(kk == 0), stop=(kk == 7)),
                            wait=[ePg, wpa, evXq] if kk == 0 else [],
                            sig="pe" if kk == 7 else None)
                    ju2, wpw = sl_pwU.get()
                    mmu = None
                    for kk in range(8):
                        mmu = p.step("tensor", lambda e, kk=kk, ju2=ju2, nn=nn, jup=jup: e.matmul(
                            PW[ju2][:, :], wupan[jup][:, kk * 128:(kk + 1) * 128],
                            xqT[:, kk * TPC + nn * 512: kk * TPC + (nn + 1) * 512],
                            start=(kk == 0), stop=(kk == 7)),
                            wait=[ePu, wpw] + ([pw_free3[ju2]] if pw_free3[ju2] else [])
                            if kk == 0 else [],
                            sig="pe" if kk == 7 else None)
                    eGs = p.step("scalar", lambda e, ja=ja, nn=nn: e.activation(
                        out=lnX[:, (nn % 2) * 512:((nn % 2) + 1) * 512], in_=PA[ja][:, :],
                        func=AF.Silu, scale=stL[:, 38:39]),
                        wait=[mmg, eSu2] + ([lastg] if nn >= 2 else []), sig="act")
                    sl_pa3.done(ja, eGs)
                    eUs = p.step("scalar", lambda e, ju2=ju2, nn=nn: e.activation(
                        out=lnS[:, (nn % 2) * 512:((nn % 2) + 1) * 512], in_=PW[ju2][:, :],
                        func=AF.Copy, scale=stL[:, 39:40]),
                        wait=[mmu, eSu2] + ([lastg] if nn >= 2 else []), sig="act")
                    sl_pwU.done(ju2, eUs)
                    pw_free3[ju2] = eUs
                    eH = p.step("vector", lambda e, nn=nn: e.tensor_tensor(
                        out=lnX[:, (nn % 2) * 512:((nn % 2) + 1) * 512],
                        in0=lnX[:, (nn % 2) * 512:((nn % 2) + 1) * 512],
                        in1=lnS[:, (nn % 2) * 512:((nn % 2) + 1) * 512], op=ALU.mult),
                        wait=[eGs, eUs], sig="dve")
                    eAm2 = p.step("vector", lambda e, nn=nn: e.reduce_max(
                        out=stL[:, 40:41], in_=lnX[:, (nn % 2) * 512:((nn % 2) + 1) * 512],
                        axis=AX.X, apply_absolute_value=True), wait=[eH], sig="dve")
                    if evHm is None:
                        evHm = p.step("vector", lambda e: e.tensor_copy(
                            out=stL[:, 41:42], in_=stL[:, 40:41]), wait=[eAm2], sig="dve")
                    else:
                        evHm = p.step("vector", lambda e: e.tensor_tensor(
                            out=stL[:, 41:42], in0=stL[:, 41:42], in1=stL[:, 40:41],
                            op=ALU.max), wait=[eAm2, evHm], sig="dve")
                    if nn == 0:
                        jhb, whb = sl_hb.get()
                    eHb5 = p.step("vector", lambda e, nn=nn, jhb=jhb: e.tensor_copy(
                        out=hbst[jhb][:, nn * 512:(nn + 1) * 512],
                        in_=lnX[:, (nn % 2) * 512:((nn % 2) + 1) * 512]),
                        wait=[eH] + ([whb] if nn == 0 else []), sig="dve")
                    if nn == 3:
                        ev_hpark = pipeHp.load(g["hT_park"][io * 128:(io + 1) * 128, :],
                                               hbst[jhb][:, :], wait=[eHb5])
                        sl_hb.done(jhb, ev_hpark)
                    lastg = eH
                sl_g.done(jgp, mmg)
                sl_u.done(jup, mmu)
                sg_prev = eSu2
            wdbufs = [wstf[0], wstf[1], wstf3]
            wdfree = [sl_wf.busy[0], sl_wf.busy[1], None]
            CHW = [1024, 1024, DFI - 2048]
            evSclK = evSLg
            evMx1K = evSLu
            evMx2K = None
            evqd = None
            for kt in range(8):
                eLs = []
                eAms = []
                for qq in range(3):
                    wid = CHW[qq]
                    eL = pipeWD.load(wdbufs[qq][:, 0:wid],
                                     g["w_down"][kt * 128:(kt + 1) * 128,
                                                 qq * 1024:qq * 1024 + wid],
                                     wait=[wdfree[qq]] + ([evqd] if kt == 0 and qq == 0 else []))
                    pcq = [10, 12, 14][qq]
                    gw = [evSclK, evMx1K, evMx2K][qq]
                    eAm = p.step("vector", lambda e, qq=qq, wid=wid, pcq=pcq: e.reduce_max(
                        out=SL[0:128, pcq:pcq + 1], in_=wdbufs[qq][:, 0:wid], axis=AX.X,
                        apply_absolute_value=True), wait=[eL, gw], sig="dve")
                    eAms.append(eAm)
                eMx1 = p.step("vector", lambda e: e.tensor_tensor(
                    out=SL[0:128, 10:11], in0=SL[0:128, 10:11], in1=SL[0:128, 12:13],
                    op=ALU.max), wait=[eAms[0], eAms[1]], sig="dve")
                evMx1K = eMx1
                eMx2 = p.step("vector", lambda e: e.tensor_tensor(
                    out=SL[0:128, 10:11], in0=SL[0:128, 10:11], in1=SL[0:128, 14:15],
                    op=ALU.max), wait=[eMx1, eAms[2]], sig="dve")
                evMx2K = eMx2
                eIv = p.step("vector", lambda e: e.reciprocal(
                    out=SL[0:128, 11:12], in_=SL[0:128, 10:11]), wait=[eMx2], sig="dve")
                eScl = p.step("vector", lambda e, kt=kt: e.tensor_scalar(
                    out=scl[:, 44 + kt:45 + kt], in0=SL[0:128, 10:11],
                    scalar1=1.0 / 127.0, scalar2=None, op0=ALU.mult), wait=[eIv], sig="dve")
                evSclK = eScl
                for qq in range(3):
                    wid = CHW[qq]
                    par = (kt * 3 + qq) % 2
                    eQ = p.step("vector", lambda e, qq=qq, wid=wid: e.tensor_scalar(
                        out=wdbufs[qq][:, 0:wid], in0=wdbufs[qq][:, 0:wid],
                        scalar1=SL[0:128, 11:12], scalar2=127.0,
                        op0=ALU.mult, op1=ALU.mult), wait=[eScl], sig="dve")
                    eQi = p.step("vector", lambda e, qq=qq, par=par, wid=wid: e.tensor_copy(
                        out=wsti2[par][:, 0:wid], in_=wdbufs[qq][:, 0:wid]),
                        wait=[eQ, evQb2[par]], sig="dve")
                    wdfree[qq] = eQi
                    jb, wb = sl_wb.get()
                    eQb = p.step("gpsimd", lambda e, jb=jb, par=par, wid=wid: e.tensor_copy(
                        out=wstb[jb][:, 0:wid], in_=wsti2[par][:, 0:wid]),
                        wait=[eQi, wb], sig="gps")
                    evQb2[par] = eQb
                    lastC = None
                    eT = None
                    for fb in range(wid // 128):
                        jt, wt = sl_pt.get()
                        eT = p.step("tensor", lambda e, jb=jb, fb=fb, jt=jt: e.transpose(
                            PTs[jt][:, 0:128], wstb[jb][:, fb * 128:(fb + 1) * 128], EYE[:, :]),
                            wait=[eQb, wt], sig="pe")
                        eC = p.step("scalar", lambda e, jt=jt, par=par, fb=fb: e.activation(
                            out=evb2[par][:, fb * 128:(fb + 1) * 128], in_=PTs[jt][:, 0:128],
                            func=AF.Copy),
                            wait=[eT] + ([evSt2[par]] if fb == 0 else []), sig="act")
                        sl_pt.done(jt, eC)
                        lastC = eC
                    eSt = pipeQs.load(cvi(g["wdTint"], qq * 1024, wid // 128, kt * 128, 128),
                                      cvo(evb2[par], wid // 128, 128), wait=[lastC])
                    evSt2[par] = eSt
                    sl_wb.done(jb, eT)
                    evqd = eSt

            ja, wpa = sl_pa3.get()
            eTm2 = p.step("tensor", lambda e, ja=ja: e.transpose(
                PA[ja][0:1, 0:128], stL[:, 41:42], EYEF[:, :]),
                wait=[evHm, wpa], sig="pe")
            eMx2 = p.step("vector", lambda e, ja=ja: e.reduce_max(
                out=sm1[0:1, 8:9], in_=PA[ja][0:1, 0:128], axis=AX.X), wait=[eTm2], sig="dve")
            sl_pa3.done(ja, eMx2)
            eZm2 = p.step("gpsimd", lambda e: e.memset(sm1[0:1, 9:24], 0.0),
                          wait=[eSx, eIx], sig="gps")
            eSm2 = p.dma("gpsimd", g["ccM_in"][0:1, 0:16], sm1[0:1, 8:24],
                         wait=[eMx2, eZm2, p.ev("dS")], sig="dS")
            evCC4 = p.step("gpsimd", lambda e: e.collective_compute(
                "AllReduce", ALU.max, replica_groups=[list(range(8))],
                ins=[g["ccM_in"].opt()], outs=[g["ccM_out"].opt()]),
                wait=[eSm2, evCC3], sig="cc")
            eGm2 = p.dma("sync", sm1[0:1, 24:32], g["ccM_out"][0:1, 8:16]
                         if False else g["ccM_out"][0:1, 0:8], wait=[evCC4, p.ev("dK")], sig="dK")
            eSgD = p.step("scalar", lambda e: e.activation(
                out=sm1[0:1, 7:8], in_=sm1[0:1, 5:6], func=AF.Sigmoid),
                wait=[eCd, p.ev("dK")], sig="act")
            eSh = p.step("vector", lambda e: e.scalar_tensor_tensor(
                out=sm1[0:1, 10:11], in0=sm1[0:1, 24:25], scalar=1.0 / 127.0,
                in1=sm1[0:1, 7:8], op0=ALU.mult, op1=ALU.mult),
                wait=[eSgD, p.ev("dK")], sig="dve")
            eIh = p.step("vector", lambda e: e.reciprocal(
                out=sm1[0:1, 11:12], in_=sm1[0:1, 10:11]), wait=[eSh], sig="dve")
            ja, wpa = sl_pa3.get()
            eB3 = p.step("tensor", lambda e, ja=ja: e.matmul(
                PA[ja][:, 0:2], ONESF[0:1, :], sm1[0:1, 10:12], start=True, stop=True),
                wait=[eIh, wpa], sig="pe")
            eB4 = p.step("vector", lambda e, ja=ja: e.tensor_copy(
                out=stL[:, 36:38], in_=PA[ja][:, 0:2]), wait=[eB3], sig="dve")
            sl_pa3.done(ja, eB4)

        if stage >= 8:
            p.dma("sync", cvrows(g["xnT1_park"], 0, 8, 0, TPC),
                  cvcols(xnT1, 8, TPC), wait=[evXq], sig="dV")
            p.dma("gpsimd", g["scl_park"][:, 0:52], scl[:, 0:52], wait=[p.ev("dve")], sig="dS")
            p.dma("gpsimd", g["scl_park"][:, 64:102], stL[:, 0:38], wait=[eB4], sig="dS")
            p.dma("gpsimd", g["scl_park"][0:1, 112:144], sm1[0:1, :], wait=[p.ev("dve")], sig="dS")
            p.barrier()
            a7.free()
            a5.free()
            a8 = nc.auto_arena()
            p.arenas.append(a8)
            xnR = a8.sbuf_tensor("xnR", [128, 8 * TPC], BF16); g["xnR"] = xnR
            wA = [a8.sbuf_tensor(f"wA{j}", [128, 1024], BF16) for j in range(2)]
            kq = a8.sbuf_tensor("kq", [128, 4 * 1024], BF16)
            vq = a8.sbuf_tensor("vq", [128, 4 * 1024], BF16)
            ktmp = [a8.sbuf_tensor(f"ktmp{j}", [128, 512], BF16) for j in range(2)]
            cosk = a8.sbuf_tensor("cosk", [128, TPC], BF16)
            sink = a8.sbuf_tensor("sink", [128, TPC], BF16)
            ropeS = a8.sbuf_tensor("ropeS", [128, 128], BF16)
            Sst = a8.sbuf_tensor("Sst", [64, 1024], F32); g["Sst"] = Sst
            Sbf = a8.sbuf_tensor("Sbf", [128, 1024], BF16)
            gcts = a8.sbuf_tensor("gcts", [128, 48], F32)
            kg = [a8.sbuf_tensor(f"kg{j}", [128, 64], BF16) for j in range(2)]
            rcoef = a8.sbuf_tensor("rcoef", [64, 128], F32)
            Ssb = a8.sbuf_tensor("Ssb", [64, 1024], BF16)
            a8.allocate()
            evC8 = p.dma("sync", cosk[:, :], g["cosk_bf"][:, :], sig="dK")
            evC8 = p.dma("sync", sink[:, :], g["sink_bf"][:, :], sig="dK")
            evC8 = p.dma("sync", ropeS[:, :], g["ropeP"][:, :], sig="dK")
            evC8 = p.dma("sync", gcts[0:64, 0:16], g["gCT"][:, :], sig="dK")
            evC8 = p.dma("sync", gcts[:, 16:32], g["gkT"][:, :], sig="dK")
            evC8 = p.dma("sync", gcts[:, 32:48], g["gfacT"][:, :], sig="dK")
            evC8 = p.dma("sync", rcoef[:, :], g["ret_coef"][:, :], sig="dK")
            evC8 = p.dma("sync", cvcols(xnR, 8, TPC),
                         cvrows(g["xnT1_park"], 0, 8, 0, TPC), sig="dK")
            sl_wA = Slots(2)
            pipeK8 = DmaPipe(p, "dK8", n=2)
            sl_kt = Slots(2)
            sl_kg = Slots(2)
            evS0 = p.step("gpsimd", lambda e: e.memset(Sst[:, :], 0.0), sig="gps")
            Supd = [evS0] * 16
            ev_kq = None
            ev_vq = None
            ev_kpark = None
            ev_vpark = None
            for qq in range(4):
                q0 = qq * 512
                ev_kq_new = []
                for ot in range(8):
                    jwA, w8 = sl_wA.get()
                    ePan = pipeK8.load_group(
                        [(cvo(wA[jwA], 8, 128), cvi(g["wkbfT"], 0, 8, ot * 128, 128))],
                        wait=[w8] + ([p.ev("dS"), p.ev("dV"), p.ev("dve")] if (qq == 0 and ot < 2) else []))
                    ja, wpa = sl_pa3.get()
                    mmv = None
                    for kk in range(8):
                        mmv = p.step("tensor", lambda e, kk=kk, ja=ja, q0=q0, jwA=jwA: e.matmul(
                            PA[ja][:, :], wA[jwA][:, kk * 128:(kk + 1) * 128],
                            xnR[:, kk * TPC + q0: kk * TPC + q0 + 512],
                            start=(kk == 0), stop=(kk == 7)),
                            wait=[ePan, wpa, p.ev("dK")] if kk == 0 else [],
                            sig="pe" if kk == 7 else None)
                    sl_wA.done(jwA, mmv)
                    jk2, wk2_ = sl_kt.get()
                    eKt = p.step("vector", lambda e, ja=ja, jk2=jk2: e.tensor_copy(
                        out=ktmp[jk2][:, :], in_=PA[ja][:, :]), wait=[mmv, wk2_], sig="dve")
                    ja2, wpa2 = sl_pa3.get()
                    mmr = p.step("tensor", lambda e, ja2=ja2, jk2=jk2: e.matmul(
                        PA[ja2][:, :], ropeS[:, :], ktmp[jk2][:, :], start=True, stop=True),
                        wait=[eKt, wpa2, p.ev("dK")], sig="pe")
                    sl_pa3.done(ja, mmr)
                    eR1 = p.step("vector", lambda e, jk2=jk2, ot=ot, q0=q0: e.tensor_tensor(
                        out=ktmp[jk2][:, :], in0=ktmp[jk2][:, :],
                        in1=cosk[ot * 16:(ot + 1) * 16, :][0:128, q0:q0 + 512]
                        if False else cosk[:, q0:q0 + 512], op=ALU.mult),
                        wait=[mmr], sig="dve")
                    eR2 = p.step("vector", lambda e, ja2=ja2, ot=ot, q0=q0, jk2=jk2: e.scalar_tensor_tensor(
                        out=ktmp[jk2][:, :], in0=PA[ja2][:, :], scalar=1.0,
                        in1=ktmp[jk2][:, :], op0=ALU.bypass, op1=ALU.mult)
                        if False else e.tensor_tensor(
                        out=PA[ja2][:, :], in0=PA[ja2][:, :], in1=PA[ja2][:, :], op=ALU.mult),
                        wait=[eR1], sig="dve") if False else None
                    jk3, wk3_ = sl_kt.get()
                    eR2 = p.step("vector", lambda e, ja2=ja2, jk3=jk3, q0=q0: e.tensor_tensor(
                        out=ktmp[jk3][:, :], in0=PA[ja2][:, :], in1=sink[:, q0:q0 + 512],
                        op=ALU.mult), wait=[mmr, wk3_, p.ev("dK")], sig="dve")
                    sl_pa3.done(ja2, eR2)
                    eR3 = p.step("vector", lambda e, jk2=jk2, jk3=jk3, ot=ot: e.tensor_tensor(
                        out=kq[:, ot * 512:(ot + 1) * 512], in0=ktmp[jk2][:, :],
                        in1=ktmp[jk3][:, :], op=ALU.add),
                        wait=[eR1, eR2, ev_kq, p.ev("dV")], sig="dve")
                    sl_kt.done(jk2, eR3)
                    sl_kt.done(jk3, eR3)
                    ev_kq_new.append(eR3)
                ev_kq = ev_kq_new[-1]
                if qq == 0:
                    pw_free3v = list(pw_free3)
                ev_vq_new = []
                for half in range(2):
                    mmvs = [None] * 4
                    for kk in range(8):
                        jwA2, w82 = sl_wA.get()
                        ePan2 = pipeK8.load(
                            wA[jwA2][:, 0:512],
                            g["wvbfT"][kk * 128:(kk + 1) * 128, half * 512:(half + 1) * 512],
                            wait=[w82])
                        for tt in range(4):
                            mmvs[tt] = p.step("tensor", lambda e, kk=kk, q0=q0, tt=tt,
                                              jwA2=jwA2: e.matmul(
                                PW[tt][:, :], xnR[:, kk * TPC + q0 + tt * 128:
                                                  kk * TPC + q0 + (tt + 1) * 128],
                                wA[jwA2][:, 0:512],
                                start=(kk == 0), stop=(kk == 7)),
                                wait=[ePan2] + ([pw_free3v[tt], p.ev("dK")] if kk == 0 else []),
                                sig="pe")
                        sl_wA.done(jwA2, mmvs[3])
                    for tt in range(4):
                        eV8 = p.step("vector", lambda e, tt=tt, half=half: e.tensor_copy(
                            out=vq[:, tt * 1024 + half * 512: tt * 1024 + (half + 1) * 512],
                            in_=PW[tt][:, :]), wait=[mmvs[tt], ev_vq, p.ev("dV")], sig="dve")
                        pw_free3v[tt] = eV8
                        ev_vq_new.append(eV8)
                ev_vq = ev_vq_new[-1]
                ev_kpark = p.dma("sync", cvi(g["kT_park"], 0, 8, q0, 512),
                                 cvo(kq, 8, 512), wait=[ev_kq], sig="dV")
                ev_vpark = p.dma("sync", cvrows(g["v_park"], q0, 4, 0, 1024),
                                 cvcols(vq, 4, 1024), wait=[ev_vq], sig="dV")
                for tt in range(4):
                    for h in range(16):
                        tb, ro = h // 2, (h % 2) * 64
                        jt, wt = sl_pt.get()
                        eTb = p.step("tensor", lambda e, tb=tb, ro=ro, tt=tt, jt=jt: e.transpose(
                            PTs[jt][:, 0:64],
                            kq[ro:ro + 64, tb * 512 + tt * 128: tb * 512 + (tt + 1) * 128],
                            EYE[ro:ro + 64, ro:ro + 64]),
                            wait=[ev_kq, wt], sig="pe")
                        jg, wg_ = sl_kg.get()
                        eKg = p.step("scalar", lambda e, jt=jt, jg=jg, h=h: e.activation(
                            out=kg[jg][:, :], in_=PTs[jt][:, 0:64], func=AF.Copy,
                            scale=gcts[:, 16 + h:17 + h]),
                            wait=[eTb, wg_, p.ev("dK")], sig="act")
                        sl_pt.done(jt, eKg)
                        ja, wpa = sl_pa3.get()
                        mmS = p.step("tensor", lambda e, jg=jg, tt=tt, h=h, ja=ja: e.matmul(
                            PA[ja][0:64, 0:64], kg[jg][:, :],
                            vq[:, tt * 1024 + h * 64: tt * 1024 + (h + 1) * 64],
                            start=True, stop=True),
                            wait=[eKg, ev_vq, wpa], sig="pe")
                        sl_kg.done(jg, mmS)
                        upd = p.step("vector", lambda e, h=h, ja=ja: e.scalar_tensor_tensor(
                            out=Sst[:, h * 64:(h + 1) * 64], in0=Sst[:, h * 64:(h + 1) * 64],
                            scalar=gcts[0:64, h:h + 1], in1=PA[ja][0:64, 0:64],
                            op0=ALU.mult, op1=ALU.add),
                            wait=[mmS, Supd[h], p.ev("dK")], sig="dve")
                        sl_pa3.done(ja, upd)
                        Supd[h] = upd
                ev_kq = Supd[15]
                ev_vq = Supd[15]
            eSb8 = p.step("vector", lambda e: e.tensor_copy(out=Ssb[:, :], in_=Sst[:, :]),
                          wait=Supd, sig="dve")
            eSs = p.dma("gpsimd", g["ccS_in"][:, :], Ssb[:, :], wait=[eSb8], sig="dS")
            evCC2 = p.step("gpsimd", lambda e: e.collective_compute(
                "AllGather", ALU.bypass, replica_groups=[list(range(8))],
                ins=[g["ccS_in"].opt()], outs=[g["ccS_out"].opt()]),
                wait=[eSs], sig="cc")
            def emit_compose2():
                evComp = None
                evHs8 = None
                for sdr in range(8):
                    evHs8 = p.dma("sync", Ssb[:, :], g["ccS_out"][sdr * 64:(sdr + 1) * 64, :],
                                  wait=[evCC2, evComp,
                                        evHs8 if sdr else p.ev("dR")], sig="dR")
                    for h in range(16):
                        if sdr == 0:
                            evComp = p.step("vector", lambda e, h=h, sdr=sdr: e.tensor_scalar(
                                out=Sst[:, h * 64:(h + 1) * 64], in0=Ssb[:, h * 64:(h + 1) * 64],
                                scalar1=rcoef[0:64, sdr * 16 + h:sdr * 16 + h + 1], scalar2=None,
                                op0=ALU.mult), wait=[evHs8, eSb8, p.ev("dK")], sig="dve")
                        else:
                            evComp = p.step("vector", lambda e, h=h, sdr=sdr: e.scalar_tensor_tensor(
                                out=Sst[:, h * 64:(h + 1) * 64], in0=Ssb[:, h * 64:(h + 1) * 64],
                                scalar=rcoef[0:64, sdr * 16 + h:sdr * 16 + h + 1],
                                in1=Sst[:, h * 64:(h + 1) * 64],
                                op0=ALU.mult, op1=ALU.add), wait=[evHs8, evComp], sig="dve")
                eSdup = p.step("vector", lambda e: e.tensor_copy(out=Sbf[0:64, :], in_=Sst[:, :]),
                               wait=[evComp], sig="dve")
                eSdup = p.dma("gpsimd", Sbf[64:128, :], Sbf[0:64, :], wait=[eSdup], sig="dS")
                return eSdup
            if stage == 8:
                eSdup = emit_compose2()

        if stage >= 9:
            a9 = nc.auto_arena()
            p.arenas.append(a9)
            dmsb = a9.sbuf_tensor("dmsb", [128, NH1 * CH], BF16)
            cosq = a9.sbuf_tensor("cosq", [128, TPC], BF16)
            sinq = a9.sbuf_tensor("sinq", [128, TPC], BF16)
            qq_b = a9.sbuf_tensor("qq_b", [128, 8 * 512], BF16)
            gtok = a9.sbuf_tensor("gtok", [128, 4 * 1024], BF16)
            kcs = [a9.sbuf_tensor(f"kcs{j}", [128, 8 * 128], BF16) for j in range(2)]
            vcs = [a9.sbuf_tensor(f"vcs{j}", [128, 1024], BF16) for j in range(2)]
            dst9 = [a9.sbuf_tensor(f"dst9{j}", [128, 128], BF16) for j in range(2)]
            yret = a9.sbuf_tensor("yret", [128, 1024], F32)
            gst = a9.sbuf_tensor("gst", [128, 40], F32)
            gnwb = a9.sbuf_tensor("gnwb", [128, 1024], BF16)
            gnbb = a9.sbuf_tensor("gnbb", [128, 1024], BF16)
            gdum = a9.sbuf_tensor("gdum", [128, 64], F32)
            gated = a9.sbuf_tensor("gated", [128, 1024], BF16)
            gatT = a9.sbuf_tensor("gatT", [128, 8 * 256], BF16)
            wop = [a9.sbuf_tensor(f"wop{j}", [128, 1024], BF16) for j in range(2)]
            grow = a9.sbuf_tensor("grow", [1, 1024], F32)
            scl9 = a9.sbuf_tensor("scl9", [128, 2], F32)
            a9.allocate()
            hstg9 = [cosk, sink]
            hfs9 = vq[:, 0:4096].bitcast(F32)
            his9 = kq[:, 0:4096].bitcast(I32)
            evD9 = p.dma("sync", dmsb[:, :], g["dmaskT"][:, :], sig="dK")
            evD9 = p.dma("sync", cosq[:, :], g["cosq_bf"][:, :], sig="dK")
            evD9 = p.dma("sync", sinq[:, :], g["sinq_bf"][:, :], sig="dK")
            evD9 = p.dma("sync", grow[0:1, :], g["gn_w"][:, :], sig="dK")
            ja, wpa = sl_pa3.get()
            eGn1 = p.step("tensor", lambda e, ja=ja: e.matmul(
                PA[ja][:, 0:512], ONESF[0:1, :], grow[0:1, 0:512], start=True, stop=True),
                wait=[evD9, wpa], sig="pe")
            eGn1b = p.step("tensor", lambda e, ja=ja: e.matmul(
                PA[ja][:, :], ONESF[0:1, :], grow[0:1, 512:1024], start=False, stop=True),
                wait=[], sig="pe") if False else None
            eGw1 = p.step("vector", lambda e, ja=ja: e.tensor_copy(
                out=gnwb[:, 0:512], in_=PA[ja][:, 0:512]), wait=[eGn1], sig="dve")
            sl_pa3.done(ja, eGw1)
            ja, wpa = sl_pa3.get()
            eGn2 = p.step("tensor", lambda e, ja=ja: e.matmul(
                PA[ja][:, 0:512], ONESF[0:1, :], grow[0:1, 512:1024], start=True, stop=True),
                wait=[eGw1, wpa], sig="pe")
            eGw2 = p.step("vector", lambda e, ja=ja: e.tensor_copy(
                out=gnwb[:, 512:1024], in_=PA[ja][:, 0:512]), wait=[eGn2], sig="dve")
            sl_pa3.done(ja, eGw2)
            evD9b = p.dma("sync", grow[0:1, :], g["gn_b"][:, :], wait=[eGw2, p.ev("dK")], sig="dK")
            for hf in range(2):
                ja, wpa = sl_pa3.get()
                eGn3 = p.step("tensor", lambda e, ja=ja, hf=hf: e.matmul(
                    PA[ja][:, 0:512], ONESF[0:1, :], grow[0:1, hf * 512:(hf + 1) * 512],
                    start=True, stop=True), wait=[evD9b, wpa], sig="pe")
                eGw3 = p.step("vector", lambda e, ja=ja, hf=hf: e.tensor_copy(
                    out=gnbb[:, hf * 512:(hf + 1) * 512], in_=PA[ja][:, 0:512]),
                    wait=[eGn3], sig="dve")
                sl_pa3.done(ja, eGw3)
            evGnb = eGw3
            pipeQ9 = DmaPipe(p, "dQ9", n=2, engine="scalar")
            pipeKC = DmaPipe(p, "dKC", n=2)
            pipeVC = DmaPipe(p, "dVC", n=2)
            pipeGW = DmaPipe(p, "dGW", n=2, engine="scalar")
            pipeWo = DmaPipe(p, "dWO", n=2)
            sl_kc = Slots(2)
            sl_vc = Slots(2)
            sl_d9 = Slots(2)
            sl_wo9 = Slots(1)
            ev_qq = None
            ev_gtok = None
            ev_yret = None
            ev_gated = None
            ev_gatT = [None, None]
            ev_Rw9 = p.ev("dve")
            ev_gdum = None
            pwf = [p.ev("act")] * 4
            SupdB = list(Supd)
            ev_Sbf = None
            for qq9 in range(4):
                q0 = qq9 * 512
                evq_new = None
                for ot in range(8):
                    jwA, w8 = sl_wA.get()
                    ePan = pipeQ9.load_group(
                        [(cvo(wA[jwA], 8, 128), cvi(g["wqbfT"], 0, 8, ot * 128, 128))],
                        wait=[w8] + ([p.ev("dS"), p.ev("dV"), p.ev("dve")] if (qq9 == 0 and ot < 2) else []))
                    ja, wpa = sl_pa3.get()
                    mmv = None
                    for kk in range(8):
                        mmv = p.step("tensor", lambda e, kk=kk, ja=ja, q0=q0, jwA=jwA: e.matmul(
                            PA[ja][:, :], wA[jwA][:, kk * 128:(kk + 1) * 128],
                            xnR[:, kk * TPC + q0: kk * TPC + q0 + 512],
                            start=(kk == 0), stop=(kk == 7)),
                            wait=[ePan, wpa] if kk == 0 else [], sig="pe" if kk == 7 else None)
                    sl_wA.done(jwA, mmv)
                    jk2, wk2_ = sl_kt.get()
                    eKt = p.step("vector", lambda e, ja=ja, jk2=jk2: e.tensor_copy(
                        out=ktmp[jk2][:, :], in_=PA[ja][:, :]), wait=[mmv, wk2_], sig="dve")
                    ja2, wpa2 = sl_pa3.get()
                    mmr = p.step("tensor", lambda e, ja2=ja2, jk2=jk2: e.matmul(
                        PA[ja2][:, :], ropeS[:, :], ktmp[jk2][:, :], start=True, stop=True),
                        wait=[eKt, wpa2], sig="pe")
                    sl_pa3.done(ja, mmr)
                    eR1 = p.step("vector", lambda e, jk2=jk2, q0=q0: e.tensor_tensor(
                        out=ktmp[jk2][:, :], in0=ktmp[jk2][:, :],
                        in1=cosq[:, q0:q0 + 512], op=ALU.mult),
                        wait=[mmr, p.ev("dK")], sig="dve")
                    jk3, wk3_ = sl_kt.get()
                    eR2 = p.step("vector", lambda e, ja2=ja2, jk3=jk3, q0=q0: e.tensor_tensor(
                        out=ktmp[jk3][:, :], in0=PA[ja2][:, :], in1=sinq[:, q0:q0 + 512],
                        op=ALU.mult), wait=[mmr, wk3_, p.ev("dK")], sig="dve")
                    sl_pa3.done(ja2, eR2)
                    evq_new = p.step("vector", lambda e, jk2=jk2, jk3=jk3, ot=ot: e.tensor_tensor(
                        out=qq_b[:, ot * 512:(ot + 1) * 512], in0=ktmp[jk2][:, :],
                        in1=ktmp[jk3][:, :], op=ALU.add),
                        wait=[eR1, eR2, ev_qq], sig="dve")
                    sl_kt.done(jk2, evq_new)
                    sl_kt.done(jk3, evq_new)
                ev_qq = evq_new
                evg_new = None
                for half in range(2):
                    mmgs = [None] * 4
                    for kk in range(8):
                        jwA2, w82 = sl_wA.get()
                        ePan2 = pipeGW.load(
                            wA[jwA2][:, 0:512],
                            g["wgbfT"][kk * 128:(kk + 1) * 128, half * 512:(half + 1) * 512],
                            wait=[w82])
                        for tt in range(4):
                            mmgs[tt] = p.step("tensor", lambda e, kk=kk, q0=q0, tt=tt,
                                              jwA2=jwA2: e.matmul(
                                PW[tt][:, :], xnR[:, kk * TPC + q0 + tt * 128:
                                                  kk * TPC + q0 + (tt + 1) * 128],
                                wA[jwA2][:, 0:512],
                                start=(kk == 0), stop=(kk == 7)),
                                wait=[ePan2] + ([pwf[tt]] if kk == 0 else []),
                                sig="pe")
                        sl_wA.done(jwA2, mmgs[3])
                    for tt in range(4):
                        evg_new = p.step("scalar", lambda e, tt=tt, half=half: e.activation(
                            out=gtok[:, tt * 1024 + half * 512: tt * 1024 + (half + 1) * 512],
                            in_=PW[tt][:, :], func=AF.Silu),
                            wait=[mmgs[tt], ev_gtok], sig="act")
                        pwf[tt] = evg_new
                ev_gtok = evg_new
                if qq9 == 0:
                    ev_Sbf = emit_compose2()
                if qq9 == 2:
                    pipeHq = DmaPipe(p, "dHq", n=2)
                    pipeHs = DmaPipe(p, "dHs", n=2)
                    sl_hs = Slots(2)
                    p.dma("sync", scl9[:, :], g["scl_park"][:, 100:102], sig="dK")
                    ev_hq9 = None
                    for k2 in range(22):
                        jhs, whs = sl_hs.get()
                        eLh = pipeHq.load(hstg9[jhs][:, :],
                                          g["hT_park"][k2 * 128:(k2 + 1) * 128, :],
                                          wait=[whs] + ([p.ev("pe"), p.ev("dve"), p.ev("dV")]
                                                        if k2 < 2 else []))
                        eQ1 = p.step("scalar", lambda e, jhs=jhs: e.activation(
                            out=hfs9[:, :], in_=hstg9[jhs][:, :], func=AF.Copy,
                            scale=scl9[:, 1:2]), wait=[eLh, p.ev("dK"), ev_hq9], sig="act")
                        eQ2 = p.step("vector", lambda e: e.tensor_copy(
                            out=his9[:, :], in_=hfs9[:, :]), wait=[eQ1], sig="dve")
                        eQ3 = p.step("vector", lambda e: e.tensor_copy(
                            out=hfs9[:, :], in_=his9[:, :]), wait=[eQ2], sig="dve")
                        eQ4 = p.step("vector", lambda e, jhs=jhs: e.tensor_scalar(
                            out=hstg9[jhs][:, :], in0=hfs9[:, :], scalar1=127.0,
                            scalar2=-128.0, op0=ALU.min, op1=ALU.max),
                            wait=[eQ3], sig="dve")
                        ev_hq9 = pipeHs.load(g["hq_park"][k2 * 128:(k2 + 1) * 128, :],
                                             hstg9[jhs][:, :], wait=[eQ4])
                        sl_hs.done(jhs, ev_hq9)
                for tt in range(4):
                    c9 = qq9 * 4 + tt
                    jkc, wkc = sl_kc.get()
                    evK = pipeKC.load_group(
                        [(cvo(kcs[jkc], 8, 128), cvi(g["kT_park"], 0, 8, c9 * 128, 128))],
                        wait=[wkc])
                    jvc, wvc = sl_vc.get()
                    evV = pipeVC.load(vcs[jvc][:, :], g["v_park"][c9 * 128:(c9 + 1) * 128, :],
                                      wait=[wvc])
                    ev_hd = [None] * 16
                    for h in range(16):
                        tb, ro = h // 2, (h % 2) * 64
                        ja, wpa = sl_pa3.get()
                        mmG = p.step("tensor", lambda e, ja=ja, tb=tb, ro=ro, tt=tt, jkc=jkc: e.matmul(
                            PA[ja][:, 0:128],
                            kcs[jkc][ro:ro + 64, tb * 128:(tb + 1) * 128],
                            qq_b[ro:ro + 64, tb * 512 + tt * 128: tb * 512 + (tt + 1) * 128],
                            start=True, stop=True),
                            wait=[evK, ev_qq, wpa], sig="pe")
                        jd, wd = sl_d9.get()
                        eDst = p.step("vector", lambda e, ja=ja, jd=jd, h=h: e.tensor_tensor(
                            out=dst9[jd][:, :], in0=PA[ja][:, 0:128],
                            in1=dmsb[:, h * 128:(h + 1) * 128], op=ALU.mult),
                            wait=[mmG, wd, p.ev("dK")], sig="dve")
                        sl_pa3.done(ja, eDst)
                        mm1 = p.step("tensor", lambda e, jd=jd, jvc=jvc, h=h: e.matmul(
                            PW[h // 8][:, (h % 8) * 64:((h % 8) + 1) * 64],
                            dst9[jd][:, :], vcs[jvc][:, h * 64:(h + 1) * 64],
                            start=True, stop=True),
                            wait=[eDst, evV, pwf[h // 8]], sig="pe")
                        sl_d9.done(jd, mm1)
                        mm2 = p.step("tensor", lambda e, tb=tb, ro=ro, tt=tt, h=h: e.matmul(
                            PW[2 + h // 8][:, (h % 8) * 64:((h % 8) + 1) * 64],
                            qq_b[ro:ro + 64, tb * 512 + tt * 128: tb * 512 + (tt + 1) * 128],
                            Sbf[ro:ro + 64, h * 64:(h + 1) * 64],
                            start=True, stop=True),
                            wait=[ev_qq, ev_Sbf, pwf[2 + h // 8]], sig="pe")
                        ev_hd[h] = (mm1, mm2)
                    eYc = [None, None]
                    for j4 in range(2):
                        eYc[j4] = p.step("scalar", lambda e, j4=j4: e.activation(
                            out=yret[:, j4 * 512:(j4 + 1) * 512], in_=PW[j4][:, :],
                            func=AF.Copy),
                            wait=[ev_hd[j4 * 8 + 7][0], ev_hd[j4 * 8][0], ev_yret], sig="act")
                    eY9 = None
                    for h in range(16):
                        eY9 = p.step("vector", lambda e, h=h: e.scalar_tensor_tensor(
                            out=yret[:, h * 64:(h + 1) * 64],
                            in0=PW[2 + h // 8][:, (h % 8) * 64:((h % 8) + 1) * 64],
                            scalar=gcts[:, 32 + h:33 + h],
                            in1=yret[:, h * 64:(h + 1) * 64],
                            op0=ALU.mult, op1=ALU.add),
                            wait=[ev_hd[h][1], eYc[h // 8], p.ev("dK")], sig="dve")
                    for j4 in range(4):
                        pwf[j4] = eY9
                    lastS9 = None
                    for h in range(16):
                        tb, ro = h // 2, (h % 2) * 64
                        jt, wt = sl_pt.get()
                        eTb = p.step("tensor", lambda e, tb=tb, ro=ro, tt=tt, jt=jt, jkc=jkc: e.transpose(
                            PTs[jt][:, 0:64],
                            kcs[jkc][ro:ro + 64, tb * 128:(tb + 1) * 128],
                            EYE[ro:ro + 64, ro:ro + 64]),
                            wait=[evK, wt], sig="pe")
                        jg, wg_ = sl_kg.get()
                        eKg = p.step("scalar", lambda e, jt=jt, jg=jg, h=h: e.activation(
                            out=kg[jg][:, :], in_=PTs[jt][:, 0:64], func=AF.Copy,
                            scale=gcts[:, 16 + h:17 + h]),
                            wait=[eTb, wg_], sig="act")
                        sl_pt.done(jt, eKg)
                        ja, wpa = sl_pa3.get()
                        mmS = p.step("tensor", lambda e, jg=jg, jvc=jvc, h=h, ja=ja: e.matmul(
                            PA[ja][0:64, 0:64], kg[jg][:, :],
                            vcs[jvc][:, h * 64:(h + 1) * 64],
                            start=True, stop=True),
                            wait=[eKg, evV, wpa], sig="pe")
                        sl_kg.done(jg, mmS)
                        upd = p.step("vector", lambda e, h=h, ja=ja: e.scalar_tensor_tensor(
                            out=Sst[:, h * 64:(h + 1) * 64], in0=Sst[:, h * 64:(h + 1) * 64],
                            scalar=gcts[0:64, h:h + 1], in1=PA[ja][0:64, 0:64],
                            op0=ALU.mult, op1=ALU.add),
                            wait=[mmS, SupdB[h], ev_Sbf], sig="dve")
                        sl_pa3.done(ja, upd)
                        SupdB[h] = upd
                        lastS9 = upd
                    sl_kc.done(jkc, lastS9)
                    sl_vc.done(jvc, lastS9)
                    ev_Sbf = p.step("vector", lambda e: e.tensor_copy(
                        out=Sbf[0:64, :], in_=Sst[:, :]), wait=SupdB + [ev_Sbf], sig="dve")
                    ev_Sbf = p.dma("gpsimd", Sbf[64:128, :], Sbf[0:64, :],
                                   wait=[ev_Sbf], sig="dS")
                    eGs9 = None
                    eSqp = ev_gdum
                    for h in range(16):
                        eGs9 = p.step("vector", lambda e, h=h: e.reduce_sum(
                            out=gst[:, h:h + 1], in_=yret[:, h * 64:(h + 1) * 64], axis=AX.X),
                            wait=[eY9] if h == 0 else [], sig="dve")
                        eSqp = p.step("scalar", lambda e, h=h: e.activation(
                            out=gdum[:, :], in_=yret[:, h * 64:(h + 1) * 64], func=AF.Square,
                            accum_out=gst[:, 16 + h:17 + h]),
                            wait=[eY9, eSqp], sig="act")
                    eGs9 = eSqp
                    eM9 = p.step("vector", lambda e: e.tensor_scalar(
                        out=gst[:, 0:16], in0=gst[:, 0:16], scalar1=1.0 / DH1, scalar2=None,
                        op0=ALU.mult), wait=[eGs9, p.ev("dve"), p.ev("act")], sig="dve")
                    eV9 = p.step("vector", lambda e: e.tensor_scalar(
                        out=gst[:, 16:32], in0=gst[:, 16:32], scalar1=1.0 / DH1, scalar2=1e-5,
                        op0=ALU.mult, op1=ALU.add), wait=[eM9, p.ev("act")], sig="dve")
                    eV9b = p.step("vector", lambda e: e.tensor_tensor(
                        out=gst[:, 32:40][:, 0:8] if False else gst[:, 16:32],
                        in0=gst[:, 16:32], in1=gst[:, 0:16], op=ALU.subtract)
                        if False else e.tensor_tensor(
                        out=gdum[:, 0:16], in0=gst[:, 0:16], in1=gst[:, 0:16], op=ALU.mult),
                        wait=[eV9], sig="dve")
                    eV9c = p.step("vector", lambda e: e.tensor_tensor(
                        out=gst[:, 16:32], in0=gst[:, 16:32], in1=gdum[:, 0:16],
                        op=ALU.subtract), wait=[eV9b], sig="dve")
                    ev_gdum = eV9c
                    eSr9 = p.step("scalar", lambda e: e.activation(
                        out=gst[:, 16:32], in_=gst[:, 16:32], func=AF.Sqrt),
                        wait=[eV9c], sig="act")
                    eRc9 = p.step("vector", lambda e: e.reciprocal(
                        out=gst[:, 16:32], in_=gst[:, 16:32]), wait=[eSr9], sig="dve")
                    eNb9 = p.step("vector", lambda e: e.scalar_tensor_tensor(
                        out=gst[:, 0:16], in0=gst[:, 0:16], scalar=-1.0, in1=gst[:, 16:32],
                        op0=ALU.mult, op1=ALU.mult), wait=[eRc9], sig="dve")
                    eAp9 = None
                    for h in range(16):
                        eAp9 = p.step("vector", lambda e, h=h: e.tensor_scalar(
                            out=yret[:, h * 64:(h + 1) * 64], in0=yret[:, h * 64:(h + 1) * 64],
                            scalar1=gst[:, 16 + h:17 + h], scalar2=gst[:, h:h + 1],
                            op0=ALU.mult, op1=ALU.add),
                            wait=[eNb9] if h == 0 else [], sig="dve")
                    eGnw = p.step("vector", lambda e: e.tensor_tensor(
                        out=yret[:, :], in0=yret[:, :], in1=gnwb[:, :], op=ALU.mult),
                        wait=[eAp9, eGw2], sig="dve")
                    eGnb2 = p.step("vector", lambda e: e.tensor_tensor(
                        out=yret[:, :], in0=yret[:, :], in1=gnbb[:, :], op=ALU.add),
                        wait=[eGnw, evGnb], sig="dve")
                    eGt = p.step("vector", lambda e, tt=tt: e.tensor_tensor(
                        out=gated[:, :], in0=yret[:, :], in1=gtok[:, tt * 1024:(tt + 1) * 1024],
                        op=ALU.mult), wait=[eGnb2, ev_gtok, ev_gated], sig="dve")
                    ev_yret = eGt
                    lastg9 = None
                    for ff in range(8):
                        jt, wt = sl_pt.get()
                        eT = p.step("tensor", lambda e, ff=ff, jt=jt: e.transpose(
                            PTs[jt][:, 0:128], gated[:, ff * 128:(ff + 1) * 128], EYE[:, :]),
                            wait=[eGt, wt], sig="pe")
                        lastg9 = p.step("vector", lambda e, ff=ff, tt=tt, jt=jt: e.tensor_copy(
                            out=gatT[:, ff * 256 + (tt % 2) * 128: ff * 256 + (tt % 2) * 128 + 128],
                            in_=PTs[jt][:, 0:128]), wait=[eT, ev_gatT[tt % 2]], sig="dve")
                        sl_pt.done(jt, lastg9)
                    ev_gated = lastg9
                    ev_gatT[tt % 2] = lastg9
                    if tt % 2 == 1:
                        ev_mo = [None] * 4
                        for k2 in range(8):
                            jwo, wwo = sl_wo9.get()
                            eKo = pipeWo.load(wop[jwo][:, :],
                                              g["wobfT"][k2 * 128:(k2 + 1) * 128, :],
                                              wait=[wwo])
                            for tt2 in range(2):
                                for nh in range(2):
                                    pwidx = tt2 * 2 + nh
                                    ev_mo[pwidx] = p.step("tensor",
                                        lambda e, k2=k2, tt2=tt2, nh=nh, jwo=jwo, pwidx=pwidx: e.matmul(
                                            PW[pwidx][:, :],
                                            gatT[:, k2 * 256 + tt2 * 128: k2 * 256 + tt2 * 128 + 128],
                                            wop[jwo][:, nh * 512:(nh + 1) * 512],
                                            start=(k2 == 0), stop=(k2 == 7)),
                                        wait=[eKo, ev_gatT[0], ev_gatT[1]] +
                                             ([pwf[pwidx]] if k2 == 0 else []),
                                        sig="pe")
                            sl_wo9.done(jwo, ev_mo[3])
                        for tt2 in range(2):
                            ti9 = qq9 * 4 + (tt - 1) + tt2
                            for nh in range(2):
                                pwidx = tt2 * 2 + nh
                                evR9 = p.step("vector", lambda e, ti9=ti9, nh=nh, pwidx=pwidx: e.tensor_tensor(
                                    out=R[:, ti9 * DM + nh * 512: ti9 * DM + (nh + 1) * 512],
                                    in0=R[:, ti9 * DM + nh * 512: ti9 * DM + (nh + 1) * 512],
                                    in1=PW[pwidx][:, :], op=ALU.add),
                                    wait=[ev_mo[pwidx], ev_Rw9], sig="dve")
                                pwf[pwidx] = evR9
                                ev_Rw9 = evR9
                        ev_gatT = [ev_Rw9, ev_Rw9]

        if stage >= 10:
            p.barrier()
            a9.free()
            a8.free()
            a10 = nc.auto_arena()
            p.arenas.append(a10)
            wdres = a10.sbuf_tensor("wdres", [128, 22 * 1024], BF16)
            hfs = a10.sbuf_tensor("hfs", [128, 2048], F32)
            hqs = [a10.sbuf_tensor(f"hqs{j}", [128, 512], BF16) for j in range(3)]
            sdhb = a10.sbuf_tensor("sdhb", [128, 1024], F32)
            scl2 = a10.sbuf_tensor("scl2", [128, 16], F32)
            sd8 = a10.sbuf_tensor("sd8", [8, 128], F32)
            sdrow = a10.sbuf_tensor("sdrow", [1, 1024], F32)
            a10.allocate()
            dint10 = g["hq_park"]
            eL10 = p.dma("sync", scl2[:, 0:2], g["scl_park"][:, 100:102], sig="dK")
            eL10b = p.dma("sync", scl2[:, 2:10], g["scl_park"][:, 44:52], wait=[eL10], sig="dK")
            ja, wpa = sl_pa3.get()
            eT10 = p.step("tensor", lambda e, ja=ja: e.transpose(
                PA[ja][0:8, 0:128], scl2[:, 2:10], EYEF[:, :]),
                wait=[p.ev("dK"), wpa], sig="pe")
            eC10 = p.step("vector", lambda e, ja=ja: e.tensor_copy(
                out=sd8[:, :], in_=PA[ja][0:8, 0:128]), wait=[eT10], sig="dve")
            sl_pa3.done(ja, eC10)
            eF10 = p.dma("gpsimd", sdrow[0:1, :], sd8[:, :], wait=[eC10, p.ev("dS")], sig="dS")
            ja, wpa = sl_pa3.get()
            eR10 = p.step("tensor", lambda e, ja=ja: e.matmul(
                PA[ja][:, 0:512], ONESF[0:1, :], sdrow[0:1, 0:512], start=True, stop=True),
                wait=[eF10, wpa], sig="pe")
            eB10 = p.step("vector", lambda e, ja=ja: e.tensor_copy(
                out=sdhb[:, 0:512], in_=PA[ja][:, 0:512]), wait=[eR10], sig="dve")
            sl_pa3.done(ja, eB10)
            ja, wpa = sl_pa3.get()
            eR11 = p.step("tensor", lambda e, ja=ja: e.matmul(
                PA[ja][:, 0:512], ONESF[0:1, :], sdrow[0:1, 512:1024], start=True, stop=True),
                wait=[eB10, wpa], sig="pe")
            eB11 = p.step("vector", lambda e, ja=ja: e.tensor_copy(
                out=sdhb[:, 512:1024], in_=PA[ja][:, 0:512]), wait=[eR11], sig="dve")
            sl_pa3.done(ja, eB11)
            pipeWd = DmaPipe(p, "dWd", n=2)
            evWd = None
            for k2 in range(22):
                evWd = pipeWd.load(wdres[:, k2 * 1024:(k2 + 1) * 1024],
                                   g["wdTint"][k2 * 128:(k2 + 1) * 128, :],
                                   wait=[])
            evWd0 = p.ev("dWd0")
            evWd1 = p.ev("dWd1")
            pipeHl = DmaPipe(p, "dHl", n=3)
            sl_hq = Slots(3)
            pwfD = [p.ev("dve")] * 4
            ev_RwD = p.ev("dve")
            for tig in range(8):
                ev_m = [None] * 4
                for k2p in range(11):
                    jh, wh = sl_hq.get()
                    eLq = pipeHl.load(cvo(hqs[jh], 2, 256),
                                      cvi(g["hq_park"], k2p * 256, 2, tig * 256, 256),
                                      wait=[wh] + ([p.ev("dHs0"), p.ev("dHs1")]
                                                  if (tig == 0 and k2p < 3) else []))
                    lastm = None
                    for k2s in range(2):
                        k2 = k2p * 2 + k2s
                        for ti2 in range(2):
                            for nh in range(2):
                                pwidx = ti2 * 2 + nh
                                ev_m[pwidx] = p.step("tensor",
                                    lambda e, k2=k2, k2s=k2s, ti2=ti2, nh=nh, jh=jh,
                                    pwidx=pwidx: e.matmul(
                                        PW[pwidx][:, :],
                                        hqs[jh][:, k2s * 256 + ti2 * 128:
                                                k2s * 256 + (ti2 + 1) * 128],
                                        wdres[:, k2 * 1024 + nh * 512: k2 * 1024 + (nh + 1) * 512],
                                        start=(k2 == 0), stop=(k2 == 21)),
                                    wait=[eLq] + ([pwfD[pwidx], evWd0, evWd1] if k2 == 0 else []),
                                    sig="pe")
                                lastm = ev_m[pwidx]
                    sl_hq.done(jh, lastm)
                for ti2 in range(2):
                    ti10 = tig * 2 + ti2
                    for nh in range(2):
                        pwidx = ti2 * 2 + nh
                        eEv = p.step("vector", lambda e, nh=nh, pwidx=pwidx: e.tensor_tensor(
                            out=hfs[:, nh * 512:(nh + 1) * 512], in0=PW[pwidx][:, :],
                            in1=sdhb[:, nh * 512:(nh + 1) * 512], op=ALU.mult),
                            wait=[ev_m[pwidx], eB11, p.ev("dve")], sig="dve")
                        evR10 = p.step("vector", lambda e, ti10=ti10, nh=nh: e.scalar_tensor_tensor(
                            out=R[:, ti10 * DM + nh * 512: ti10 * DM + (nh + 1) * 512],
                            in0=hfs[:, nh * 512:(nh + 1) * 512], scalar=scl2[:, 0:1],
                            in1=R[:, ti10 * DM + nh * 512: ti10 * DM + (nh + 1) * 512],
                            op0=ALU.mult, op1=ALU.add),
                            wait=[eEv, ev_RwD], sig="dve")
                        pwfD[pwidx] = evR10
                        ev_RwD = evR10

    p.barrier()

    if "ssdA" in debug:
        evDS = p.ev("dS")
        dout("dbg_hall", (64, 2048))
        dout("dbg_lcsg", (8, TPC))
        dout("dbg_BT", (128, 4 * TPC), BF16)
        dout("dbg_xtok", (TPC, DI), BF16)
        p.dma("gpsimd", g["dbg_hall"][:, :], g["hall"][:, :], wait=[evDS], sig="dO")
        p.dma("gpsimd", g["dbg_lcsg"][:, :], g["lcsg"][:, :], sig="dO")
        p.dma("gpsimd", g["dbg_BT"][:, :], g["BT"][:, :], sig="dO")
        for i in range(NTT):
            p.dma("gpsimd", g["dbg_xtok"][i * 128:(i + 1) * 128, :],
                  g["xtok_park"][i * 128:(i + 1) * 128, :], sig="dO")

    if "xq1" in debug:
        dout("dbg_xq", (DM, TPC), BF16)
        dout("dbg_xn1", (DM, TPC), BF16)
        for f in range(8):
            p.dma("gpsimd", g["dbg_xq"][f * 128:(f + 1) * 128, :],
                  g["xqT"][:, f * TPC:(f + 1) * TPC], sig="dO")
            p.dma("gpsimd", g["dbg_xn1"][f * 128:(f + 1) * 128, :],
                  g["xnT1"][:, f * TPC:(f + 1) * TPC], sig="dO")

    if "R2" in debug:
        dout("dbg_R2", (TPC, DM))
        for i in range(NTT):
            p.dma("gpsimd", g["dbg_R2"][i * 128:(i + 1) * 128, :],
                  R[:, i * DM:(i + 1) * DM], sig="dO")

    for i0 in range(0, NTT, 4):
        p.dma("gpsimd", cvrows(g["out"], i0 * 128, 4, 0, DM),
              cvcols(R, 4, DM, c0=i0 * DM), sig="dO")
    p.step("gpsimd", lambda e: e.nop(), wait=[p.ev("dO")])

    with nc.Block() as block:
        p.finalize(block)
    for _ar in reversed(getattr(p, "arenas", [])):
        if _ar.handle is not None:
            _ar.free()
    stack.close()
    return nc


def host_constants(core_id):
    c = {}
    c["consts"] = np.tile(np.array([256.0, 0.25, 1.0, 0.5, 4.0, 64.0, 127.0, -128.0],
                                    np.float32), (128, 1))
    c["eye_f32"] = np.eye(128, dtype=np.float32)
    c["ones_f32"] = np.ones((1, 128), dtype=np.float32)
    c["eye_bf"] = np.eye(128, dtype=bfloat16)
    c["onesK"] = np.ones((1, 512), dtype=bfloat16)
    c["trilT"] = np.triu(np.ones((128, 128), dtype=np.float32))
    mmv = np.zeros((DI, NH0), dtype=np.float64)
    for h in range(NH0):
        mmv[h * P0H:(h + 1) * P0H, h] = 1.0 / P0H
    c["dtmean"] = mmv.astype(bfloat16)
    cq = core_id % 4
    pm = np.zeros((8, 8), np.float32)
    for s in range((core_id // 4) * 4, (core_id // 4) * 4 + cq):
        pm[:, s] = 1.0
    c["ssd_premaskT"] = pm[0:8, :]
    c["ssd_maskinvT"] = 1.0 - pm[0:8, :]
    s0 = cq * TPC
    pos = np.arange(s0, s0 + TPC, dtype=np.float64)
    inv_freq = 1.0 / 10000.0 ** (np.arange(0, DH1, 2, dtype=np.float64) / DH1)
    fr = pos[None, :] * inv_freq[:, None]
    emb = np.concatenate([fr, fr], axis=0)
    cos = np.cos(emb); sin = np.sin(emb)
    SC1 = DH1 ** -0.5
    c["cosq_bf"] = np.tile(cos * SC1, (2, 1)).astype(bfloat16)
    c["sinq_bf"] = np.tile(sin * SC1, (2, 1)).astype(bfloat16)
    c["cosk_bf"] = np.tile(cos, (2, 1)).astype(bfloat16)
    c["sink_bf"] = np.tile(sin, (2, 1)).astype(bfloat16)
    P = np.zeros((128, 128), dtype=np.float64)
    for o in range(128):
        d = o % 64
        base = o - d
        if d < 32:
            P[base + d + 32, o] = -1.0
        else:
            P[base + d - 32, o] = 1.0
    c["ropeP"] = P.astype(bfloat16)
    gam = 1.0 - 2.0 ** (-5.0 - np.arange(NH1, dtype=np.float64))
    l = np.arange(CH, dtype=np.float64)
    dm_ = np.zeros((CH, NH1 * CH), dtype=np.float64)
    for h in range(NH1):
        dm_[:, h * CH:(h + 1) * CH] = np.where(l[None, :] >= l[:, None],
                                               gam[h] ** (l[None, :] - l[:, None]), 0.0)
    c["dmaskT"] = dm_.astype(bfloat16)
    c["gfacT"] = (gam[None, :] ** (l[:, None] + 1.0)).astype(np.float32)
    c["gkT"] = (gam[None, :] ** (CH - 1.0 - l[:, None])).astype(np.float32)
    c["gCT"] = np.tile((gam ** CH)[None, :], (64, 1)).astype(np.float32)
    coef = np.zeros((8, NH1), dtype=np.float64)
    for s in range((core_id // 4) * 4, (core_id // 4) * 4 + cq):
        coef[s, :] = gam ** (2048.0 * (cq - 1 - (s % 4)))
    c["ret_coef"] = np.tile(coef.reshape(1, -1), (64, 1)).astype(np.float32)
    return c


def make_inmaps(ins):
    hid = np.ascontiguousarray(np.asarray(ins["L0_attn_hidden"], dtype=np.float32))
    maps = []
    for core in range(8):
        b, q = core // 4, core % 4
        m = {}
        m["hid"] = np.ascontiguousarray(hid[b, q * TPC:(q + 1) * TPC, :])
        if q == 0:
            m["hid_halo"] = np.zeros((3, DM), np.float32)
        else:
            m["hid_halo"] = np.ascontiguousarray(hid[b, q * TPC - 3:q * TPC, :])
        m["w_norm_in"] = np.asarray(ins["L0_attn_w_norm_in"], np.float32).reshape(1, DM)
        m["w_norm_in_t"] = np.asarray(ins["L0_attn_w_norm_in"], np.float32).reshape(8, 128).T.copy()
        m["w_in_proj"] = np.asarray(ins["L0_attn_w_in_proj"], np.float32)
        cw = np.asarray(ins["L0_attn_conv_w"], np.float32).reshape(DI, 4)
        cwr = np.zeros((128, 64), np.float32)
        for t in range(4):
            cwr[:, t * 16:(t + 1) * 16] = cw[:, t].reshape(16, 128).T
        m["conv_w"] = cwr
        m["conv_b"] = np.asarray(ins["L0_attn_conv_b"], np.float32).reshape(16, 128).T.copy()
        m["w_dt"] = np.asarray(ins["L0_attn_w_dt"], np.float32)
        m["b_dt"] = np.asarray(ins["L0_attn_b_dt"], np.float32).reshape(16, 128).T.copy()
        m["A_log"] = np.asarray(ins["L0_attn_A_log"], np.float32).reshape(8, 1)
        m["Dp"] = np.asarray(ins["L0_attn_D"], np.float32).reshape(8, 1)
        m["Dp_row"] = np.asarray(ins["L0_attn_D"], np.float32).reshape(1, 8)
        m["w_norm_out"] = np.asarray(ins["L0_attn_w_norm_out"], np.float32).reshape(1, DI)
        m["w_norm_out_t"] = np.asarray(ins["L0_attn_w_norm_out"], np.float32).reshape(16, 128).T.copy()
        m["w_out_proj"] = np.asarray(ins["L0_attn_w_out_proj"], np.float32)
        m["w1"] = np.asarray(ins["L0_ffn_w1"], np.float32)
        m["b1"] = np.asarray(ins["L0_ffn_b1"], np.float32).reshape(32, 128).T.copy()
        m["b1row"] = np.asarray(ins["L0_ffn_b1"], np.float32).reshape(1, DF)
        m["w2"] = np.asarray(ins["L0_ffn_w2"], np.float32)
        m["b2"] = np.asarray(ins["L0_ffn_b2"], np.float32).reshape(1, DM)
        m["Wq"] = np.asarray(ins["L1_attn_Wq"], np.float32)
        m["Wk"] = np.asarray(ins["L1_attn_Wk"], np.float32)
        m["Wv"] = np.asarray(ins["L1_attn_Wv"], np.float32)
        m["Wg"] = np.asarray(ins["L1_attn_Wg"], np.float32)
        m["Wo"] = np.asarray(ins["L1_attn_Wo"], np.float32)
        m["gn_w"] = np.asarray(ins["L1_attn_gn_w"], np.float32).reshape(1, DM)
        m["gn_b"] = np.asarray(ins["L1_attn_gn_b"], np.float32).reshape(1, DM)
        m["clip_x"] = np.asarray(ins["L1_ffn_clip_x"], np.float32).reshape(1, 1)
        m["clip_down"] = np.asarray(ins["L1_ffn_clip_down"], np.float32).reshape(1, 1)
        m["w_gate"] = np.asarray(ins["L1_ffn_w_gate"], np.float32)
        m["w_up"] = np.asarray(ins["L1_ffn_w_up"], np.float32)
        m["w_down"] = np.asarray(ins["L1_ffn_w_down"], np.float32)
        m.update(host_constants(core))
        maps.append(m)
    return maps


def kernel(debug=(), stage=99, lnsub=9, lntiles=99, **inputs):
    nc = build_program(debug=debug, stage=stage, lnsub=lnsub, lntiles=lntiles)
    maps = make_inmaps(inputs)
    res = run_bass_kernel_spmd(nc, maps, core_ids=list(range(8)))
    outs = [res.results[c]["out"] for c in range(8)]
    full = np.stack([np.concatenate(outs[0:4], axis=0),
                     np.concatenate(outs[4:8], axis=0)])
    if debug:
        dbg = [{k[4:]: v for k, v in res.results[c].items() if k.startswith("dbg_")}
               for c in range(8)]
        return full, dbg
    return full


# version 06 (final)
# speedup vs baseline: 1.0372x; 1.0372x over previous
import numpy as np
import ml_dtypes
import concourse.bass as bass
import concourse.mybir as mybir
from concourse.bass_utils import run_bass_kernel_spmd

F32 = mybir.dt.float32
BF16 = mybir.dt.bfloat16
I32 = mybir.dt.int32
AF = mybir.ActivationFunctionType
ALU = mybir.AluOpType
AX = mybir.AxisListType
bfloat16 = ml_dtypes.bfloat16

TPC = 2048
NTT = 16
DM = 1024
DI = 2048
DF = 4096
DFI = 2816
NH0, P0H, N0 = 8, 256, 64
NH1, DH1 = 16, 64
CH = 128
NCH = TPC // CH
EPS0 = float(np.finfo(np.float32).eps)

ENGINES = ("tensor", "vector", "scalar", "gpsimd", "sync")


class Prog:

    def __init__(self, nc):
        self.nc = nc
        self.streams = {e: [] for e in ENGINES}
        self.sems = {}
        self.semcnt = {}
        self.ctxs = []

    def sem(self, name):
        cm = self.nc.semaphore(name)
        h = cm.__enter__()
        self.ctxs.append(cm)
        self.sems[name] = h
        self.semcnt[name] = 0
        return h

    def ev(self, name):
        return (name, self.semcnt[name])

    DMA_SEMS = ["dC", "dR", "dW", "dP", "dK", "dO", "dS", "dV", "dM"]

    def step(self, engine, fn, wait=(), sig=None, n=1):
        wait = [w for w in wait if w is not None]
        byname = {}
        for (s, v) in wait:
            byname[s] = max(byname.get(s, 0), v)
        wait = list(byname.items())
        for (s, v) in wait:
            assert v <= self.semcnt[s], f"wait on future event {s}:{v} > {self.semcnt[s]}"
            if s in self.DMA_SEMS:
                assert v == self.semcnt[s], (
                    f"partial DMA wait {s}:{v} != {self.semcnt[s]} (completions are "
                    f"unordered; wait must cover all issued DMAs on this sem)")
        ev = None
        if sig is not None:
            self.semcnt[sig] += n
            ev = (sig, self.semcnt[sig])
        sems = self.sems
        waits = tuple(wait)

        def emit(eng, fn=fn, waits=waits, signame=sig, n=n):
            for (s, v) in waits:
                if v > 0:
                    eng.wait_ge(sems[s], v)
            inst = fn(eng)
            if signame is not None:
                inst.then_inc(sems[signame], n)
        self.streams[engine].append(emit)
        return ev

    def dma(self, engine, out, in_, wait=(), sig=None):
        return self.step(engine, lambda e: e.dma_start(out=out, in_=in_),
                         wait=wait, sig=sig, n=16)

    def barrier(self, engines=("tensor", "vector", "scalar", "gpsimd", "sync")):
        evs = []
        for e in engines:
            cw = [self.const_ev, ("bar_" + e, self.semcnt["bar_" + e])]
            if e == "vector":
                evs.append(self.step(e, lambda eng: eng.tensor_scalar(
                    out=self.dummy_v[0:1, 0:1], in0=self.const_src, scalar1=0.0, scalar2=None,
                    op0=ALU.mult), wait=cw, sig="bar_" + e))
            elif e == "gpsimd":
                evs.append(self.step(e, lambda eng: eng.tensor_scalar(
                    out=self.dummy_g[0:1, 0:1], in0=self.const_src, scalar1=0.0, scalar2=None,
                    op0=ALU.mult), wait=cw, sig="bar_" + e))
            elif e == "scalar":
                evs.append(self.step(e, lambda eng: eng.mul(
                    out=self.dummy_s[0:1, 0:1], in_=self.const_src, mul=0.0),
                    wait=cw, sig="bar_" + e))
            else:
                evs.append(self.step(e, lambda eng: eng.nop(), wait=cw[1:], sig="bar_" + e))
        dwaits = [(s, self.semcnt[s]) for s in self.DMA_SEMS
                  if s in self.semcnt and self.semcnt[s] > 0]
        for e in engines:
            self.step(e, lambda eng: eng.nop(),
                      wait=[x for x in evs if x[0] != "bar_" + e] + dwaits)

    def finalize(self, block):
        for ename in ENGINES:
            stream = self.streams[ename]
            if not stream:
                continue
            dec = getattr(block, ename)

            def make(stream):
                def _(eng):
                    for f in stream:
                        f(eng)
                return _
            dec(make(stream))


class DmaPipe:

    def __init__(self, p, base, n=2, engine="sync"):
        self.p = p
        self.base = base
        self.n = n
        self.engine = engine
        self.i = 0
        for j in range(n):
            name = base + str(j)
            if name not in p.sems:
                p.sem(name)
                p.DMA_SEMS = tuple(list(p.DMA_SEMS) + [name])

    def load(self, out, in_, wait=()):
        s = self.base + str(self.i % self.n)
        self.i += 1
        return self.p.dma(self.engine, out, in_, wait=wait, sig=s)

    def load_group(self, pairs, wait=()):
        s = self.base + str(self.i % self.n)
        self.i += 1
        ev = None
        for (out, in_) in pairs:
            ev = self.p.dma(self.engine, out, in_, wait=wait, sig=s)
        return ev


class Slots:

    def __init__(self, n):
        self.n = n
        self.busy = [None] * n
        self.i = 0

    def get(self):
        i = self.i % self.n
        self.i += 1
        return i, self.busy[i]

    def done(self, i, ev):
        self.busy[i] = ev


def build_program(debug=(), stage=99, lnsub=9, lntiles=99):
    nc = bass.Bass("TRN2", target_bir_lowering=False)
    g = {}

    def din(name, shape, dt=F32):
        g[name] = nc.dram_tensor(name, list(shape), dt, kind="ExternalInput").ap()

    def dout(name, shape, dt=F32):
        g[name] = nc.dram_tensor(name, list(shape), dt, kind="ExternalOutput").ap()

    def dint(name, shape, dt=F32, shared=False):
        g[name] = nc.dram_tensor(name, list(shape), dt,
                                 addr_space="Shared" if shared else "Local").ap()

    din("hid", (TPC, DM))
    din("hid_halo", (3, DM))
    din("w_norm_in", (1, DM))
    din("w_in_proj", (DM, 5128))
    din("conv_w", (128, 16 * 4))
    din("conv_b", (128, 16))
    din("w_dt", (8, DI))
    din("b_dt", (128, 16))
    din("A_log", (8, 1))
    din("Dp", (8, 1))
    din("w_norm_out", (1, DI))
    din("w_out_proj", (DI, DM))
    din("w1", (DM, DF))
    din("b1", (128, 32))
    din("b1row", (1, DF))
    din("w2", (DF, DM))
    din("b2", (1, DM))
    din("Wq", (DM, DM)); din("Wk", (DM, DM)); din("Wv", (DM, DM))
    din("Wg", (DM, DM)); din("Wo", (DM, DM))
    din("gn_w", (1, DM)); din("gn_b", (1, DM))
    din("clip_x", (1, 1)); din("clip_down", (1, 1))
    din("w_gate", (DFI, DM)); din("w_up", (DFI, DM)); din("w_down", (DM, DFI))
    din("consts", (128, 8))
    din("eye_bf", (128, 128), BF16)
    din("eye_f32", (128, 128))
    din("ones_f32", (1, 128))
    din("Dp_row", (1, 8))
    din("ssd_premaskT", (8, 8))
    din("ssd_maskinvT", (8, 8))
    din("trilT", (128, 128))
    din("dtmean", (DI, 8), BF16)
    din("w_norm_in_t", (128, 8))
    din("w_norm_out_t", (128, 16))
    din("onesK", (1, 512), BF16)
    din("ropeP", (128, 128), BF16)
    din("cosq_bf", (128, TPC), BF16); din("sinq_bf", (128, TPC), BF16)
    din("cosk_bf", (128, TPC), BF16); din("sink_bf", (128, TPC), BF16)
    din("dmaskT", (128, NH1 * CH), BF16)
    din("gfacT", (128, NH1)); din("gkT", (128, NH1)); din("gCT", (64, NH1))
    din("ret_coef", (64, 8 * NH1))
    dout("out", (TPC, DM))
    dint("w1bf", (DM + 1, DF), BF16)
    dint("w2bf", (DF + 1, DM), BF16)
    dint("ccA_in", (65, TPC), BF16)
    dint("winbf", (DM, 5128), BF16)
    dint("woutbf", (DI, DM), BF16)
    dint("xtok_park", (TPC, DI), BF16)
    dint("wgTint", (DM, DFI), BF16)
    dint("wuTint", (DM, DFI), BF16)
    dint("wdTint", (DFI, DM), BF16)
    dint("hT_park", (DFI, TPC), BF16)
    dint("hq_park", (DFI, TPC), BF16)
    dint("kT_park", (DM, TPC), BF16)
    dint("v_park", (TPC, DM), BF16)
    dint("g_park", (TPC, DM), BF16)
    dint("ccM_in", (1, 16))
    dint("ccM_out", (1, 16), shared=True)
    dint("ccM2_in", (1, 16))
    dint("ccM2_out", (1, 16), shared=True)
    dint("ccS_in", (64, 1024), BF16)
    dint("ccS_out", (8 * 64, 1024), BF16, shared=True)
    dint("xnT1_park", (DM, TPC), BF16)
    dint("scl_park", (128, 144))
    dint("wqbfT", (DM, DM), BF16)
    dint("wkbfT", (DM, DM), BF16)
    dint("wvbfT", (DM, DM), BF16)
    dint("wgbfT", (DM, DM), BF16)
    dint("wobfT", (DM, DM), BF16)
    dint("ccA_out", (8 * 65, TPC), BF16, shared=True)

    p = Prog(nc)
    from contextlib import ExitStack
    stack = ExitStack()

    def sb(name, shape, dt):
        t = stack.enter_context(nc.sbuf_tensor(name, list(shape), dt))
        g[name] = t
        return t

    def psum(name, shape, dt=F32):
        t = stack.enter_context(nc.psum_tensor(name, list(shape), dt))
        g[name] = t
        return t

    def cvo(buf, nk, w, c0=0):
        return buf[:, c0:c0 + nk * w].rearrange("p (K j) -> p K j", j=w)

    def cvi(dram, k0, nk, c0, w):
        return dram[k0:k0 + nk * 128, c0:c0 + w].rearrange("(K p) n -> p K n", p=128)

    def cvrows(dram, r0, nt, c0, w, pknum=128):
        return dram[r0:r0 + nt * pknum, c0:c0 + w].rearrange("(T p) n -> p T n", p=pknum)

    def cvcols(buf, nt, w, c0=0):
        return buf[:, c0:c0 + nt * w].rearrange("p (T j) -> p T j", j=w)

    for s in ("dC", "dR", "dW", "dP", "dK", "dO", "dS", "dV", "dM", "pe", "act", "dve", "gps", "cc",
              "bar_tensor", "bar_vector", "bar_scalar", "bar_gpsimd", "bar_sync"):
        p.sem(s)

    CP = sb("cpool", (128, 8), F32)
    for _ci, _cv in enumerate([256.0, 0.25, 1.0, 0.5, 4.0, 64.0, 127.0, -128.0]):
        nc.const_aps.register(F32, _cv, CP[:, _ci:_ci + 1])
    p.dummy_v = sb("bardummy_v", (1, 8), F32)
    p.dummy_g = sb("bardummy_g", (1, 8), F32)
    p.dummy_s = sb("bardummy_s", (1, 8), F32)
    EYE = sb("eye", (128, 128), BF16)
    EYEF = sb("eyef", (128, 128), F32)
    ONESK = sb("onesk", (1, 512), BF16)
    ONESF = sb("onesf", (1, 128), F32)
    R = sb("R", (128, NTT * DM), F32)
    small = sb("small", (128, 15), F32)
    bias = sb("biasp", (128, 48), F32)
    aH = nc.auto_arena()
    h1T = aH.sbuf_tensor("h1T", [128, 8 * (TPC + 3)], BF16)
    aH.allocate()
    a12 = nc.auto_arena()
    sb12 = lambda name, shape, dt: a12.sbuf_tensor(name, list(shape), dt)
    xnT = sb12("xnT", (128, 8 * TPC), BF16)
    lnx0 = sb12("lnx0", (128, 1024), F32)
    lnx = [lnx0, lnx0]
    lnsq = sb12("lnsq", (128, 1024), F32)
    pass
    pw_g0 = [sb12(f"pwg0{j}", (128, 512), F32) for j in range(2)]
    pw_g1 = [sb12(f"pwg1{j}", (128, 512), F32) for j in range(2)]
    pw_d = [sb12(f"pwd{j}", (128, 512), F32) for j in range(2)]
    pw_i = [sb12(f"pwi{j}", (128, 512), I32) for j in range(2)]
    pw_j = [sb12(f"pwj{j}", (128, 512), I32) for j in range(2)]
    syT = sb12("syT", (128, 32 * 512), BF16)
    lnxb0 = syT[:, 0:1024]
    lnh0 = syT[:, 1024:2048]
    lnxb = [lnxb0, lnxb0]
    lnh = [lnh0, lnh0]
    wpan0 = sb12("wpan0", (128, 8 * 128 + 128), BF16)
    wpan = [wpan0, wpan0]
    wk2 = [sb12(f"wk2{j}", (128, 1024), BF16) for j in range(2)]
    wprep = sb12("wprep", (128, 1024), F32)
    pw_w = [wprep[:, 0:512], wprep[:, 512:1024]]
    pw_e = pw_g1
    pw_z = pw_w
    wprepb0 = sb12("wprepb0", (128, 1024), BF16)
    wprepb = [wprepb0, wprepb0]
    b2sb = wprepb0
    vout0 = lnsq[:, 0:512]
    vouti0 = sb12("vouti0", (128, 512), I32)
    voutb0 = sb12("voutb0", (128, 512), BF16)
    vout, vouti, voutb = [vout0, vout0], [vouti0, vouti0], [voutb0, voutb0]
    a12.allocate()
    p.arenas = [aH, a12]

    PA = [psum(f"PA{j}", (128, 512), F32) for j in range(2)]
    PTs = [psum(f"PT{j}", (128, 512), BF16) for j in range(2)]
    PW = [psum(f"PW{j}", (128, 512), F32) for j in range(4)]

    p.dma("sync", CP[:, :], g["consts"][:, :], sig="dC")
    p.dma("sync", EYE[:, :], g["eye_bf"][:, :], sig="dC")
    p.dma("sync", EYEF[:, :], g["eye_f32"][:, :], sig="dC")
    p.dma("sync", ONESF[:, :], g["ones_f32"][:, :], sig="dC")
    p.dma("sync", ONESK[:, :], g["onesK"][:, :], sig="dC")
    p.dma("sync", bias[:, 0:16], g["b_dt"][:, :], sig="dC")
    p.dma("sync", bias[:, 16:32], g["conv_b"][:, :], sig="dC")
    evC = p.ev("dC")
    p.const_src = CP[0:1, 0:1]
    p.const_ev = evC
    for i in range(NTT):
        p.dma("sync", R[:, i * DM:(i + 1) * DM], g["hid"][i * 128:(i + 1) * 128, :], sig="dR")
    evR = p.ev("dR")

    sl_prep = Slots(1)
    wprev = None
    for k in range(8):
        for q in range(4):
            j, wre = sl_prep.get()
            eL = p.dma("sync", wprep[:, 0:1024],
                       g["w1"][k * 128:(k + 1) * 128, q * 1024:(q + 1) * 1024],
                       wait=[wre, wprev], sig="dW")
            eC1 = p.step("vector", lambda e, j=j: e.tensor_copy(
                out=wprepb[j][:, :], in_=wprep[:, 0:1024]), wait=[eL], sig="dve")
            wprev = eC1
            eS1 = p.dma("gpsimd", g["w1bf"][k * 128:(k + 1) * 128, q * 1024:(q + 1) * 1024],
                        wprepb[j][:, :], wait=[eC1], sig="dS")
            sl_prep.done(j, eS1)
    for q in range(4):
        j, wre = sl_prep.get()
        eL = p.dma("sync", wprep[0:1, 0:1024], g["b1row"][0:1, q * 1024:(q + 1) * 1024],
                   wait=[wre, wprev], sig="dW")
        eC1 = p.step("vector", lambda e, j=j: e.tensor_copy(
            out=wprepb[j][0:1, :], in_=wprep[0:1, 0:1024]), wait=[eL], sig="dve")
        wprev = eC1
        eS1 = p.dma("gpsimd", g["w1bf"][DM:DM + 1, q * 1024:(q + 1) * 1024],
                    wprepb[j][0:1, :], wait=[eC1], sig="dS")
        sl_prep.done(j, eS1)
    lastw = None
    for k in range(32):
        j, wre = sl_prep.get()
        eL = p.dma("sync", wprep[:, 0:1024], g["w2"][k * 128:(k + 1) * 128, :],
                   wait=[wre, wprev], sig="dW")
        eC = p.step("vector", lambda e, j=j: e.tensor_scalar(
            out=wprepb[j][:, :], in0=wprep[:, 0:1024], scalar1=1.0 / 256.0, scalar2=None,
            op0=ALU.mult), wait=[eL], sig="dve")
        wprev = eC
        lastw = p.dma("gpsimd", g["w2bf"][k * 128:(k + 1) * 128, :], wprepb[j][:, :],
                      wait=[eC], sig="dS")
        sl_prep.done(j, lastw)
    eLb2 = p.dma("sync", wprep[0:1, 0:1024], g["b2"][:, :], wait=[wprev], sig="dW")
    eCb2 = p.step("vector", lambda e: e.tensor_copy(out=b2sb[0:1, :], in_=wprep[0:1, 0:1024]),
                  wait=[eLb2, lastw], sig="dve")
    evW2done = p.dma("gpsimd", g["w2bf"][DF:DF + 1, :], b2sb[0:1, :], wait=[eCb2], sig="dS")

    SL = small
    sl_ln = Slots(1)
    sl_pt = Slots(2)
    for i in range(min(lntiles, 0 if stage < 1 else (NTT if stage == 1 else NTT + 1))):
        halo = (i == NTT)
        np_ = 3 if halo else 128
        j, wre = sl_ln.get()
        X = lnx[j]
        XB = lnxb[j]
        H1B = lnh[j]
        if halo:
            evh = p.dma("sync", X[0:3, 0:DM], g["hid_halo"][:, :], wait=[wre], sig="dR")
            src = X[0:3, 0:DM]
            w0 = [evh, evC]
        else:
            src = R[:, i * DM:(i + 1) * DM]
            w0 = [evR, evC, wre]
        eS2 = p.step("scalar", lambda e, src=src, np_=np_: e.activation(
            out=lnsq[0:np_, :], in_=src, func=AF.Square, accum_out=SL[0:np_, 0:1]),
            wait=w0, sig="act")
        eS1 = p.step("vector", lambda e, src=src, np_=np_: e.reduce_sum(
            out=SL[0:np_, 1:2], in_=src, axis=AX.X), wait=w0, sig="dve")
        eM = p.step("vector", lambda e, np_=np_: e.tensor_scalar(
            out=SL[0:np_, 2:3], in0=SL[0:np_, 1:2], scalar1=1.0 / DM, scalar2=None, op0=ALU.mult),
            wait=[eS1], sig="dve")
        eV1 = p.step("vector", lambda e, np_=np_: e.tensor_scalar(
            out=SL[0:np_, 3:4], in0=SL[0:np_, 0:1], scalar1=1.0 / DM, scalar2=1e-5,
            op0=ALU.mult, op1=ALU.add), wait=[eS2], sig="dve")
        eV2 = p.step("vector", lambda e, np_=np_: e.scalar_tensor_tensor(
            out=SL[0:np_, 4:5], in0=SL[0:np_, 2:3], scalar=SL[0:np_, 2:3], in1=SL[0:np_, 3:4],
            op0=ALU.mult, op1=ALU.subtract), wait=[eM, eV1], sig="dve")
        eV3 = p.step("vector", lambda e, np_=np_: e.tensor_scalar(
            out=SL[0:np_, 4:5], in0=SL[0:np_, 4:5], scalar1=-1.0, scalar2=None, op0=ALU.mult),
            wait=[eV2], sig="dve")
        eSq = p.step("scalar", lambda e, np_=np_: e.activation(
            out=SL[0:np_, 5:6], in_=SL[0:np_, 4:5], func=AF.Sqrt), wait=[eV3], sig="act")
        eRs = p.step("vector", lambda e, np_=np_: e.reciprocal(
            out=SL[0:np_, 6:7], in_=SL[0:np_, 5:6]), wait=[eSq], sig="dve")
        eNb = p.step("vector", lambda e, np_=np_: e.scalar_tensor_tensor(
            out=SL[0:np_, 7:8], in0=SL[0:np_, 2:3], scalar=-1.0, in1=SL[0:np_, 6:7],
            op0=ALU.mult, op1=ALU.mult), wait=[eRs], sig="dve")
        if not halo:
            eXn = p.step("vector", lambda e, src=src, np_=np_, X=X: e.tensor_scalar(
                out=X[0:np_, :], in0=src, scalar1=SL[0:np_, 6:7], scalar2=SL[0:np_, 7:8],
                op0=ALU.mult, op1=ALU.add), wait=[eNb, eRs], sig="dve")
        else:
            eXn = p.step("vector", lambda e, np_=np_, X=X: e.tensor_scalar(
                out=lnsq[0:np_, 0:DM], in0=X[0:np_, 0:DM], scalar1=SL[0:np_, 6:7],
                scalar2=SL[0:np_, 7:8], op0=ALU.mult, op1=ALU.add), wait=[eNb, eRs], sig="dve")
            eXn = p.step("vector", lambda e, np_=np_, X=X: e.tensor_copy(
                out=X[0:np_, 0:DM], in_=lnsq[0:np_, 0:DM]), wait=[eXn], sig="dve")
        eRq = p.step("scalar", lambda e, np_=np_, X=X: e.activation(
            out=lnsq[0:np_, 0:DM], in_=X[0:np_, 0:DM], func=AF.Square,
            accum_out=SL[0:np_, 8:9]), wait=[eXn], sig="act")
        eRv = p.step("vector", lambda e, np_=np_: e.tensor_scalar(
            out=SL[0:np_, 9:10], in0=SL[0:np_, 8:9], scalar1=1.0 / DM, scalar2=EPS0,
            op0=ALU.mult, op1=ALU.add), wait=[eRq], sig="dve")
        eRsq = p.step("scalar", lambda e, np_=np_: e.activation(
            out=SL[0:np_, 10:11], in_=SL[0:np_, 9:10], func=AF.Sqrt), wait=[eRv], sig="act")
        eRr = p.step("vector", lambda e, np_=np_: e.reciprocal(
            out=SL[0:np_, 11:12], in_=SL[0:np_, 10:11]), wait=[eRsq], sig="dve")
        if lnsub < 2:
            sl_ln.done(j, eXn)
            continue
        eXb = p.step("vector", lambda e, np_=np_, X=X, XB=XB: e.tensor_copy(
            out=XB[0:np_, 0:DM], in_=X[0:np_, 0:DM]), wait=[eXn], sig="dve") if not halo else None
        eH1 = p.step("scalar", lambda e, np_=np_, X=X, H1B=H1B: e.activation(
            out=H1B[0:np_, 0:DM], in_=X[0:np_, 0:DM], func=AF.Copy, scale=SL[0:np_, 11:12]),
            wait=[eRr, eXn], sig="act")
        if not halo:
            eRadd = p.step("vector", lambda e, i=i, X=X: e.tensor_tensor(
                out=R[:, i * DM:(i + 1) * DM], in0=R[:, i * DM:(i + 1) * DM],
                in1=X[0:128, :], op=ALU.add), wait=[eXn], sig="dve")
        if lnsub < 3:
            sl_ln.done(j, eRadd if not halo else eH1)
            continue
        lastcp = None
        for f in range(8):
            if not halo:
                jt, wt = sl_pt.get()
                eT1 = p.step("tensor", lambda e, f=f, jt=jt, XB=XB: e.transpose(
                    PTs[jt][:, 0:128], XB[0:128, f * 128:(f + 1) * 128], EYE[:, :]),
                    wait=[eXb, evC, wt], sig="pe")
                c1 = p.step("vector", lambda e, i=i, f=f, jt=jt: e.tensor_copy(
                    out=xnT[:, f * TPC + i * 128: f * TPC + (i + 1) * 128],
                    in_=PTs[jt][:, 0:128]), wait=[eT1], sig="dve")
                sl_pt.done(jt, c1)
                jt, wt = sl_pt.get()
                eT2 = p.step("tensor", lambda e, f=f, jt=jt, H1B=H1B: e.transpose(
                    PTs[jt][:, 0:128], H1B[0:128, f * 128:(f + 1) * 128], EYE[:, :]),
                    wait=[eH1, evC, wt], sig="pe")
                c2 = p.step("vector", lambda e, i=i, f=f, jt=jt: e.tensor_copy(
                    out=h1T[:, f * (TPC + 3) + 3 + i * 128: f * (TPC + 3) + 3 + (i + 1) * 128],
                    in_=PTs[jt][:, 0:128]), wait=[eT2], sig="dve")
                sl_pt.done(jt, c2)
                lastcp = c2
            else:
                jt, wt = sl_pt.get()
                eT2 = p.step("tensor", lambda e, f=f, jt=jt, H1B=H1B: e.transpose(
                    PTs[jt][:, 0:3], H1B[0:3, f * 128:(f + 1) * 128], EYE[0:3, 0:3]),
                    wait=[eH1, evC, wt], sig="pe")
                c2 = p.step("vector", lambda e, f=f, jt=jt: e.tensor_copy(
                    out=h1T[:, f * (TPC + 3): f * (TPC + 3) + 3],
                    in_=PTs[jt][:, 0:3]), wait=[eT2], sig="dve")
                sl_pt.done(jt, c2)
                lastcp = c2
        sl_ln.done(j, lastcp)

    p.barrier()

    if "xnT" in debug:
        dout("dbg_xnT", (DM, TPC), BF16)
        for f in range(8):
            p.dma("gpsimd", g["dbg_xnT"][f * 128:(f + 1) * 128, :],
                  xnT[:, f * TPC:(f + 1) * TPC], sig="dO")
    if "h1T" in debug:
        dout("dbg_h1T", (DM, TPC + 3), BF16)
        for f in range(8):
            p.dma("gpsimd", g["dbg_h1T"][f * 128:(f + 1) * 128, :],
                  h1T[:, f * (TPC + 3):(f + 1) * (TPC + 3)], sig="dO")

    sl_pwl = Slots(2)
    sl_wpan = Slots(1)
    sl_wk2 = Slots(2)
    sl_vout = Slots(1)
    sl_pa = Slots(2)
    sl_pt2 = Slots(2)
    pipeP = DmaPipe(p, "dPp", n=2)
    pipeK = DmaPipe(p, "dKk", n=2)
    pw_free = [None] * 4
    for sc in range(4 if stage >= 2 else 0):
        c0 = sc * 512
        ev_syT_guard = p.ev("pe")
        for m in range(32):
            jw, wwp = sl_wpan.get()
            wp = wpan[jw]
            ePan = pipeP.load_group(
                [(cvo(wp, 8, 128), cvi(g["w1bf"], 0, 8, m * 128, 128)),
                 (wp[0:1, 1024:1152], g["w1bf"][DM:DM + 1, m * 128:(m + 1) * 128])],
                wait=[wwp, evW2done])
            ja, wpa = sl_pa.get()
            PAm = PA[ja]
            mm_ev = None
            for k in range(9):
                if k < 8:
                    mm_ev = p.step("tensor", lambda e, k=k, wp=wp, PAm=PAm, c0=c0: e.matmul(
                        PAm[:, :], wp[:, k * 128:(k + 1) * 128],
                        xnT[:, k * TPC + c0: k * TPC + c0 + 512],
                        start=(k == 0), stop=False),
                        wait=[ePan, wpa] if k == 0 else [],
                        sig=None)
                else:
                    mm_ev = p.step("tensor", lambda e, wp=wp, PAm=PAm: e.matmul(
                        PAm[:, :], wp[0:1, 1024:1152], ONESK[0:1, :],
                        start=False, stop=True),
                        wait=[evC], sig="pe")
            sl_wpan.done(jw, mm_ev)
            jp, wpw = sl_pwl.get()
            W_, G0, G1, D_, E_, Z_, I_, J_ = (pw_w[jp], pw_g0[jp], pw_g1[jp], pw_d[jp],
                                              pw_e[jp], pw_z[jp], pw_i[jp], pw_j[jp])
            eW = p.step("scalar", lambda e, PAm=PAm, W_=W_: e.activation(
                out=W_[:, :], in_=PAm[:, :], func=AF.Copy, scale=256.0),
                wait=[mm_ev, wpw], sig="act")
            sl_pa.done(ja, eW)
            eSi = p.step("vector", lambda e, W_=W_, I_=I_: e.tensor_copy(
                out=I_[:, :], in_=W_[:, :]), wait=[eW], sig="dve")
            eIi = p.step("vector", lambda e, I_=I_, J_=J_: e.tensor_scalar(
                out=J_[:, :], in0=I_[:, :], scalar1=6, scalar2=None,
                op0=ALU.arith_shift_right), wait=[eSi], sig="dve")
            eFi = p.step("vector", lambda e, I_=I_: e.tensor_scalar(
                out=I_[:, :], in0=I_[:, :], scalar1=63, scalar2=None,
                op0=ALU.bitwise_and), wait=[eIi], sig="dve")
            eG0 = p.step("scalar", lambda e, J_=J_, G0=G0: e.activation(
                out=G0[:, :], in_=J_[:, :], func=AF.Gelu, scale=0.25), wait=[eIi], sig="act")
            eG1 = p.step("scalar", lambda e, J_=J_, G1=G1: e.activation(
                out=G1[:, :], in_=J_[:, :], func=AF.Gelu, scale=0.25, bias=0.25),
                wait=[eIi], sig="act")
            eD = p.step("gpsimd", lambda e, G0=G0, G1=G1, D_=D_: e.tensor_tensor(
                out=D_[:, :], in0=G1[:, :], in1=G0[:, :], op=ALU.subtract),
                wait=[eG0, eG1], sig="gps")
            eE4 = p.step("vector", lambda e, I_=I_, D_=D_, E_=E_: e.scalar_tensor_tensor(
                out=E_[:, :], in0=I_[:, :], scalar=4.0, in1=D_[:, :],
                op0=ALU.mult, op1=ALU.mult), wait=[eD, eFi], sig="dve")
            eZ = p.step("vector", lambda e, G0=G0, E_=E_, Z_=Z_: e.scalar_tensor_tensor(
                out=Z_[:, :], in0=G0[:, :], scalar=256.0, in1=E_[:, :],
                op0=ALU.mult, op1=ALU.add), wait=[eE4], sig="dve")
            eZi = p.step("vector", lambda e, Z_=Z_, I_=I_: e.tensor_copy(
                out=I_[:, :], in_=Z_[:, :]), wait=[eZ], sig="dve")
            eSy = p.step("gpsimd", lambda e, m=m, I_=I_: e.tensor_copy(
                out=syT[:, m * 512:(m + 1) * 512], in_=I_[:, :]),
                wait=[eZi, ev_syT_guard], sig="gps")
            sl_pwl.done(jp, eSy)
        evPWL = p.ev("gps")
        for og in range(2):
            ev_o = [None] * 4
            for k in range(33):
                jk, wk_ = sl_wk2.get()
                wkt = wk2[jk]
                if k < 32:
                    eK = pipeK.load(wkt[:, :], g["w2bf"][k * 128:(k + 1) * 128, :],
                                    wait=[wk_])
                else:
                    eK = pipeK.load(wkt[0:1, :], g["w2bf"][DF:DF + 1, :],
                                    wait=[wk_])
                last_mm = None
                for o in range(4):
                    oo = og * 4 + o
                    if k < 32:
                        last_mm = p.step("tensor", lambda e, k=k, o=o, oo=oo, wkt=wkt: e.matmul(
                            PW[o][:, :], wkt[:, oo * 128:(oo + 1) * 128],
                            syT[:, k * 512:(k + 1) * 512],
                            start=(k == 0), stop=False),
                            wait=[eK, evPWL] + ([pw_free[o]] if k == 0 else []),
                            sig="pe" if o == 3 else None)
                    else:
                        last_mm = p.step("tensor", lambda e, o=o, oo=oo, wkt=wkt: e.matmul(
                            PW[o][:, :], wkt[0:1, oo * 128:(oo + 1) * 128],
                            ONESK[0:1, :], start=False, stop=True),
                            wait=[eK, evC], sig="pe")
                        ev_o[o] = last_mm
                sl_wk2.done(jk, last_mm)
            for o in range(4):
                oo = og * 4 + o
                jv, wv = sl_vout.get()
                V, VI, VB = vout[jv], vouti[jv], voutb[jv]
                eV = p.step("scalar", lambda e, o=o, V=V: e.activation(
                    out=V[:, :], in_=PW[o][:, :], func=AF.Copy, scale=256.0),
                    wait=[ev_o[o], wv], sig="act")
                ev_o[o] = eV
                pw_free[o] = eV
                eVi = p.step("vector", lambda e, V=V, VI=VI: e.tensor_copy(
                    out=VI[:, :], in_=V[:, :]), wait=[eV], sig="dve")
                eVb = p.step("vector", lambda e, VI=VI, VB=VB: e.tensor_copy(
                    out=VB[:, :], in_=VI[:, :]), wait=[eVi], sig="dve")
                lastr = None
                for tt in range(4):
                    jt, wt = sl_pt2.get()
                    eT = p.step("tensor", lambda e, tt=tt, jt=jt, VB=VB: e.transpose(
                        PTs[jt][:, 0:128], VB[:, tt * 128:(tt + 1) * 128], EYE[:, :]),
                        wait=[eVb, wt], sig="pe")
                    ti = sc * 4 + tt
                    lastr = p.step("vector", lambda e, ti=ti, oo=oo, jt=jt: e.scalar_tensor_tensor(
                        out=R[:, ti * DM + oo * 128: ti * DM + (oo + 1) * 128],
                        in0=PTs[jt][:, 0:128], scalar=1.0 / 256.0,
                        in1=R[:, ti * DM + oo * 128: ti * DM + (oo + 1) * 128],
                        op0=ALU.mult, op1=ALU.add), wait=[eT], sig="dve")
                    sl_pt2.done(jt, lastr)
                sl_vout.done(jv, lastr)


    if stage >= 3:
        a12.free()
        a3c = nc.auto_arena()
        p.arenas.append(a3c)
        BT = a3c.sbuf_tensor("BT", [128, 4 * TPC], BF16); g["BT"] = BT
        eTall = a3c.sbuf_tensor("eTall", [128, 16 * 24], F32)
        lcsg = a3c.sbuf_tensor("lcsg", [8, TPC], F32); g["lcsg"] = lcsg
        hall = a3c.sbuf_tensor("hall", [64, 2048], F32); g["hall"] = hall
        hbf = a3c.sbuf_tensor("hbf", [128, 2048], BF16)
        wdtb = a3c.sbuf_tensor("wdtb", [8, DI], BF16)
        deye = a3c.sbuf_tensor("deye", [128, 8 * 128], BF16)
        lambc = a3c.sbuf_tensor("lambc", [64, 384], F32)
        coefb = a3c.sbuf_tensor("coefb", [64, 64], F32)
        smal2 = a3c.sbuf_tensor("smal2", [24, 32], F32)
        lgb = a3c.sbuf_tensor("lgb", [8, 8], BF16)
        lgf = a3c.sbuf_tensor("lgf", [8, 8], F32)
        lmT = a3c.sbuf_tensor("lmT", [8, 8], F32)
        crow = a3c.sbuf_tensor("crow", [1, 96], F32)
        ones8 = a3c.sbuf_tensor("ones8", [8, 512], F32)
        dtraw = a3c.sbuf_tensor("dtraw", [8, 512], BF16)
        dtmn = a3c.sbuf_tensor("dtmn", [128, 16 * 8], BF16)
        hsb = a3c.sbuf_tensor("hsb", [64, 2048], BF16)
        convw = a3c.sbuf_tensor("convw", [128, 64], F32)
        pmT = a3c.sbuf_tensor("pmT", [8, 8], F32)
        pmiT = a3c.sbuf_tensor("pmiT", [8, 8], F32)
        a3c.allocate()
        a3a = nc.auto_arena()
        p.arenas.append(a3a)
        xin = [a3a.sbuf_tensor(f"xin{j}", [128, 515], F32) for j in range(2)]
        cacc = [a3a.sbuf_tensor(f"cacc{j}", [128, 512], F32) for j in range(4)]
        xcb = [a3a.sbuf_tensor(f"xcb{j}", [128, 512], BF16) for j in range(2)]
        xtok = a3a.sbuf_tensor("xtok", [128, 4 * 2048], BF16)
        stash = a3a.sbuf_tensor("stash", [128, 48], F32)
        expd = [a3a.sbuf_tensor(f"expd{j}", [128, 512], F32) for j in range(2)]
        dtb = [a3a.sbuf_tensor(f"dtb{j}", [128, 512], BF16) for j in range(2)]
        wpanA = [a3a.sbuf_tensor(f"wpanA{j}", [128, 1024], BF16) for j in range(2)]
        Bs = [a3a.sbuf_tensor(f"Bs{j}", [128, 64], BF16) for j in range(2)]
        dtw = [a3a.sbuf_tensor(f"dtw{j}", [8, 512], F32) for j in range(2)]
        eraw = [a3a.sbuf_tensor(f"eraw{j}", [8, 3 * 128], F32) for j in range(2)]
        a3a.allocate()

        pipeW = DmaPipe(p, "dWi", n=2)
        sl_xin = Slots(2)
        sl_cacc = Slots(4)
        sl_xcb = Slots(2)
        sl_expd = Slots(2)
        sl_dtb = Slots(2)
        sl_bs = Slots(2)
        sl_dtw = Slots(2)
        sl_eraw = Slots(2)
        sl_pa3 = Slots(2)
        sl_pt3 = Slots(2)
        pw_free3 = [p.ev("act")] * 4

        evP = p.ev("dve")
        wnin = bias[:, 40:48]
        eWn = p.dma("sync", bias[:, 40:48], g["w_norm_in_t"][:, :], sig="dW")
        eCw = p.dma("sync", convw[:, :], g["conv_w"][:, :], sig="dW")
        sl_wp3 = Slots(1)
        wlast = None
        for kk in range(8):
            for q in range(6):
                w0 = 1024 * q
                w1_ = min(1024 * (q + 1), 5128)
                if w0 >= 5128:
                    break
                jj, wre = sl_wp3.get()
                eL = p.dma("sync", wprep[:, 0:w1_ - w0],
                           g["w_in_proj"][kk * 128:(kk + 1) * 128, w0:w1_],
                           wait=[wre, wlast], sig="dW")
                eC = p.step("vector", lambda e, jj=jj, kk=kk, n=w1_ - w0: e.tensor_scalar(
                    out=wprepb[jj][:, 0:n], in0=wprep[:, 0:n],
                    scalar1=bias[:, 40 + kk:41 + kk], scalar2=None, op0=ALU.mult),
                    wait=[eL, eWn], sig="dve")
                wlast = eC
                eS = p.dma("gpsimd", g["winbf"][kk * 128:(kk + 1) * 128, w0:w1_],
                           wprepb[jj][:, 0:w1_ - w0], wait=[eC], sig="dS")
                sl_wp3.done(jj, eS)
        eWno = p.dma("sync", bias[:, 40:48], g["w_norm_out_t"][:, 0:8], wait=[wlast], sig="dW")
        eWno2 = p.dma("sync", stash[:, 0:8], g["w_norm_out_t"][:, 8:16], sig="dW")
        for kk in range(16):
            jj, wre = sl_wp3.get()
            eL = p.dma("sync", wprep[:, 0:1024], g["w_out_proj"][kk * 128:(kk + 1) * 128, :],
                       wait=[wre, wlast], sig="dW")
            sccol = bias[:, 40 + kk:41 + kk] if kk < 8 else stash[:, kk - 8:kk - 7]
            eC = p.step("vector", lambda e, jj=jj, sccol=sccol: e.tensor_scalar(
                out=wprepb[jj][:, 0:1024], in0=wprep[:, 0:1024],
                scalar1=sccol, scalar2=None, op0=ALU.mult),
                wait=[eL, eWno, eWno2], sig="dve")
            wlast = eC
            eS = p.dma("gpsimd", g["woutbf"][kk * 128:(kk + 1) * 128, :],
                       wprepb[jj][:, 0:1024], wait=[eC], sig="dS")
            sl_wp3.done(jj, eS)
        for q in range(2):
            jj, wre = sl_wp3.get()
            eL = p.dma("sync", wprep[0:8, 0:1024], g["w_dt"][:, q * 1024:(q + 1) * 1024],
                       wait=[wre, wlast], sig="dW")
            eC = p.step("vector", lambda e, q=q: e.tensor_copy(
                out=wdtb[:, q * 1024:(q + 1) * 1024], in_=wprep[0:8, 0:1024]),
                wait=[eL], sig="dve")
            wlast = eC
            sl_wp3.done(jj, eC)
        evPrep = wlast
        for kk in range(16):
            p.dma("sync", dtmn[:, kk * 8:(kk + 1) * 8], g["dtmean"][kk * 128:(kk + 1) * 128, :],
                  sig="dK")
        evDtmn = p.ev("dK")
        eAl = p.dma("sync", smal2[0:8, 0:1], g["A_log"][:, :], sig="dK")
        eA1 = p.step("scalar", lambda e: e.activation(
            out=smal2[0:8, 1:2], in_=smal2[0:8, 0:1], func=AF.Exp), wait=[eAl], sig="act")
        eA2 = p.step("vector", lambda e: e.tensor_scalar(
            out=bias[0:8, 32:33], in0=smal2[0:8, 1:2], scalar1=-1.0, scalar2=None,
            op0=ALU.mult), wait=[eA1], sig="dve")
        eZc = p.step("vector", lambda e: e.tensor_scalar(
            out=bias[0:8, 33:34], in0=bias[0:8, 32:33], scalar1=0.0, scalar2=None,
            op0=ALU.mult), wait=[eA2], sig="dve")
        eO8 = p.step("vector", lambda e: e.tensor_scalar(
            out=ones8[:, :], in0=ones8[:, :], scalar1=0.0, scalar2=1.0,
            op0=ALU.mult, op1=ALU.add), wait=[], sig="dve") if False else None
        eO8 = p.step("gpsimd", lambda e: e.memset(ones8[:, :], 1.0), sig="gps")
        eDl = p.dma("sync", smal2[0:1, 8:16], g["Dp_row"][:, :], wait=[eA1], sig="dK")
        eDb = p.step("tensor", lambda e: e.matmul(
            PA[0][:, 0:8], ONESF[0:1, 0:128].transpose([1, 0]) if False else ONESF[0:1, :],
            smal2[0:1, 8:16], start=True, stop=True), wait=[eDl, evC], sig="pe") if False else None
        eDb = p.step("tensor", lambda e: e.matmul(
            PA[0][:, 0:8], ONESF[0:1, :], smal2[0:1, 8:16], start=True, stop=True),
            wait=[eDl, evC], sig="pe")
        eDbc = p.step("vector", lambda e: e.tensor_copy(
            out=smal2[0:24, 16:24] if False else stash[:, 8:16], in_=PA[0][:, 0:8]),
            wait=[eDb], sig="dve")
        evDeye = []
        for h in range(8):
            evDeye.append(p.step("vector", lambda e, h=h: e.tensor_scalar(
                out=deye[:, h * 128:(h + 1) * 128], in0=EYE[:, :],
                scalar1=stash[:, 8 + h:9 + h], scalar2=None, op0=ALU.mult),
                wait=[eDbc, evC], sig="dve"))
        evDeyeAll = evDeye[-1]

        sl_pa3.busy[0] = eDbc
        IMX = list(range(16, 32))
        IMB = list(range(32, 36))
        evStashInit = None
        for ii, m in enumerate(IMX):
            ja, wpa = sl_pa3.get()
            ePan = pipeW.load_group(
                [(cvo(wpanA[ja], 8, 128), cvi(g["winbf"], 0, 8, m * 128, 128))],
                wait=[wpa, evPrep])
            mmv = None
            for kk in range(8):
                mmv = p.step("tensor", lambda e, kk=kk, ja=ja: e.matmul(
                    PA[ja][:, 0:3], wpanA[ja][:, kk * 128:(kk + 1) * 128],
                    h1T[:, kk * (TPC + 3): kk * (TPC + 3) + 3],
                    start=(kk == 0), stop=(kk == 7)),
                    wait=[ePan, wpa] if kk == 0 else [], sig="pe" if kk == 7 else None)
            ev = p.step("vector", lambda e, ii=ii, ja=ja: e.tensor_copy(
                out=stash[:, 16 + ii * 2:16 + ii * 2 + 2][:, 0:2] if False
                else stash[:, 0:3], in_=PA[ja][:, 0:3]), wait=[mmv], sig="dve") if False else None
            ev = p.step("vector", lambda e, ii=ii, ja=ja: e.tensor_copy(
                out=stash[:, ii * 3:ii * 3 + 3], in_=PA[ja][:, 0:3]), wait=[mmv], sig="dve")
            sl_pa3.done(ja, ev)
            evStashInit = ev

        evHall0 = p.step("gpsimd", lambda e: e.memset(hall[:, :], 0.0), sig="gps")
        ev_xtok_dma = None
        ev_dtraw = None
        ev_scan_prev = None
        ev_lambc = [None] * 4
        ev_eT = [None] * 16
        ev_BT = [None] * 4
        evStash = [evStashInit] * 16
        hall_upd = [evHall0] * 8
        for s in range(4):
            cb3 = s * 512 + 3
            ev_xt_copies = []
            for ii, m in enumerate(IMX):
                ja, wpa = sl_pa3.get()
                ePan = pipeW.load_group(
                    [(cvo(wpanA[ja % 2], 8, 128), cvi(g["winbf"], 0, 8, m * 128, 128))],
                    wait=[wpa, evPrep])
                mmv = None
                for kk in range(8):
                    mmv = p.step("tensor", lambda e, kk=kk, ja=ja, cb3=cb3: e.matmul(
                        PA[ja][:, :], wpanA[ja % 2][:, kk * 128:(kk + 1) * 128],
                        h1T[:, kk * (TPC + 3) + cb3: kk * (TPC + 3) + cb3 + 512],
                        start=(kk == 0), stop=(kk == 7)),
                        wait=[ePan, wpa] if kk == 0 else [], sig="pe" if kk == 7 else None)
                jx, wxin = sl_xin.get()
                evXin3 = p.step("vector", lambda e, ii=ii, jx=jx: e.tensor_copy(
                    out=xin[jx][:, 0:3], in_=stash[:, ii * 3:ii * 3 + 3]),
                    wait=[evStash[ii], wxin], sig="dve")
                evXin = p.step("scalar", lambda e, jx=jx, ja=ja: e.activation(
                    out=xin[jx][:, 3:515], in_=PA[ja][:, :], func=AF.Copy),
                    wait=[mmv, wxin], sig="act")
                sl_pa3.done(ja, evXin)
                evStash[ii] = p.step("vector", lambda e, ii=ii, jx=jx: e.tensor_copy(
                    out=stash[:, ii * 3:ii * 3 + 3], in_=xin[jx][:, 512:515]),
                    wait=[evXin, evXin3], sig="dve")
                jc, wc = sl_cacc.get()
                jc2, wc2 = sl_cacc.get()
                a0 = p.step("vector", lambda e, jx=jx, jc=jc, ii=ii: e.tensor_scalar(
                    out=cacc[jc][:, :], in0=xin[jx][:, 0:512],
                    scalar1=convw[:, ii:ii + 1], scalar2=None, op0=ALU.mult),
                    wait=[evXin, evXin3, wc, evPrep], sig="dve")
                a1 = p.step("vector", lambda e, jx=jx, jc=jc, jc2=jc2, ii=ii: e.scalar_tensor_tensor(
                    out=cacc[jc2][:, :], in0=xin[jx][:, 1:513], scalar=convw[:, 16 + ii:17 + ii],
                    in1=cacc[jc][:, :], op0=ALU.mult, op1=ALU.add),
                    wait=[a0, wc2], sig="dve")
                a2 = p.step("vector", lambda e, jx=jx, jc=jc, jc2=jc2, ii=ii: e.scalar_tensor_tensor(
                    out=cacc[jc][:, :], in0=xin[jx][:, 2:514], scalar=convw[:, 32 + ii:33 + ii],
                    in1=cacc[jc2][:, :], op0=ALU.mult, op1=ALU.add),
                    wait=[a1], sig="dve")
                a3 = p.step("vector", lambda e, jx=jx, jc=jc, jc2=jc2, ii=ii: e.scalar_tensor_tensor(
                    out=cacc[jc2][:, :], in0=xin[jx][:, 3:515], scalar=convw[:, 48 + ii:49 + ii],
                    in1=cacc[jc][:, :], op0=ALU.mult, op1=ALU.add),
                    wait=[a2], sig="dve")
                sl_xin.done(jx, a3)
                sl_cacc.done(jc, a3)
                jxc, wxc = sl_xcb.get()
                exc = p.step("scalar", lambda e, jc2=jc2, jxc=jxc, ii=ii: e.activation(
                    out=xcb[jxc][:, :], in_=cacc[jc2][:, :], func=AF.Silu,
                    bias=bias[:, 16 + ii:17 + ii], scale=1.0),
                    wait=[a3, wxc, evC], sig="act")
                sl_cacc.done(jc2, exc)
                lastT = None
                for tt in range(4):
                    jt3, wt3 = sl_pt3.get()
                    eT = p.step("tensor", lambda e, tt=tt, jt3=jt3, jxc=jxc: e.transpose(
                        PTs[jt3][:, 0:128], xcb[jxc][:, tt * 128:(tt + 1) * 128], EYE[:, :]),
                        wait=[exc, wt3, evC], sig="pe")
                    ecp = p.step("vector", lambda e, tt=tt, jt3=jt3, ii=ii: e.tensor_copy(
                        out=xtok[:, tt * 2048 + ii * 128: tt * 2048 + (ii + 1) * 128],
                        in_=PTs[jt3][:, 0:128]),
                        wait=[eT, ev_xtok_dma], sig="dve")
                    sl_pt3.done(jt3, ecp)
                    ev_xt_copies.append(ecp)
                    lastT = eT
                sl_xcb.done(jxc, lastT)
            ev_xtok_ready = ev_xt_copies[-1]
            for ib, m in enumerate(IMB):
                ja, wpa = sl_pa3.get()
                ePan = pipeW.load_group(
                    [(cvo(wpanA[ja % 2], 8, 128), cvi(g["winbf"], 0, 8, m * 128, 128))],
                    wait=[wpa, evPrep])
                mmv = None
                for kk in range(8):
                    mmv = p.step("tensor", lambda e, kk=kk, ja=ja, cb3=cb3: e.matmul(
                        PA[ja][:, :], wpanA[ja % 2][:, kk * 128:(kk + 1) * 128],
                        h1T[:, kk * (TPC + 3) + cb3: kk * (TPC + 3) + cb3 + 512],
                        start=(kk == 0), stop=(kk == 7)),
                        wait=[ePan, wpa] if kk == 0 else [], sig="pe" if kk == 7 else None)
                evB = p.step("scalar", lambda e, ib=ib, ja=ja, s=s: e.activation(
                    out=BT[:, ib * TPC + s * 512: ib * TPC + (s + 1) * 512],
                    in_=PA[ja][:, :], func=AF.Copy),
                    wait=[mmv] + ([ev_BT[ib]] if ev_BT[ib] else []), sig="act")
                sl_pa3.done(ja, evB)
                ev_BT[ib] = evB
            ja, wpa = sl_pa3.get()
            ePan = pipeW.load_group(
                [(cvo(wpanA[ja % 2], 8, 8, c0=960), cvi(g["winbf"], 0, 8, 5120, 8))],
                wait=[wpa, evPrep])
            mmv = None
            for kk in range(8):
                mmv = p.step("tensor", lambda e, kk=kk, ja=ja, cb3=cb3: e.matmul(
                    PA[ja][0:8, :], wpanA[ja % 2][:, kk * 8 + 960: kk * 8 + 968],
                    h1T[:, kk * (TPC + 3) + cb3: kk * (TPC + 3) + cb3 + 512],
                    start=(kk == 0), stop=(kk == 7)),
                    wait=[ePan, wpa] if kk == 0 else [], sig="pe" if kk == 7 else None)
            evDtraw = p.step("scalar", lambda e, ja=ja: e.activation(
                out=dtraw[:, :], in_=PA[ja][0:8, :], func=AF.Copy),
                wait=[mmv, ev_dtraw], sig="act")
            sl_pa3.done(ja, evDtraw)
            lastmean = None
            for m2 in range(16):
                ja2, wpa2 = sl_pa3.get()
                mmd = p.step("tensor", lambda e, m2=m2, ja2=ja2: e.matmul(
                    PA[ja2][:, :], wdtb[:, m2 * 128:(m2 + 1) * 128], dtraw[:, :],
                    start=True, stop=True),
                    wait=[evDtraw, wpa2, evPrep], sig="pe")
                je, wex = sl_expd.get()
                eEx = p.step("scalar", lambda e, m2=m2, ja2=ja2, je=je: e.activation(
                    out=expd[je][:, :], in_=PA[ja2][:, :], func=AF.Exp,
                    bias=bias[:, m2:m2 + 1], scale=1.0),
                    wait=[mmd, wex, evC], sig="act")
                sl_pa3.done(ja2, eEx)
                jd, wdt_ = sl_dtb.get()
                eLn = p.step("scalar", lambda e, je=je, jd=jd: e.activation(
                    out=dtb[jd][:, :], in_=expd[je][:, :], func=AF.Ln, bias=1.0),
                    wait=[eEx, wdt_], sig="act")
                sl_expd.done(je, eLn)
                lastmean = p.step("tensor", lambda e, m2=m2, jd=jd: e.matmul(
                    PW[3][0:8, :], dtmn[:, m2 * 8:(m2 + 1) * 8], dtb[jd][:, :],
                    start=(m2 == 0), stop=(m2 == 15)),
                    wait=[eLn, p.ev("dK")] + ([pw_free3[3]] if m2 == 0 else []),
                    sig="pe" if m2 == 15 else None)
                sl_dtb.done(jd, lastmean if m2 == 15 else p.ev("pe"))
            jw, wdtw = sl_dtw.get()
            evdtA = p.step("vector", lambda e, jw=jw: e.tensor_scalar(
                out=dtw[jw][:, :], in0=PW[3][0:8, :], scalar1=bias[0:8, 32:33],
                scalar2=None, op0=ALU.mult),
                wait=[lastmean, wdtw, eA2], sig="dve")
            pw_free3[3] = evdtA
            evScan = p.step("vector", lambda e, s=s, jw=jw: e.tensor_tensor_scan(
                out=lcsg[:, s * 512:(s + 1) * 512], data0=ones8[:, :], data1=dtw[jw][:, :],
                initial=(0.0 if s == 0 else lcsg[0:8, s * 512 - 1:s * 512]),
                op0=ALU.mult, op1=ALU.add),
                wait=[evdtA, eO8] + ([ev_scan_prev] if ev_scan_prev else []), sig="dve")
            ev_scan_prev = evScan
            sl_dtw.done(jw, evScan)
            for tt in range(4):
                c = s * 4 + tt
                je2, wer = sl_eraw.get()
                jw2, wdtw2 = sl_dtw.get()
                base = bias[0:8, 33:34] if c == 0 else lcsg[0:8, c * 128 - 1:c * 128]
                eD1 = p.step("vector", lambda e, c=c, jw2=jw2, base=base: e.tensor_scalar(
                    out=dtw[jw2][:, 0:128], in0=lcsg[:, c * 128:(c + 1) * 128],
                    scalar1=base, scalar2=None, op0=ALU.subtract),
                    wait=[evScan, wdtw2, eZc], sig="dve")
                eD2 = p.step("vector", lambda e, c=c, jw2=jw2: e.tensor_scalar(
                    out=dtw[jw2][:, 128:256], in0=lcsg[:, c * 128:(c + 1) * 128],
                    scalar1=lcsg[0:8, (c + 1) * 128 - 1:(c + 1) * 128], scalar2=None,
                    op0=ALU.subtract),
                    wait=[evScan, eD1], sig="dve")
                eE1 = p.step("scalar", lambda e, je2=je2, jw2=jw2: e.activation(
                    out=eraw[je2][:, 0:128], in_=dtw[jw2][:, 0:128], func=AF.Exp),
                    wait=[eD1, wer], sig="act")
                eE3 = p.step("scalar", lambda e, je2=je2, jw2=jw2: e.activation(
                    out=eraw[je2][:, 128:256], in_=dtw[jw2][:, 0:128], func=AF.Exp, scale=-1.0),
                    wait=[eD1], sig="act")
                eE2 = p.step("scalar", lambda e, je2=je2, jw2=jw2: e.activation(
                    out=eraw[je2][:, 256:384], in_=dtw[jw2][:, 128:256], func=AF.Exp, scale=-1.0),
                    wait=[eD2], sig="act")
                sl_dtw.done(jw2, eE2)
                ja3, wpa3 = sl_pa3.get()
                eTr = None
                for gci, gev in ((0, eE1), (1, eE3), (2, eE2)):
                    eTr = p.step("tensor", lambda e, je2=je2, ja3=ja3, gci=gci: e.transpose(
                        PA[ja3][:, gci * 8:(gci + 1) * 8], eraw[je2][:, gci * 128:(gci + 1) * 128],
                        EYEF[0:8, 0:8]),
                        wait=[gev, evC] + ([wpa3] if gci == 0 else []),
                        sig="pe" if gci == 2 else None)
                sl_eraw.done(je2, eTr)
                ecp = p.step("vector", lambda e, c=c, ja3=ja3: e.tensor_copy(
                    out=eTall[:, c * 24:(c + 1) * 24], in_=PA[ja3][:, 0:24]),
                    wait=[eTr] + ([ev_eT[c]] if ev_eT[c] else []), sig="dve")
                sl_pa3.done(ja3, ecp)
                ev_eT[c] = ecp
            ja4, wpa4 = sl_pa3.get()
            evLrow = p.dma("gpsimd", crow[0:1, 0:96], eTall[127:128, s * 96:(s + 1) * 96],
                           wait=[ev_eT[s * 4], ev_eT[s * 4 + 1], ev_eT[s * 4 + 2],
                                 ev_eT[s * 4 + 3]], sig="dS")
            eLb = p.step("tensor", lambda e, s=s, ja4=ja4: e.matmul(
                PA[ja4][0:64, 0:96], ONESF[0:1, 0:64], crow[0:1, 0:96],
                start=True, stop=True),
                wait=[evLrow, wpa4, evC], sig="pe")
            eLbc = p.step("vector", lambda e, s=s, ja4=ja4: e.tensor_copy(
                out=lambc[:, s * 96:(s + 1) * 96], in_=PA[ja4][0:64, 0:96]),
                wait=[eLb] + ([ev_lambc[s]] if ev_lambc[s] else []), sig="dve")
            sl_pa3.done(ja4, eLbc)
            ev_lambc[s] = eLbc
            for tt in range(4):
                c = s * 4 + tt
                for h in range(8):
                    tb, ro = h // 2, (h % 2) * 64
                    jt3, wt3 = sl_pt3.get()
                    eTb = p.step("tensor", lambda e, tb=tb, ro=ro, s=s, tt=tt, jt3=jt3: e.transpose(
                        PTs[jt3][:, 0:64],
                        BT[ro:ro + 64, tb * TPC + s * 512 + tt * 128: tb * TPC + s * 512 + (tt + 1) * 128],
                        EYE[ro:ro + 64, ro:ro + 64]),
                        wait=[ev_BT[tb], wt3, evC], sig="pe")
                    jb, wb = sl_bs.get()
                    eBs = p.step("scalar", lambda e, jt3=jt3, jb=jb, c=c, h=h: e.activation(
                        out=Bs[jb][:, :], in_=PTs[jt3][:, 0:64], func=AF.Copy,
                        scale=eTall[:, c * 24 + 16 + h:c * 24 + 17 + h]),
                        wait=[eTb, wb, ev_eT[c]], sig="act")
                    sl_pt3.done(jt3, eBs)
                    mmD = p.step("tensor", lambda e, jb=jb, tt=tt, h=h: e.matmul(
                        PW[h // 2][0:64, (h % 2) * 256:((h % 2) + 1) * 256],
                        Bs[jb][:, :], xtok[:, tt * 2048 + h * 256: tt * 2048 + (h + 1) * 256],
                        start=True, stop=True),
                        wait=[eBs, ev_xtok_ready, pw_free3[h // 2], hall_upd[(h // 2) * 2],
                              hall_upd[(h // 2) * 2 + 1]],
                        sig="pe")
                    sl_bs.done(jb, mmD)
                    upd = p.step("vector", lambda e, h=h, c=c: e.scalar_tensor_tensor(
                        out=hall[:, h * 256:(h + 1) * 256], in0=hall[:, h * 256:(h + 1) * 256],
                        scalar=lambc[0:64, c * 24 + h:c * 24 + h + 1],
                        in1=PW[h // 2][0:64, (h % 2) * 256:((h % 2) + 1) * 256],
                        op0=ALU.mult, op1=ALU.add),
                        wait=[mmD, ev_lambc[s], hall_upd[h]], sig="dve")
                    hall_upd[h] = upd
                    pw_free3[h // 2] = upd
            prevdma = ev_xtok_dma
            ev_xtok_dma = p.dma("sync",
                cvrows(g["xtok_park"], s * 512, 4, 0, DI),
                cvcols(xtok, 4, DI),
                wait=[hall_upd[7], ev_xtok_ready], sig="dV")

        if stage >= 4:
            eHb = p.step("vector", lambda e: e.tensor_copy(out=hbf[0:64, :], in_=hall[:, :]),
                         wait=hall_upd, sig="dve")
            eLs1 = p.step("scalar", lambda e: e.activation(
                out=smal2[0:8, 2:3], in_=lcsg[0:8, TPC - 1:TPC], func=AF.Exp),
                wait=[ev_scan_prev], sig="act")
            ja5, wpa5 = sl_pa3.get()
            eTl = p.step("tensor", lambda e, ja5=ja5: e.transpose(
                PA[ja5][0:1, 0:8], smal2[0:8, 2:3], EYEF[0:8, 0:8]),
                wait=[eLs1, wpa5, evC], sig="pe")
            eLrow2 = p.step("vector", lambda e, ja5=ja5: e.tensor_copy(
                out=lgb[0:1, 0:8], in_=PA[ja5][0:1, 0:8]), wait=[eTl], sig="dve")
            sl_pa3.done(ja5, eLrow2)
            eS1 = p.dma("gpsimd", g["ccA_in"][0:64, :], hbf[0:64, :], wait=[eHb], sig="dS")
            eZr = p.step("gpsimd", lambda e: e.memset(hsb[0:1, :], 0.0), sig="gps")
            eZr2 = p.step("vector", lambda e: e.tensor_copy(
                out=hsb[0:1, 0:8], in_=lgb[0:1, 0:8]), wait=[eZr, eLrow2], sig="dve")
            eS2 = p.dma("gpsimd", g["ccA_in"][64:65, :], hsb[0:1, :], wait=[eZr2], sig="dS")
            evCC = p.step("gpsimd", lambda e: e.collective_compute(
                "AllGather", ALU.bypass, replica_groups=[list(range(8))],
                ins=[g["ccA_in"].opt()], outs=[g["ccA_out"].opt()]),
                wait=[eS1, eS2], sig="cc")
            def emit_compose():
                eM1 = p.dma("sync", pmT[:, :], g["ssd_premaskT"][:, :], wait=[evCC], sig="dM")
                eM2 = p.dma("sync", pmiT[:, :], g["ssd_maskinvT"][:, :], wait=[eM1], sig="dM")
                eLg = None
                for sdr in range(8):
                    eLg = p.dma("sync", lgb[sdr:sdr + 1, 0:8],
                                g["ccA_out"][sdr * 65 + 64:sdr * 65 + 65, 0:8],
                                wait=[eM2 if sdr == 0 else eLg], sig="dM")
                eLgf = p.step("vector", lambda e: e.tensor_copy(out=lgf[:, :], in_=lgb[:, :]),
                              wait=[eLg], sig="dve")
                ja6, wpa6 = sl_pa3.get()
                eTg = p.step("tensor", lambda e, ja6=ja6: e.transpose(
                    PA[ja6][0:8, 0:8], lgf[:, :], EYEF[0:8, 0:8]),
                    wait=[eLgf, wpa6], sig="pe")
                eLm = p.step("vector", lambda e, ja6=ja6: e.tensor_copy(
                    out=lmT[:, :], in_=PA[ja6][0:8, 0:8]), wait=[eTg], sig="dve")
                sl_pa3.done(ja6, eLm)
                eLp1 = p.step("vector", lambda e: e.tensor_tensor(
                    out=lmT[:, :], in0=lmT[:, :], in1=pmT[:, :], op=ALU.mult),
                    wait=[eLm, p.ev("dM")], sig="dve")
                eLp2 = p.step("vector", lambda e: e.tensor_tensor(
                    out=lmT[:, :], in0=lmT[:, :], in1=pmiT[:, :], op=ALU.add),
                    wait=[eLp1], sig="dve")
                evR0 = p.step("vector", lambda e: e.tensor_tensor(
                    out=lgf[:, :], in0=pmT[:, :], in1=pmiT[:, :], op=ALU.add),
                    wait=[eLp2, eLgf], sig="dve")
                evSuf = evR0
                for jj in range(1, 8):
                    evSuf = p.step("vector", lambda e, jj=jj: e.tensor_scalar(
                        out=lgf[:, 0:jj], in0=lgf[:, 0:jj], scalar1=lmT[0:8, jj:jj + 1],
                        scalar2=None, op0=ALU.mult), wait=[evSuf, eLp2], sig="dve")
                eCoef = p.step("vector", lambda e: e.tensor_tensor(
                    out=lgf[:, :], in0=lgf[:, :], in1=pmT[:, :], op=ALU.mult),
                    wait=[evSuf], sig="dve")
                eFlat = p.dma("gpsimd", crow[0:1, 0:64], lgf[:, :], wait=[eCoef], sig="dS")
                ja7, wpa7 = sl_pa3.get()
                eRb = p.step("tensor", lambda e, ja7=ja7: e.matmul(
                    PA[ja7][0:64, 0:64], ONESF[0:1, 0:64], crow[0:1, 0:64],
                    start=True, stop=True), wait=[eFlat, wpa7], sig="pe")
                eCb = p.step("vector", lambda e, ja7=ja7: e.tensor_copy(
                    out=coefb[:, :], in_=PA[ja7][0:64, 0:64]), wait=[eRb], sig="dve")
                sl_pa3.done(ja7, eCb)
                evHs = None
                evCompose = eHb
                for sdr in range(8):
                    evHs = p.dma("sync", hsb[:, :], g["ccA_out"][sdr * 65:sdr * 65 + 64, :],
                                 wait=[evCompose, evHs], sig="dR")
                    for h in range(8):
                        if sdr == 0:
                            evCompose = p.step("vector", lambda e, h=h, sdr=sdr: e.tensor_scalar(
                                out=hall[:, h * 256:(h + 1) * 256],
                                in0=hsb[:, h * 256:(h + 1) * 256],
                                scalar1=coefb[0:64, h * 8 + sdr:h * 8 + sdr + 1], scalar2=None,
                                op0=ALU.mult),
                                wait=[evHs, eCb, eHb], sig="dve")
                        else:
                            evCompose = p.step("vector", lambda e, h=h, sdr=sdr: e.scalar_tensor_tensor(
                                out=hall[:, h * 256:(h + 1) * 256],
                                in0=hsb[:, h * 256:(h + 1) * 256],
                                scalar=coefb[0:64, h * 8 + sdr:h * 8 + sdr + 1],
                                in1=hall[:, h * 256:(h + 1) * 256],
                                op0=ALU.mult, op1=ALU.add),
                                wait=[evHs, eCb, evCompose], sig="dve")
                evHin = evCompose
                eHb2 = p.step("vector", lambda e: e.tensor_copy(out=hbf[0:64, :], in_=hall[:, :]),
                              wait=[evHin], sig="dve")
                eHb2 = p.dma("gpsimd", hbf[64:128, :], hbf[0:64, :], wait=[eHb2], sig="dS")
                return eHb2
            if stage == 4:
                eHb2 = emit_compose()

        if stage >= 5:
            a3a.free()
            a3b = nc.auto_arena()
            p.arenas.append(a3b)
            xtokB0 = a3b.sbuf_tensor("xtokB0", [128, 2048], BF16)
            xtokB = [xtokB0, xtokB0]
            sztok = a3b.sbuf_tensor("sztok", [128, 4 * 2048], BF16)
            CTs = a3b.sbuf_tensor("CTs", [128, 4 * 512], BF16)
            dstb = [a3b.sbuf_tensor(f"dstb{j}", [128, 128], BF16) for j in range(2)]
            yrow = a3b.sbuf_tensor("yrow", [128, 2048], F32)
            ynb = a3b.sbuf_tensor("ynb", [128, 2048], BF16)
            ynT = a3b.sbuf_tensor("ynT", [128, 16 * 256], BF16)
            wko0 = a3b.sbuf_tensor("wko0", [128, 1024], BF16)
            wko = [wko0, wko0]
            wpanB = [a3b.sbuf_tensor(f"wpanB{j}", [128, 1024], BF16) for j in range(2)]
            zsil0 = a3b.sbuf_tensor("zsil0", [128, 512], BF16)
            zsil = [zsil0, zsil0]
            trilsb = a3b.sbuf_tensor("trilsb", [128, 128], F32)
            a3b.allocate()
            pipeX = DmaPipe(p, "dXt", n=2, engine="scalar")
            pipeZ = DmaPipe(p, "dZz", n=2, engine="scalar")
            pipeO = DmaPipe(p, "dWo", n=2)
            sl_xB = Slots(1)
            sl_zs = Slots(1)
            sl_dst = Slots(2)
            sl_wko = Slots(1)
            sl_wpB = Slots(2)
            eTril = p.dma("sync", trilsb[:, :], g["trilT"][:, :],
                          wait=[p.ev("dK"), p.ev("dV")], sig="dK")
            ev_yrow = None
            ev_ynb = None
            ev_ynT = [None, None]
            ev_sz = None
            ev_CT = None
            ev_hbf = None
            ev_Rw = p.ev("dve")
            for s in range(4):
                cb3 = s * 512 + 3
                ev_sz_copies = []
                for ii in range(16):
                    jw2, wpB = sl_wpB.get()
                    ePan = pipeZ.load_group(
                        [(cvo(wpanB[jw2], 8, 128), cvi(g["winbf"], 0, 8, ii * 128, 128))],
                        wait=[wpB, evPrep] + ([p.ev("dV")] if (s == 0 and ii < 2) else []))
                    ja, wpa = sl_pa3.get()
                    mmv = None
                    for kk in range(8):
                        mmv = p.step("tensor", lambda e, kk=kk, ja=ja, cb3=cb3, jw2=jw2: e.matmul(
                            PA[ja][:, :], wpanB[jw2][:, kk * 128:(kk + 1) * 128],
                            h1T[:, kk * (TPC + 3) + cb3: kk * (TPC + 3) + cb3 + 512],
                            start=(kk == 0), stop=(kk == 7)),
                            wait=[ePan, wpa] if kk == 0 else [], sig="pe" if kk == 7 else None)
                    sl_wpB.done(jw2, mmv)
                    jz, wz = sl_zs.get()
                    ez = p.step("scalar", lambda e, ja=ja, jz=jz: e.activation(
                        out=zsil[jz][:, :], in_=PA[ja][:, :], func=AF.Silu),
                        wait=[mmv, wz], sig="act")
                    sl_pa3.done(ja, ez)
                    lastT = None
                    for tt in range(4):
                        jt3, wt3 = sl_pt3.get()
                        eT = p.step("tensor", lambda e, tt=tt, jt3=jt3, jz=jz: e.transpose(
                            PTs[jt3][:, 0:128], zsil[jz][:, tt * 128:(tt + 1) * 128], EYE[:, :]),
                            wait=[ez, wt3], sig="pe")
                        ecp = p.step("vector", lambda e, tt=tt, jt3=jt3, ii=ii: e.tensor_copy(
                            out=sztok[:, tt * 2048 + ii * 128: tt * 2048 + (ii + 1) * 128],
                            in_=PTs[jt3][:, 0:128]),
                            wait=[eT, ev_sz], sig="dve")
                        sl_pt3.done(jt3, ecp)
                        ev_sz_copies.append(ecp)
                        lastT = eT
                    sl_zs.done(jz, lastT)
                ev_sz = ev_sz_copies[-1]
                evCT_new = None
                for ic in range(4):
                    m = 36 + ic
                    jw2, wpB = sl_wpB.get()
                    ePan = pipeZ.load_group(
                        [(cvo(wpanB[jw2], 8, 128), cvi(g["winbf"], 0, 8, m * 128, 128))],
                        wait=[wpB, evPrep])
                    ja, wpa = sl_pa3.get()
                    mmv = None
                    for kk in range(8):
                        mmv = p.step("tensor", lambda e, kk=kk, ja=ja, cb3=cb3, jw2=jw2: e.matmul(
                            PA[ja][:, :], wpanB[jw2][:, kk * 128:(kk + 1) * 128],
                            h1T[:, kk * (TPC + 3) + cb3: kk * (TPC + 3) + cb3 + 512],
                            start=(kk == 0), stop=(kk == 7)),
                            wait=[ePan, wpa] if kk == 0 else [], sig="pe" if kk == 7 else None)
                    sl_wpB.done(jw2, mmv)
                    evCT_new = p.step("scalar", lambda e, ja=ja, ic=ic: e.activation(
                        out=CTs[:, ic * 512:(ic + 1) * 512], in_=PA[ja][:, :], func=AF.Copy),
                        wait=[mmv, ev_CT], sig="act")
                    sl_pa3.done(ja, evCT_new)
                ev_CT = evCT_new
                if s == 0:
                    ev_hbf = emit_compose()
                for tt in range(4):
                    c = s * 4 + tt
                    jx, wx = sl_xB.get()
                    evX = pipeX.load(xtokB[jx][:, :],
                                     g["xtok_park"][c * 128:(c + 1) * 128, :],
                                     wait=[wx, p.ev("dV")])
                    ev_ev = [None] * 8
                    for h in range(8):
                        tb, ro = h // 2, (h % 2) * 64
                        ja, wpa = sl_pa3.get()
                        mmG = p.step("tensor", lambda e, ja=ja, tb=tb, ro=ro, c=c, tt=tt: e.matmul(
                            PA[ja][:, 0:128],
                            BT[ro:ro + 64, tb * TPC + c * 128: tb * TPC + (c + 1) * 128],
                            CTs[ro:ro + 64, tb * 512 + tt * 128: tb * 512 + (tt + 1) * 128],
                            start=True, stop=True),
                            wait=[ev_BT[tb], ev_CT, wpa], sig="pe")
                        jd, wd = sl_dst.get()
                        eDst = p.step("vector", lambda e, ja=ja, jd=jd, c=c, h=h: e.scalar_tensor_tensor(
                            out=dstb[jd][:, :], in0=PA[ja][:, 0:128],
                            scalar=eTall[:, c * 24 + 8 + h:c * 24 + 9 + h],
                            in1=trilsb[:, :], op0=ALU.mult, op1=ALU.mult),
                            wait=[mmG, wd, ev_eT[c], p.ev("dK")], sig="dve")
                        sl_pa3.done(ja, eDst)
                        mm1 = p.step("tensor", lambda e, jd=jd, jx=jx, h=h: e.matmul(
                            PW[h // 2][:, (h % 2) * 256:((h % 2) + 1) * 256],
                            dstb[jd][:, :], xtokB[jx][:, h * 256:(h + 1) * 256],
                            start=True, stop=False),
                            wait=[eDst, evX, pw_free3[h // 2]], sig=None)
                        mm2 = p.step("tensor", lambda e, tb=tb, ro=ro, tt=tt, h=h: e.matmul(
                            PW[h // 2][:, (h % 2) * 256:((h % 2) + 1) * 256],
                            CTs[ro:ro + 64, tb * 512 + tt * 128: tb * 512 + (tt + 1) * 128],
                            hbf[ro:ro + 64, h * 256:(h + 1) * 256],
                            start=False, stop=False),
                            wait=[ev_hbf, ev_CT], sig=None)
                        mm3 = p.step("tensor", lambda e, jx=jx, h=h: e.matmul(
                            PW[h // 2][:, (h % 2) * 256:((h % 2) + 1) * 256],
                            deye[:, h * 128:(h + 1) * 128], xtokB[jx][:, h * 256:(h + 1) * 256],
                            start=False, stop=True),
                            wait=[evDeyeAll], sig="pe")
                        sl_dst.done(jd, mm1)
                        sl_xB.done(jx, mm3)
                        ev_ev[h] = mm3
                        eY = p.step("scalar", lambda e, h=h, c=c: e.activation(
                            out=yrow[:, h * 256:(h + 1) * 256],
                            in_=PW[h // 2][:, (h % 2) * 256:((h % 2) + 1) * 256],
                            func=AF.Copy, scale=eTall[:, c * 24 + h:c * 24 + 1 + h]),
                            wait=[mm3, ev_yrow, ev_eT[c]], sig="act")
                        pw_free3[h // 2] = eY
                        ev_ev[h] = eY
                    for h in range(8):
                        tb, ro = h // 2, (h % 2) * 64
                        jt3, wt3 = sl_pt3.get()
                        eTb = p.step("tensor", lambda e, tb=tb, ro=ro, c=c, jt3=jt3: e.transpose(
                            PTs[jt3][:, 0:64],
                            BT[ro:ro + 64, tb * TPC + c * 128: tb * TPC + (c + 1) * 128],
                            EYE[ro:ro + 64, ro:ro + 64]),
                            wait=[ev_BT[tb], wt3], sig="pe")
                        jb, wb = sl_bs.get()
                        eBs = p.step("scalar", lambda e, jt3=jt3, jb=jb, c=c, h=h: e.activation(
                            out=Bs[jb][:, :], in_=PTs[jt3][:, 0:64], func=AF.Copy,
                            scale=eTall[:, c * 24 + 16 + h:c * 24 + 17 + h]),
                            wait=[eTb, wb, ev_eT[c]], sig="act")
                        sl_pt3.done(jt3, eBs)
                        ja, wpa = sl_pa3.get()
                        mmD = p.step("tensor", lambda e, jb=jb, jx=jx, h=h, ja=ja: e.matmul(
                            PA[ja][0:64, 0:256], Bs[jb][:, :],
                            xtokB[jx][:, h * 256:(h + 1) * 256],
                            start=True, stop=True),
                            wait=[eBs, evX, wpa], sig="pe")
                        sl_bs.done(jb, mmD)
                        upd = p.step("vector", lambda e, h=h, c=c, ja=ja: e.scalar_tensor_tensor(
                            out=hall[:, h * 256:(h + 1) * 256], in0=hall[:, h * 256:(h + 1) * 256],
                            scalar=lambc[0:64, c * 24 + h:c * 24 + h + 1],
                            in1=PA[ja][0:64, 0:256],
                            op0=ALU.mult, op1=ALU.add),
                            wait=[mmD, ev_lambc[s], hall_upd[h], ev_hbf], sig="dve")
                        sl_pa3.done(ja, upd)
                        hall_upd[h] = upd
                        last_mmD = mmD
                    sl_xB.done(jx, last_mmD)
                    ev_hbf = p.step("vector", lambda e: e.tensor_copy(
                        out=hbf[0:64, :], in_=hall[:, :]), wait=hall_upd + [ev_hbf], sig="dve")
                    ev_hbf = p.dma("gpsimd", hbf[64:128, :], hbf[0:64, :],
                                   wait=[ev_hbf], sig="dS")
                    eGat = p.step("vector", lambda e, tt=tt: e.tensor_tensor(
                        out=yrow[:, :], in0=yrow[:, :], in1=sztok[:, tt * 2048:(tt + 1) * 2048],
                        op=ALU.mult), wait=ev_ev + [ev_sz], sig="dve")
                    eSq = p.step("scalar", lambda e: e.activation(
                        out=ynb[:, :], in_=yrow[:, :], func=AF.Square,
                        accum_out=small[0:128, 12:13]),
                        wait=[eGat, ev_ynb], sig="act")
                    eRv2 = p.step("vector", lambda e: e.tensor_scalar(
                        out=small[0:128, 13:14], in0=small[0:128, 12:13],
                        scalar1=1.0 / DI, scalar2=EPS0, op0=ALU.mult, op1=ALU.add),
                        wait=[eSq], sig="dve")
                    eSr = p.step("scalar", lambda e: e.activation(
                        out=small[0:128, 14:15], in_=small[0:128, 13:14], func=AF.Sqrt),
                        wait=[eRv2], sig="act")
                    eRc = p.step("vector", lambda e: e.reciprocal(
                        out=small[0:128, 13:14], in_=small[0:128, 14:15]),
                        wait=[eSr], sig="dve")
                    eYn = p.step("scalar", lambda e: e.activation(
                        out=ynb[:, :], in_=yrow[:, :], func=AF.Copy,
                        scale=small[0:128, 13:14]),
                        wait=[eRc, eSq], sig="act")
                    ev_yrow = eYn
                    lastTy = None
                    for ff in range(16):
                        jt3, wt3 = sl_pt3.get()
                        eT = p.step("tensor", lambda e, ff=ff, jt3=jt3: e.transpose(
                            PTs[jt3][:, 0:128], ynb[:, ff * 128:(ff + 1) * 128], EYE[:, :]),
                            wait=[eYn, wt3], sig="pe")
                        ecp = p.step("vector", lambda e, ff=ff, tt=tt, jt3=jt3: e.tensor_copy(
                            out=ynT[:, ff * 256 + (tt % 2) * 128: ff * 256 + (tt % 2) * 128 + 128],
                            in_=PTs[jt3][:, 0:128]),
                            wait=[eT, ev_ynT[tt % 2]], sig="dve")
                        sl_pt3.done(jt3, ecp)
                        lastTy = ecp
                    ev_ynb = lastTy
                    ev_ynT[tt % 2] = lastTy
                    if tt % 2 == 1:
                        ev_mm_out = [None] * 4
                        for k2 in range(16):
                            jko, wko_ = sl_wko.get()
                            eKo = pipeO.load(wko[jko][:, :],
                                             g["woutbf"][k2 * 128:(k2 + 1) * 128, :],
                                             wait=[wko_])
                            for tt2 in range(2):
                                for nh in range(2):
                                    pwidx = tt2 * 2 + nh
                                    ev_mm_out[pwidx] = p.step("tensor",
                                        lambda e, k2=k2, tt2=tt2, nh=nh, jko=jko, pwidx=pwidx: e.matmul(
                                            PW[pwidx][:, :],
                                            ynT[:, k2 * 256 + tt2 * 128: k2 * 256 + tt2 * 128 + 128],
                                            wko[jko][:, nh * 512:(nh + 1) * 512],
                                            start=(k2 == 0), stop=(k2 == 15)),
                                        wait=[eKo, ev_ynT[0], ev_ynT[1]] +
                                             ([pw_free3[pwidx]] if k2 == 0 else []),
                                        sig="pe" if (pwidx == 3 or k2 == 15) else None)
                            sl_wko.done(jko, ev_mm_out[3] or p.ev("pe"))
                        for tt2 in range(2):
                            ti = s * 4 + (tt - 1) + tt2
                            for nh in range(2):
                                pwidx = tt2 * 2 + nh
                                evR = p.step("vector", lambda e, ti=ti, nh=nh, pwidx=pwidx: e.tensor_tensor(
                                    out=R[:, ti * DM + nh * 512: ti * DM + (nh + 1) * 512],
                                    in0=R[:, ti * DM + nh * 512: ti * DM + (nh + 1) * 512],
                                    in1=PW[pwidx][:, :], op=ALU.add),
                                    wait=[ev_mm_out[pwidx], ev_Rw], sig="dve")
                                pw_free3[pwidx] = evR
                                ev_Rw = evR
                        ev_ynT = [ev_Rw, ev_Rw]

        if stage >= 6:
            p.barrier()
            a3b.free()
            a3c.free()
            aH.free()
            a5 = nc.auto_arena()
            p.arenas.append(a5)
            xnT1 = a5.sbuf_tensor("xnT1", [128, 8 * TPC], BF16); g["xnT1"] = xnT1
            xqT = a5.sbuf_tensor("xqT", [128, 8 * TPC], BF16); g["xqT"] = xqT
            stL = a5.sbuf_tensor("stL", [128, 48], F32)
            lnX = a5.sbuf_tensor("lnX", [128, 1024], F32)
            lnS = a5.sbuf_tensor("lnS", [128, 1024], F32)
            lnB = a5.sbuf_tensor("lnB", [128, 1024], BF16)
            lnI = a5.sbuf_tensor("lnI", [128, 1024], I32)
            wst = a5.sbuf_tensor("wst", [128, 1024], F32)
            wstb = [a5.sbuf_tensor(f"wstb{j}", [128, 1024], BF16) for j in range(2)]
            wstf = [wst, a5.sbuf_tensor("wstf1", [128, 1024], F32)]
            wstf3 = a5.sbuf_tensor("wstf3", [128, DFI - 2048], F32)
            lnI2 = a5.sbuf_tensor("lnI2", [128, 1024], I32)
            wstc = a5.sbuf_tensor("wstc", [128, 1024], BF16)
            scl = a5.sbuf_tensor("scl", [128, 64], F32); g["scl"] = scl
            sm1 = a5.sbuf_tensor("sm1", [8, 32], F32)
            a5.allocate()
            evR5 = p.ev("dve")
            evA5 = p.ev("act")
            sl_ln1 = Slots(1)
            sl_wb = Slots(2)
            evMax = None
            for i in range(NTT):
                src_ = R[:, i * DM:(i + 1) * DM]
                w0 = [evR5, evA5] if i == 0 else []
                eS2 = p.step("scalar", lambda e, src_=src_: e.activation(
                    out=lnS[:, :], in_=src_, func=AF.Square, accum_out=SL[0:128, 0:1]),
                    wait=w0 + [p.ev("dve")], sig="act")
                eS1 = p.step("vector", lambda e, src_=src_: e.reduce_sum(
                    out=SL[0:128, 1:2], in_=src_, axis=AX.X), wait=w0, sig="dve")
                eM = p.step("vector", lambda e: e.tensor_scalar(
                    out=SL[0:128, 2:3], in0=SL[0:128, 1:2], scalar1=1.0 / DM, scalar2=None,
                    op0=ALU.mult), wait=[eS1], sig="dve")
                eV1 = p.step("vector", lambda e: e.tensor_scalar(
                    out=SL[0:128, 3:4], in0=SL[0:128, 0:1], scalar1=1.0 / DM, scalar2=1e-5,
                    op0=ALU.mult, op1=ALU.add), wait=[eS2], sig="dve")
                eV2 = p.step("vector", lambda e: e.scalar_tensor_tensor(
                    out=SL[0:128, 4:5], in0=SL[0:128, 2:3], scalar=SL[0:128, 2:3],
                    in1=SL[0:128, 3:4], op0=ALU.mult, op1=ALU.subtract),
                    wait=[eM, eV1], sig="dve")
                eV3 = p.step("vector", lambda e: e.tensor_scalar(
                    out=SL[0:128, 4:5], in0=SL[0:128, 4:5], scalar1=-1.0, scalar2=None,
                    op0=ALU.mult), wait=[eV2], sig="dve")
                eSq = p.step("scalar", lambda e: e.activation(
                    out=SL[0:128, 5:6], in_=SL[0:128, 4:5], func=AF.Sqrt), wait=[eV3], sig="act")
                eRs = p.step("vector", lambda e, i=i: e.reciprocal(
                    out=stL[:, i:i + 1], in_=SL[0:128, 5:6]), wait=[eSq], sig="dve")
                eNb = p.step("vector", lambda e, i=i: e.scalar_tensor_tensor(
                    out=stL[:, 16 + i:17 + i], in0=SL[0:128, 2:3], scalar=-1.0,
                    in1=stL[:, i:i + 1], op0=ALU.mult, op1=ALU.mult),
                    wait=[eRs], sig="dve")
                j1, wl1 = sl_ln1.get()
                eXn = p.step("vector", lambda e, src_=src_, i=i: e.tensor_scalar(
                    out=lnX[:, :], in0=src_, scalar1=stL[:, i:i + 1],
                    scalar2=stL[:, 16 + i:17 + i], op0=ALU.mult, op1=ALU.add),
                    wait=[eNb, wl1], sig="dve")
                eAm = p.step("vector", lambda e: e.reduce_max(
                    out=SL[0:128, 6:7], in_=lnX[:, :], axis=AX.X, apply_absolute_value=True),
                    wait=[eXn], sig="dve")
                if i == 0:
                    evMax = p.step("vector", lambda e: e.tensor_copy(
                        out=SL[0:128, 7:8], in_=SL[0:128, 6:7]), wait=[eAm], sig="dve")
                else:
                    evMax = p.step("vector", lambda e: e.tensor_tensor(
                        out=SL[0:128, 7:8], in0=SL[0:128, 7:8], in1=SL[0:128, 6:7],
                        op=ALU.max), wait=[eAm, evMax], sig="dve")
                eXb = p.step("vector", lambda e: e.tensor_copy(
                    out=lnB[:, :], in_=lnX[:, :]), wait=[eXn], sig="dve")
                lastc = None
                for f in range(8):
                    jt, wt = sl_pt.get()
                    eT = p.step("tensor", lambda e, f=f, jt=jt: e.transpose(
                        PTs[jt][:, 0:128], lnB[:, f * 128:(f + 1) * 128], EYE[:, :]),
                        wait=[eXb, wt], sig="pe")
                    lastc = p.step("vector", lambda e, i=i, f=f, jt=jt: e.tensor_copy(
                        out=xnT1[:, f * TPC + i * 128: f * TPC + (i + 1) * 128],
                        in_=PTs[jt][:, 0:128]), wait=[eT], sig="dve")
                    sl_pt.done(jt, lastc)
                sl_ln1.done(j1, lastc)
            ja, wpa = sl_pa3.get()
            eTm = p.step("tensor", lambda e, ja=ja: e.transpose(
                PA[ja][0:1, 0:128], SL[0:128, 7:8], EYEF[:, :]),
                wait=[evMax, wpa], sig="pe")
            eMx = p.step("vector", lambda e, ja=ja: e.reduce_max(
                out=sm1[0:1, 0:1], in_=PA[ja][0:1, 0:128], axis=AX.X), wait=[eTm], sig="dve")
            sl_pa3.done(ja, eMx)
            eZm = p.step("gpsimd", lambda e: e.memset(sm1[0:1, 1:16], 0.0), sig="gps")
            eSm = p.dma("gpsimd", g["ccM_in"][0:1, 0:16], sm1[0:1, 0:16],
                        wait=[eMx, eZm], sig="dS")
            evCC3 = p.step("gpsimd", lambda e: e.collective_compute(
                "AllReduce", ALU.max, replica_groups=[list(range(8))],
                ins=[g["ccM_in"].opt()], outs=[g["ccM_out"].opt()]),
                wait=[eSm], sig="cc")
            eGm = p.dma("sync", sm1[0:1, 16:32], g["ccM_out"][0:1, 0:16],
                        wait=[evCC3], sig="dK")
            eCx = p.dma("sync", sm1[0:1, 4:5], g["clip_x"][:, :], wait=[eGm], sig="dK")
            eCd = p.dma("sync", sm1[0:1, 5:6], g["clip_down"][:, :], wait=[eCx], sig="dK")
            eSg = p.step("scalar", lambda e: e.activation(
                out=sm1[0:1, 6:7], in_=sm1[0:1, 4:5], func=AF.Sigmoid),
                wait=[eCx, p.ev("dK")], sig="act")
            eSx = p.step("vector", lambda e: e.scalar_tensor_tensor(
                out=sm1[0:1, 2:3], in0=sm1[0:1, 16:17], scalar=1.0 / 127.0,
                in1=sm1[1:2, 1:2].partition_broadcast(1) if False else sm1[0:1, 1:2],
                op0=ALU.mult, op1=ALU.mult), wait=[eSg, eGm], sig="dve") if False else None
            eSx = p.step("vector", lambda e: e.scalar_tensor_tensor(
                out=sm1[0:1, 2:3], in0=sm1[0:1, 16:17], scalar=1.0 / 127.0,
                in1=sm1[0:1, 6:7], op0=ALU.mult, op1=ALU.mult),
                wait=[eSg, p.ev("dK")], sig="dve")
            eIx = p.step("vector", lambda e: e.reciprocal(
                out=sm1[0:1, 3:4], in_=sm1[0:1, 2:3]), wait=[eSx], sig="dve")
            ja, wpa = sl_pa3.get()
            eB1 = p.step("tensor", lambda e, ja=ja: e.matmul(
                PA[ja][:, 0:4], ONESF[0:1, :], sm1[0:1, 0:4], start=True, stop=True),
                wait=[eIx, wpa], sig="pe")
            eB2 = p.step("vector", lambda e, ja=ja: e.tensor_copy(
                out=stL[:, 32:36], in_=PA[ja][:, 0:4]), wait=[eB1], sig="dve")
            sl_pa3.done(ja, eB2)
            evXq = None
            for i in range(NTT):
                src_ = R[:, i * DM:(i + 1) * DM]
                eR2 = p.step("vector", lambda e, i=i: e.tensor_tensor(
                    out=SL[0:128, 8:9], in0=stL[:, i:i + 1], in1=stL[:, 35:36],
                    op=ALU.mult), wait=[eB2] + ([evXq] if evXq else []), sig="dve")
                eN2 = p.step("vector", lambda e, i=i: e.tensor_tensor(
                    out=SL[0:128, 9:10], in0=stL[:, 16 + i:17 + i], in1=stL[:, 35:36],
                    op=ALU.mult), wait=[eR2], sig="dve")
                eW5 = p.step("vector", lambda e, src_=src_: e.tensor_scalar(
                    out=lnX[:, :], in0=src_, scalar1=SL[0:128, 8:9], scalar2=SL[0:128, 9:10],
                    op0=ALU.mult, op1=ALU.add), wait=[eN2], sig="dve")
                eI5 = p.step("vector", lambda e: e.tensor_copy(out=lnI[:, :], in_=lnX[:, :]),
                             wait=[eW5], sig="dve")
                eF5 = p.step("vector", lambda e: e.tensor_copy(out=lnX[:, :], in_=lnI[:, :]),
                             wait=[eI5], sig="dve")
                eC5 = p.step("vector", lambda e: e.tensor_scalar(
                    out=lnB[:, :], in0=lnX[:, :], scalar1=127.0, scalar2=-128.0,
                    op0=ALU.min, op1=ALU.max), wait=[eF5], sig="dve")
                lastc = None
                for f in range(8):
                    jt, wt = sl_pt.get()
                    eT = p.step("tensor", lambda e, f=f, jt=jt: e.transpose(
                        PTs[jt][:, 0:128], lnB[:, f * 128:(f + 1) * 128], EYE[:, :]),
                        wait=[eC5, wt], sig="pe")
                    lastc = p.step("vector", lambda e, i=i, f=f, jt=jt: e.tensor_copy(
                        out=xqT[:, f * TPC + i * 128: f * TPC + (i + 1) * 128],
                        in_=PTs[jt][:, 0:128]), wait=[eT], sig="dve")
                    sl_pt.done(jt, lastc)
                evXq = lastc

        if stage >= 7:
            wsti = lnI
            wsti2 = [lnI, lnI2]
            evb2 = [lnB, wstc]
            sl_wf = Slots(2)
            evSL = [None, None]
            evQb2 = [None, None]
            evSt2 = [None, None]
            pipeQs = DmaPipe(p, "dQs", n=2)
            pipeQl = DmaPipe(p, "dQl", n=2)
            pipeWD = DmaPipe(p, "dWD", n=3)

            def qp2_core(wname, dstname, kt, kwid, eIv, pc, par, extra_wait=()):
                jw = qp2_jw[0]
                eQ = p.step("vector", lambda e, jw=jw, kwid=kwid, pc=pc: e.tensor_scalar(
                    out=wstf[jw][:, 0:kwid], in0=wstf[jw][:, 0:kwid],
                    scalar1=SL[0:128, pc + 1:pc + 2], scalar2=127.0,
                    op0=ALU.mult, op1=ALU.mult), wait=[eIv] + list(extra_wait), sig="dve")
                evSL[par] = eQ
                eQi = p.step("vector", lambda e, jw=jw, par=par, kwid=kwid: e.tensor_copy(
                    out=wsti2[par][:, 0:kwid], in_=wstf[jw][:, 0:kwid]),
                    wait=[eQ, evQb2[par]], sig="dve")
                sl_wf.done(jw, eQi)
                jb, wb = sl_wb.get()
                eQb = p.step("gpsimd", lambda e, jb=jb, par=par, kwid=kwid: e.tensor_copy(
                    out=wstb[jb][:, 0:kwid], in_=wsti2[par][:, 0:kwid]),
                    wait=[eQi, wb], sig="gps")
                evQb2[par] = eQb
                lastC = None
                eT = None
                for fb in range(kwid // 128):
                    jt, wt = sl_pt.get()
                    eT = p.step("tensor", lambda e, jb=jb, fb=fb, jt=jt: e.transpose(
                        PTs[jt][:, 0:128], wstb[jb][:, fb * 128:(fb + 1) * 128], EYE[:, :]),
                        wait=[eQb, wt], sig="pe")
                    eC = p.step("scalar", lambda e, jt=jt, par=par, fb=fb: e.activation(
                        out=evb2[par][:, fb * 128:(fb + 1) * 128], in_=PTs[jt][:, 0:128],
                        func=AF.Copy),
                        wait=[eT] + ([evSt2[par]] if fb == 0 else []), sig="act")
                    sl_pt.done(jt, eC)
                    lastC = eC
                eSt = pipeQs.load(cvi(g[dstname], 0, kwid // 128, kt * 128, 128),
                                  cvo(evb2[par], kwid // 128, 128), wait=[lastC])
                evSt2[par] = eSt
                sl_wb.done(jb, eT)
                return eSt

            qp2_jw = [0]

            def quant_prep2(wname, dstname, sc0, ktiles, kwid, evprev):
                eSt = evprev
                for kt in range(ktiles):
                    par = kt % 2
                    jw, wwf = sl_wf.get()
                    qp2_jw[0] = jw
                    eL = pipeQl.load(wstf[jw][:, 0:kwid],
                                     g[wname][kt * 128:(kt + 1) * 128, 0:kwid],
                                     wait=[wwf] + ([evprev] if kt == 0 else []))
                    pc = 10 + 2 * par
                    eAm = p.step("vector", lambda e, jw=jw, kwid=kwid, pc=pc: e.reduce_max(
                        out=SL[0:128, pc:pc + 1], in_=wstf[jw][:, 0:kwid], axis=AX.X,
                        apply_absolute_value=True), wait=[eL, evSL[par]], sig="dve")
                    eIv = p.step("vector", lambda e, pc=pc: e.reciprocal(
                        out=SL[0:128, pc + 1:pc + 2], in_=SL[0:128, pc:pc + 1]),
                        wait=[eAm], sig="dve")
                    p.step("vector", lambda e, kt=kt, sc0=sc0, pc=pc: e.tensor_scalar(
                        out=scl[:, sc0 + kt:sc0 + kt + 1], in0=SL[0:128, pc:pc + 1],
                        scalar1=1.0 / 127.0, scalar2=None, op0=ALU.mult),
                        wait=[eIv], sig="dve")
                    eSt = qp2_core(wname, dstname, kt, kwid, eIv, pc, par)
                return eSt
            a7 = nc.auto_arena()
            p.arenas.append(a7)
            wupan = [a7.sbuf_tensor(f"wupan{j}", [128, 1024], BF16) for j in range(2)]
            wgpan = [a7.sbuf_tensor(f"wgpan{j}", [128, 1024], BF16) for j in range(2)]
            hbst = [a7.sbuf_tensor(f"hbst{j}", [128, 2048], BF16) for j in range(2)]
            sl_hb = Slots(2)
            pipeHp = DmaPipe(p, "dHp", n=2)
            a7.allocate()
            sl_g = Slots(2)
            sl_u = Slots(2)
            sl_pwU = Slots(2)
            evHm = None
            ev_hpark = None
            sg_prev = None
            evSLg = None
            evSLu = None
            for io in range(22):
                jgp, wgp = sl_g.get()
                jup, wup = sl_u.get()
                egv = [None, None]
                for wi, (wname, pans, jslot, wslot, pcA, sccol) in enumerate((
                        ("w_gate", wgpan, jgp, wgp, 10, io),
                        ("w_up", wupan, jup, wup, 12, 22 + io))):
                    jw, wwf = sl_wf.get()
                    eL = pipeQl.load(wstf[jw][:, :], g[wname][io * 128:(io + 1) * 128, :],
                                     wait=[wwf])
                    prevQ = evSLg if wi == 0 else evSLu
                    eAm = p.step("vector", lambda e, jw=jw, pcA=pcA: e.reduce_max(
                        out=SL[0:128, pcA:pcA + 1], in_=wstf[jw][:, :], axis=AX.X,
                        apply_absolute_value=True), wait=[eL, prevQ], sig="dve")
                    eIv = p.step("vector", lambda e, pcA=pcA: e.reciprocal(
                        out=SL[0:128, pcA + 1:pcA + 2], in_=SL[0:128, pcA:pcA + 1]),
                        wait=[eAm], sig="dve")
                    p.step("vector", lambda e, sccol=sccol, pcA=pcA: e.tensor_scalar(
                        out=scl[:, sccol:sccol + 1], in0=SL[0:128, pcA:pcA + 1],
                        scalar1=1.0 / 127.0, scalar2=None, op0=ALU.mult),
                        wait=[eIv], sig="dve")
                    eQ = p.step("vector", lambda e, jw=jw, pcA=pcA: e.tensor_scalar(
                        out=wstf[jw][:, :], in0=wstf[jw][:, :],
                        scalar1=SL[0:128, pcA + 1:pcA + 2], scalar2=127.0,
                        op0=ALU.mult, op1=ALU.mult), wait=[eIv], sig="dve")
                    if wi == 0:
                        evSLg = eQ
                    else:
                        evSLu = eQ
                    eQi = p.step("vector", lambda e, jw=jw, wi=wi: e.tensor_copy(
                        out=wsti2[wi][:, :], in_=wstf[jw][:, :]),
                        wait=[eQ, evQb2[wi]], sig="dve")
                    sl_wf.done(jw, eQi)
                    jb, wb = sl_wb.get()
                    eQb = p.step("gpsimd", lambda e, jb=jb, wi=wi: e.tensor_copy(
                        out=wstb[jb][:, :], in_=wsti2[wi][:, :]), wait=[eQi, wb], sig="gps")
                    evQb2[wi] = eQb
                    lastT = None
                    ev8 = None
                    for fb in range(8):
                        jt, wt = sl_pt.get()
                        eT = p.step("tensor", lambda e, jb=jb, fb=fb, jt=jt: e.transpose(
                            PTs[jt][:, 0:128], wstb[jb][:, fb * 128:(fb + 1) * 128], EYE[:, :]),
                            wait=[eQb, wt], sig="pe")
                        lastT = eT
                        ev8 = p.step("scalar", lambda e, jt=jt, pans=pans, jslot=jslot,
                                     fb=fb: e.activation(
                            out=pans[jslot][:, fb * 128:(fb + 1) * 128], in_=PTs[jt][:, 0:128],
                            func=AF.Copy),
                            wait=[eT] + ([wslot] if fb == 0 else []), sig="act")
                        sl_pt.done(jt, ev8)
                    egv[wi] = ev8
                    sl_wb.done(jb, lastT)
                ePg, ePu = egv
                eSg2 = p.step("vector", lambda e, io=io: e.tensor_tensor(
                    out=stL[:, 38:39], in0=scl[:, io:io + 1], in1=stL[:, 34:35],
                    op=ALU.mult), wait=[eB2, p.ev("dve"), sg_prev], sig="dve")
                eSu2 = p.step("vector", lambda e, io=io: e.tensor_tensor(
                    out=stL[:, 39:40], in0=scl[:, 22 + io:23 + io], in1=stL[:, 34:35],
                    op=ALU.mult), wait=[eSg2], sig="dve")
                lastg = None
                for nn in range(4):
                    ja, wpa = sl_pa3.get()
                    mmg = None
                    for kk in range(8):
                        mmg = p.step("tensor", lambda e, kk=kk, ja=ja, nn=nn, jgp=jgp: e.matmul(
                            PA[ja][:, :], wgpan[jgp][:, kk * 128:(kk + 1) * 128],
                            xqT[:, kk * TPC + nn * 512: kk * TPC + (nn + 1) * 512],
                            start=(kk == 0), stop=(kk == 7)),
                            wait=[ePg, wpa, evXq] if kk == 0 else [],
                            sig="pe" if kk == 7 else None)
                    ju2, wpw = sl_pwU.get()
                    mmu = None
                    for kk in range(8):
                        mmu = p.step("tensor", lambda e, kk=kk, ju2=ju2, nn=nn, jup=jup: e.matmul(
                            PW[ju2][:, :], wupan[jup][:, kk * 128:(kk + 1) * 128],
                            xqT[:, kk * TPC + nn * 512: kk * TPC + (nn + 1) * 512],
                            start=(kk == 0), stop=(kk == 7)),
                            wait=[ePu, wpw] + ([pw_free3[ju2]] if pw_free3[ju2] else [])
                            if kk == 0 else [],
                            sig="pe" if kk == 7 else None)
                    eGs = p.step("scalar", lambda e, ja=ja, nn=nn: e.activation(
                        out=lnX[:, (nn % 2) * 512:((nn % 2) + 1) * 512], in_=PA[ja][:, :],
                        func=AF.Silu, scale=stL[:, 38:39]),
                        wait=[mmg, eSu2] + ([lastg] if nn >= 2 else []), sig="act")
                    sl_pa3.done(ja, eGs)
                    eUs = p.step("scalar", lambda e, ju2=ju2, nn=nn: e.activation(
                        out=lnS[:, (nn % 2) * 512:((nn % 2) + 1) * 512], in_=PW[ju2][:, :],
                        func=AF.Copy, scale=stL[:, 39:40]),
                        wait=[mmu, eSu2] + ([lastg] if nn >= 2 else []), sig="act")
                    sl_pwU.done(ju2, eUs)
                    pw_free3[ju2] = eUs
                    eH = p.step("vector", lambda e, nn=nn: e.tensor_tensor(
                        out=lnX[:, (nn % 2) * 512:((nn % 2) + 1) * 512],
                        in0=lnX[:, (nn % 2) * 512:((nn % 2) + 1) * 512],
                        in1=lnS[:, (nn % 2) * 512:((nn % 2) + 1) * 512], op=ALU.mult),
                        wait=[eGs, eUs], sig="dve")
                    eAm2 = p.step("vector", lambda e, nn=nn: e.reduce_max(
                        out=stL[:, 40:41], in_=lnX[:, (nn % 2) * 512:((nn % 2) + 1) * 512],
                        axis=AX.X, apply_absolute_value=True), wait=[eH], sig="dve")
                    if evHm is None:
                        evHm = p.step("vector", lambda e: e.tensor_copy(
                            out=stL[:, 41:42], in_=stL[:, 40:41]), wait=[eAm2], sig="dve")
                    else:
                        evHm = p.step("vector", lambda e: e.tensor_tensor(
                            out=stL[:, 41:42], in0=stL[:, 41:42], in1=stL[:, 40:41],
                            op=ALU.max), wait=[eAm2, evHm], sig="dve")
                    if nn == 0:
                        jhb, whb = sl_hb.get()
                    eHb5 = p.step("vector", lambda e, nn=nn, jhb=jhb: e.tensor_copy(
                        out=hbst[jhb][:, nn * 512:(nn + 1) * 512],
                        in_=lnX[:, (nn % 2) * 512:((nn % 2) + 1) * 512]),
                        wait=[eH] + ([whb] if nn == 0 else []), sig="dve")
                    if nn == 3:
                        ev_hpark = pipeHp.load(g["hT_park"][io * 128:(io + 1) * 128, :],
                                               hbst[jhb][:, :], wait=[eHb5])
                        sl_hb.done(jhb, ev_hpark)
                    lastg = eH
                sl_g.done(jgp, mmg)
                sl_u.done(jup, mmu)
                sg_prev = eSu2
            wdbufs = [wstf[0], wstf[1], wstf3]
            wdfree = [sl_wf.busy[0], sl_wf.busy[1], None]
            CHW = [1024, 1024, DFI - 2048]
            evSclK = evSLg
            evMx1K = evSLu
            evMx2K = None
            evqd = None
            for kt in range(8):
                eLs = []
                eAms = []
                for qq in range(3):
                    wid = CHW[qq]
                    eL = pipeWD.load(wdbufs[qq][:, 0:wid],
                                     g["w_down"][kt * 128:(kt + 1) * 128,
                                                 qq * 1024:qq * 1024 + wid],
                                     wait=[wdfree[qq]] + ([evqd] if kt == 0 and qq == 0 else []))
                    pcq = [10, 12, 14][qq]
                    gw = [evSclK, evMx1K, evMx2K][qq]
                    eAm = p.step("vector", lambda e, qq=qq, wid=wid, pcq=pcq: e.reduce_max(
                        out=SL[0:128, pcq:pcq + 1], in_=wdbufs[qq][:, 0:wid], axis=AX.X,
                        apply_absolute_value=True), wait=[eL, gw], sig="dve")
                    eAms.append(eAm)
                eMx1 = p.step("vector", lambda e: e.tensor_tensor(
                    out=SL[0:128, 10:11], in0=SL[0:128, 10:11], in1=SL[0:128, 12:13],
                    op=ALU.max), wait=[eAms[0], eAms[1]], sig="dve")
                evMx1K = eMx1
                eMx2 = p.step("vector", lambda e: e.tensor_tensor(
                    out=SL[0:128, 10:11], in0=SL[0:128, 10:11], in1=SL[0:128, 14:15],
                    op=ALU.max), wait=[eMx1, eAms[2]], sig="dve")
                evMx2K = eMx2
                eIv = p.step("vector", lambda e: e.reciprocal(
                    out=SL[0:128, 11:12], in_=SL[0:128, 10:11]), wait=[eMx2], sig="dve")
                eScl = p.step("vector", lambda e, kt=kt: e.tensor_scalar(
                    out=scl[:, 44 + kt:45 + kt], in0=SL[0:128, 10:11],
                    scalar1=1.0 / 127.0, scalar2=None, op0=ALU.mult), wait=[eIv], sig="dve")
                evSclK = eScl
                for qq in range(3):
                    wid = CHW[qq]
                    par = (kt * 3 + qq) % 2
                    eQ = p.step("vector", lambda e, qq=qq, wid=wid: e.tensor_scalar(
                        out=wdbufs[qq][:, 0:wid], in0=wdbufs[qq][:, 0:wid],
                        scalar1=SL[0:128, 11:12], scalar2=127.0,
                        op0=ALU.mult, op1=ALU.mult), wait=[eScl], sig="dve")
                    eQi = p.step("vector", lambda e, qq=qq, par=par, wid=wid: e.tensor_copy(
                        out=wsti2[par][:, 0:wid], in_=wdbufs[qq][:, 0:wid]),
                        wait=[eQ, evQb2[par]], sig="dve")
                    wdfree[qq] = eQi
                    jb, wb = sl_wb.get()
                    eQb = p.step("gpsimd", lambda e, jb=jb, par=par, wid=wid: e.tensor_copy(
                        out=wstb[jb][:, 0:wid], in_=wsti2[par][:, 0:wid]),
                        wait=[eQi, wb], sig="gps")
                    evQb2[par] = eQb
                    lastC = None
                    eT = None
                    for fb in range(wid // 128):
                        jt, wt = sl_pt.get()
                        eT = p.step("tensor", lambda e, jb=jb, fb=fb, jt=jt: e.transpose(
                            PTs[jt][:, 0:128], wstb[jb][:, fb * 128:(fb + 1) * 128], EYE[:, :]),
                            wait=[eQb, wt], sig="pe")
                        eC = p.step("scalar", lambda e, jt=jt, par=par, fb=fb: e.activation(
                            out=evb2[par][:, fb * 128:(fb + 1) * 128], in_=PTs[jt][:, 0:128],
                            func=AF.Copy),
                            wait=[eT] + ([evSt2[par]] if fb == 0 else []), sig="act")
                        sl_pt.done(jt, eC)
                        lastC = eC
                    eSt = pipeQs.load(cvi(g["wdTint"], qq * 1024, wid // 128, kt * 128, 128),
                                      cvo(evb2[par], wid // 128, 128), wait=[lastC])
                    evSt2[par] = eSt
                    sl_wb.done(jb, eT)
                    evqd = eSt

            ja, wpa = sl_pa3.get()
            eTm2 = p.step("tensor", lambda e, ja=ja: e.transpose(
                PA[ja][0:1, 0:128], stL[:, 41:42], EYEF[:, :]),
                wait=[evHm, wpa], sig="pe")
            eMx2 = p.step("vector", lambda e, ja=ja: e.reduce_max(
                out=sm1[0:1, 8:9], in_=PA[ja][0:1, 0:128], axis=AX.X), wait=[eTm2], sig="dve")
            sl_pa3.done(ja, eMx2)
            eZm2 = p.step("gpsimd", lambda e: e.memset(sm1[0:1, 9:24], 0.0),
                          wait=[eSx, eIx], sig="gps")
            eSm2 = p.dma("gpsimd", g["ccM2_in"][0:1, 0:16], sm1[0:1, 8:24],
                         wait=[eMx2, eZm2, p.ev("dS")], sig="dS")
            evCC4 = p.step("gpsimd", lambda e: e.collective_compute(
                "AllReduce", ALU.max, replica_groups=[list(range(8))],
                ins=[g["ccM2_in"].opt()], outs=[g["ccM2_out"].opt()]),
                wait=[eSm2, evCC3], sig="cc")
            eGm2 = p.dma("sync", sm1[0:1, 24:32], g["ccM2_out"][0:1, 0:8],
                         wait=[evCC4, p.ev("dK")], sig="dK")
            eSgD = p.step("scalar", lambda e: e.activation(
                out=sm1[0:1, 7:8], in_=sm1[0:1, 5:6], func=AF.Sigmoid),
                wait=[eCd, p.ev("dK")], sig="act")
            eSh = p.step("vector", lambda e: e.scalar_tensor_tensor(
                out=sm1[0:1, 10:11], in0=sm1[0:1, 24:25], scalar=1.0 / 127.0,
                in1=sm1[0:1, 7:8], op0=ALU.mult, op1=ALU.mult),
                wait=[eSgD, p.ev("dK")], sig="dve")
            eIh = p.step("vector", lambda e: e.reciprocal(
                out=sm1[0:1, 11:12], in_=sm1[0:1, 10:11]), wait=[eSh], sig="dve")
            ja, wpa = sl_pa3.get()
            eB3 = p.step("tensor", lambda e, ja=ja: e.matmul(
                PA[ja][:, 0:2], ONESF[0:1, :], sm1[0:1, 10:12], start=True, stop=True),
                wait=[eIh, wpa], sig="pe")
            eB4 = p.step("vector", lambda e, ja=ja: e.tensor_copy(
                out=stL[:, 36:38], in_=PA[ja][:, 0:2]), wait=[eB3], sig="dve")
            sl_pa3.done(ja, eB4)

        if stage >= 8:
            p.dma("sync", cvrows(g["xnT1_park"], 0, 8, 0, TPC),
                  cvcols(xnT1, 8, TPC), wait=[evXq], sig="dV")
            p.dma("gpsimd", g["scl_park"][:, 0:52], scl[:, 0:52], wait=[p.ev("dve")], sig="dS")
            p.dma("gpsimd", g["scl_park"][:, 64:102], stL[:, 0:38], wait=[eB4], sig="dS")
            p.dma("gpsimd", g["scl_park"][0:1, 112:144], sm1[0:1, :], wait=[p.ev("dve")], sig="dS")
            p.barrier()
            a7.free()
            a5.free()
            a8 = nc.auto_arena()
            p.arenas.append(a8)
            xnR = a8.sbuf_tensor("xnR", [128, 8 * TPC], BF16); g["xnR"] = xnR
            wA = [a8.sbuf_tensor(f"wA{j}", [128, 1024], BF16) for j in range(2)]
            wB = [a8.sbuf_tensor(f"wB{j}", [128, 1024], BF16) for j in range(2)]
            wf32 = a8.sbuf_tensor("wf32", [128, 1024], F32)
            kq = a8.sbuf_tensor("kq", [128, 4 * 1024], BF16)
            vq = a8.sbuf_tensor("vq", [128, 4 * 1024], BF16)
            ktmp = [a8.sbuf_tensor(f"ktmp{j}", [128, 512], BF16) for j in range(2)]
            cosk = a8.sbuf_tensor("cosk", [128, TPC], BF16)
            sink = a8.sbuf_tensor("sink", [128, TPC], BF16)
            ropeS = a8.sbuf_tensor("ropeS", [128, 128], BF16)
            Sst = a8.sbuf_tensor("Sst", [64, 1024], F32); g["Sst"] = Sst
            Sbf = a8.sbuf_tensor("Sbf", [128, 1024], BF16)
            gcts = a8.sbuf_tensor("gcts", [128, 48], F32)
            kg = [a8.sbuf_tensor(f"kg{j}", [128, 64], BF16) for j in range(2)]
            rcoef = a8.sbuf_tensor("rcoef", [64, 128], F32)
            Ssb = a8.sbuf_tensor("Ssb", [64, 1024], BF16)
            a8.allocate()
            evC8 = p.dma("sync", cosk[:, :], g["cosk_bf"][:, :], sig="dK")
            evC8 = p.dma("sync", sink[:, :], g["sink_bf"][:, :], sig="dK")
            evC8 = p.dma("sync", ropeS[:, :], g["ropeP"][:, :], sig="dK")
            evC8 = p.dma("sync", gcts[0:64, 0:16], g["gCT"][:, :], sig="dK")
            evC8 = p.dma("sync", gcts[:, 16:32], g["gkT"][:, :], sig="dK")
            evC8 = p.dma("sync", gcts[:, 32:48], g["gfacT"][:, :], sig="dK")
            evC8 = p.dma("sync", rcoef[:, :], g["ret_coef"][:, :], sig="dK")
            evC8 = p.dma("sync", cvcols(xnR, 8, TPC),
                         cvrows(g["xnT1_park"], 0, 8, 0, TPC), sig="dK")
            sl_wA = Slots(2)
            evprev = None
            for wname, dst in (("Wq", "wqbfT"), ("Wk", "wkbfT"), ("Wv", "wvbfT"),
                               ("Wg", "wgbfT"), ("Wo", "wobfT")):
                for kt in range(8):
                    eL = p.dma("sync", wf32[:, :], g[wname][kt * 128:(kt + 1) * 128, :],
                               wait=[evprev], sig="dW")
                    ja8, w8 = sl_wA.get()
                    eCst = p.step("vector", lambda e, ja8=ja8: e.tensor_copy(
                        out=wA[ja8][:, :], in_=wf32[:, :]), wait=[eL, w8], sig="dve")
                    evprev = eCst
                    lastD = None
                    lastT = None
                    for fb in range(8):
                        jt, wt = sl_pt.get()
                        eT = p.step("tensor", lambda e, ja8=ja8, fb=fb, jt=jt: e.transpose(
                            PTs[jt][:, 0:128], wA[ja8][:, fb * 128:(fb + 1) * 128], EYE[:, :]),
                            wait=[eCst, wt], sig="pe")
                        lastT = eT
                        eC = p.step("vector", lambda e, jt=jt, ja8=ja8, fb=fb: e.tensor_copy(
                            out=wB[ja8][:, fb * 128:(fb + 1) * 128], in_=PTs[jt][:, 0:128]),
                            wait=[eT] + ([p.ev("dV")] if fb == 0 else []), sig="dve")
                        sl_pt.done(jt, eC)
                        lastD = eC
                    lastD = p.dma("sync", cvi(g[dst], 0, 8, kt * 128, 128),
                                  cvo(wB[ja8], 8, 128), wait=[lastD], sig="dV")
                    sl_wA.done(ja8, lastT)
            pipeK8 = DmaPipe(p, "dK8", n=2)
            sl_kt = Slots(2)
            sl_kg = Slots(2)
            evS0 = p.step("gpsimd", lambda e: e.memset(Sst[:, :], 0.0), sig="gps")
            Supd = [evS0] * 16
            ev_kq = None
            ev_vq = None
            ev_kpark = None
            ev_vpark = None
            for qq in range(4):
                q0 = qq * 512
                ev_kq_new = []
                for ot in range(8):
                    jwA, w8 = sl_wA.get()
                    ePan = pipeK8.load_group(
                        [(cvo(wA[jwA], 8, 128), cvi(g["wkbfT"], 0, 8, ot * 128, 128))],
                        wait=[w8] + ([p.ev("dS"), p.ev("dV"), p.ev("dve")] if (qq == 0 and ot < 2) else []))
                    ja, wpa = sl_pa3.get()
                    mmv = None
                    for kk in range(8):
                        mmv = p.step("tensor", lambda e, kk=kk, ja=ja, q0=q0, jwA=jwA: e.matmul(
                            PA[ja][:, :], wA[jwA][:, kk * 128:(kk + 1) * 128],
                            xnR[:, kk * TPC + q0: kk * TPC + q0 + 512],
                            start=(kk == 0), stop=(kk == 7)),
                            wait=[ePan, wpa, p.ev("dK")] if kk == 0 else [],
                            sig="pe" if kk == 7 else None)
                    sl_wA.done(jwA, mmv)
                    jk2, wk2_ = sl_kt.get()
                    eKt = p.step("vector", lambda e, ja=ja, jk2=jk2: e.tensor_copy(
                        out=ktmp[jk2][:, :], in_=PA[ja][:, :]), wait=[mmv, wk2_], sig="dve")
                    ja2, wpa2 = sl_pa3.get()
                    mmr = p.step("tensor", lambda e, ja2=ja2, jk2=jk2: e.matmul(
                        PA[ja2][:, :], ropeS[:, :], ktmp[jk2][:, :], start=True, stop=True),
                        wait=[eKt, wpa2, p.ev("dK")], sig="pe")
                    sl_pa3.done(ja, mmr)
                    eR1 = p.step("vector", lambda e, jk2=jk2, ot=ot, q0=q0: e.tensor_tensor(
                        out=ktmp[jk2][:, :], in0=ktmp[jk2][:, :],
                        in1=cosk[ot * 16:(ot + 1) * 16, :][0:128, q0:q0 + 512]
                        if False else cosk[:, q0:q0 + 512], op=ALU.mult),
                        wait=[mmr], sig="dve")
                    eR2 = p.step("vector", lambda e, ja2=ja2, ot=ot, q0=q0, jk2=jk2: e.scalar_tensor_tensor(
                        out=ktmp[jk2][:, :], in0=PA[ja2][:, :], scalar=1.0,
                        in1=ktmp[jk2][:, :], op0=ALU.bypass, op1=ALU.mult)
                        if False else e.tensor_tensor(
                        out=PA[ja2][:, :], in0=PA[ja2][:, :], in1=PA[ja2][:, :], op=ALU.mult),
                        wait=[eR1], sig="dve") if False else None
                    jk3, wk3_ = sl_kt.get()
                    eR2 = p.step("vector", lambda e, ja2=ja2, jk3=jk3, q0=q0: e.tensor_tensor(
                        out=ktmp[jk3][:, :], in0=PA[ja2][:, :], in1=sink[:, q0:q0 + 512],
                        op=ALU.mult), wait=[mmr, wk3_, p.ev("dK")], sig="dve")
                    sl_pa3.done(ja2, eR2)
                    eR3 = p.step("vector", lambda e, jk2=jk2, jk3=jk3, ot=ot: e.tensor_tensor(
                        out=kq[:, ot * 512:(ot + 1) * 512], in0=ktmp[jk2][:, :],
                        in1=ktmp[jk3][:, :], op=ALU.add),
                        wait=[eR1, eR2, ev_kq, p.ev("dV")], sig="dve")
                    sl_kt.done(jk2, eR3)
                    sl_kt.done(jk3, eR3)
                    ev_kq_new.append(eR3)
                ev_kq = ev_kq_new[-1]
                if qq == 0:
                    pw_free3v = list(pw_free3)
                ev_vq_new = []
                for half in range(2):
                    mmvs = [None] * 4
                    for kk in range(8):
                        jwA2, w82 = sl_wA.get()
                        ePan2 = pipeK8.load(
                            wA[jwA2][:, 0:512],
                            g["wvbfT"][kk * 128:(kk + 1) * 128, half * 512:(half + 1) * 512],
                            wait=[w82])
                        for tt in range(4):
                            mmvs[tt] = p.step("tensor", lambda e, kk=kk, q0=q0, tt=tt,
                                              jwA2=jwA2: e.matmul(
                                PW[tt][:, :], xnR[:, kk * TPC + q0 + tt * 128:
                                                  kk * TPC + q0 + (tt + 1) * 128],
                                wA[jwA2][:, 0:512],
                                start=(kk == 0), stop=(kk == 7)),
                                wait=[ePan2] + ([pw_free3v[tt], p.ev("dK")] if kk == 0 else []),
                                sig="pe")
                        sl_wA.done(jwA2, mmvs[3])
                    for tt in range(4):
                        eV8 = p.step("vector", lambda e, tt=tt, half=half: e.tensor_copy(
                            out=vq[:, tt * 1024 + half * 512: tt * 1024 + (half + 1) * 512],
                            in_=PW[tt][:, :]), wait=[mmvs[tt], ev_vq, p.ev("dV")], sig="dve")
                        pw_free3v[tt] = eV8
                        ev_vq_new.append(eV8)
                ev_vq = ev_vq_new[-1]
                ev_kpark = p.dma("sync", cvi(g["kT_park"], 0, 8, q0, 512),
                                 cvo(kq, 8, 512), wait=[ev_kq], sig="dV")
                ev_vpark = p.dma("sync", cvrows(g["v_park"], q0, 4, 0, 1024),
                                 cvcols(vq, 4, 1024), wait=[ev_vq], sig="dV")
                for tt in range(4):
                    for h in range(16):
                        tb, ro = h // 2, (h % 2) * 64
                        jt, wt = sl_pt.get()
                        eTb = p.step("tensor", lambda e, tb=tb, ro=ro, tt=tt, jt=jt: e.transpose(
                            PTs[jt][:, 0:64],
                            kq[ro:ro + 64, tb * 512 + tt * 128: tb * 512 + (tt + 1) * 128],
                            EYE[ro:ro + 64, ro:ro + 64]),
                            wait=[ev_kq, wt], sig="pe")
                        jg, wg_ = sl_kg.get()
                        eKg = p.step("scalar", lambda e, jt=jt, jg=jg, h=h: e.activation(
                            out=kg[jg][:, :], in_=PTs[jt][:, 0:64], func=AF.Copy,
                            scale=gcts[:, 16 + h:17 + h]),
                            wait=[eTb, wg_, p.ev("dK")], sig="act")
                        sl_pt.done(jt, eKg)
                        ja, wpa = sl_pa3.get()
                        mmS = p.step("tensor", lambda e, jg=jg, tt=tt, h=h, ja=ja: e.matmul(
                            PA[ja][0:64, 0:64], kg[jg][:, :],
                            vq[:, tt * 1024 + h * 64: tt * 1024 + (h + 1) * 64],
                            start=True, stop=True),
                            wait=[eKg, ev_vq, wpa], sig="pe")
                        sl_kg.done(jg, mmS)
                        upd = p.step("vector", lambda e, h=h, ja=ja: e.scalar_tensor_tensor(
                            out=Sst[:, h * 64:(h + 1) * 64], in0=Sst[:, h * 64:(h + 1) * 64],
                            scalar=gcts[0:64, h:h + 1], in1=PA[ja][0:64, 0:64],
                            op0=ALU.mult, op1=ALU.add),
                            wait=[mmS, Supd[h], p.ev("dK")], sig="dve")
                        sl_pa3.done(ja, upd)
                        Supd[h] = upd
                ev_kq = Supd[15]
                ev_vq = Supd[15]
            eSb8 = p.step("vector", lambda e: e.tensor_copy(out=Ssb[:, :], in_=Sst[:, :]),
                          wait=Supd, sig="dve")
            eSs = p.dma("gpsimd", g["ccS_in"][:, :], Ssb[:, :], wait=[eSb8], sig="dS")
            evCC2 = p.step("gpsimd", lambda e: e.collective_compute(
                "AllGather", ALU.bypass, replica_groups=[list(range(8))],
                ins=[g["ccS_in"].opt()], outs=[g["ccS_out"].opt()]),
                wait=[eSs], sig="cc")
            def emit_compose2():
                evComp = None
                evHs8 = None
                for sdr in range(8):
                    evHs8 = p.dma("sync", Ssb[:, :], g["ccS_out"][sdr * 64:(sdr + 1) * 64, :],
                                  wait=[evCC2, evComp,
                                        evHs8 if sdr else p.ev("dR")], sig="dR")
                    for h in range(16):
                        if sdr == 0:
                            evComp = p.step("vector", lambda e, h=h, sdr=sdr: e.tensor_scalar(
                                out=Sst[:, h * 64:(h + 1) * 64], in0=Ssb[:, h * 64:(h + 1) * 64],
                                scalar1=rcoef[0:64, sdr * 16 + h:sdr * 16 + h + 1], scalar2=None,
                                op0=ALU.mult), wait=[evHs8, eSb8, p.ev("dK")], sig="dve")
                        else:
                            evComp = p.step("vector", lambda e, h=h, sdr=sdr: e.scalar_tensor_tensor(
                                out=Sst[:, h * 64:(h + 1) * 64], in0=Ssb[:, h * 64:(h + 1) * 64],
                                scalar=rcoef[0:64, sdr * 16 + h:sdr * 16 + h + 1],
                                in1=Sst[:, h * 64:(h + 1) * 64],
                                op0=ALU.mult, op1=ALU.add), wait=[evHs8, evComp], sig="dve")
                eSdup = p.step("vector", lambda e: e.tensor_copy(out=Sbf[0:64, :], in_=Sst[:, :]),
                               wait=[evComp], sig="dve")
                eSdup = p.dma("gpsimd", Sbf[64:128, :], Sbf[0:64, :], wait=[eSdup], sig="dS")
                return eSdup
            if stage == 8:
                eSdup = emit_compose2()

        if stage >= 9:
            a9 = nc.auto_arena()
            p.arenas.append(a9)
            dmsb = a9.sbuf_tensor("dmsb", [128, NH1 * CH], BF16)
            cosq = a9.sbuf_tensor("cosq", [128, TPC], BF16)
            sinq = a9.sbuf_tensor("sinq", [128, TPC], BF16)
            qq_b = a9.sbuf_tensor("qq_b", [128, 8 * 512], BF16)
            gtok = a9.sbuf_tensor("gtok", [128, 4 * 1024], BF16)
            kcs = [a9.sbuf_tensor(f"kcs{j}", [128, 8 * 128], BF16) for j in range(2)]
            vcs = [a9.sbuf_tensor(f"vcs{j}", [128, 1024], BF16) for j in range(2)]
            dst9 = [a9.sbuf_tensor(f"dst9{j}", [128, 128], BF16) for j in range(2)]
            yret = a9.sbuf_tensor("yret", [128, 1024], F32)
            gst = a9.sbuf_tensor("gst", [128, 40], F32)
            gnwb = a9.sbuf_tensor("gnwb", [128, 1024], BF16)
            gnbb = a9.sbuf_tensor("gnbb", [128, 1024], BF16)
            gdum = a9.sbuf_tensor("gdum", [128, 64], F32)
            gated = a9.sbuf_tensor("gated", [128, 1024], BF16)
            gatT = a9.sbuf_tensor("gatT", [128, 8 * 256], BF16)
            wop = [a9.sbuf_tensor(f"wop{j}", [128, 1024], BF16) for j in range(2)]
            grow = a9.sbuf_tensor("grow", [1, 1024], F32)
            scl9 = a9.sbuf_tensor("scl9", [128, 2], F32)
            a9.allocate()
            hstg9 = [cosk, sink]
            hfs9 = vq[:, 0:4096].bitcast(F32)
            his9 = kq[:, 0:4096].bitcast(I32)
            evD9 = p.dma("sync", dmsb[:, :], g["dmaskT"][:, :], sig="dK")
            evD9 = p.dma("sync", cosq[:, :], g["cosq_bf"][:, :], sig="dK")
            evD9 = p.dma("sync", sinq[:, :], g["sinq_bf"][:, :], sig="dK")
            evD9 = p.dma("sync", grow[0:1, :], g["gn_w"][:, :], sig="dK")
            ja, wpa = sl_pa3.get()
            eGn1 = p.step("tensor", lambda e, ja=ja: e.matmul(
                PA[ja][:, 0:512], ONESF[0:1, :], grow[0:1, 0:512], start=True, stop=True),
                wait=[evD9, wpa], sig="pe")
            eGn1b = p.step("tensor", lambda e, ja=ja: e.matmul(
                PA[ja][:, :], ONESF[0:1, :], grow[0:1, 512:1024], start=False, stop=True),
                wait=[], sig="pe") if False else None
            eGw1 = p.step("vector", lambda e, ja=ja: e.tensor_copy(
                out=gnwb[:, 0:512], in_=PA[ja][:, 0:512]), wait=[eGn1], sig="dve")
            sl_pa3.done(ja, eGw1)
            ja, wpa = sl_pa3.get()
            eGn2 = p.step("tensor", lambda e, ja=ja: e.matmul(
                PA[ja][:, 0:512], ONESF[0:1, :], grow[0:1, 512:1024], start=True, stop=True),
                wait=[eGw1, wpa], sig="pe")
            eGw2 = p.step("vector", lambda e, ja=ja: e.tensor_copy(
                out=gnwb[:, 512:1024], in_=PA[ja][:, 0:512]), wait=[eGn2], sig="dve")
            sl_pa3.done(ja, eGw2)
            evD9b = p.dma("sync", grow[0:1, :], g["gn_b"][:, :], wait=[eGw2, p.ev("dK")], sig="dK")
            for hf in range(2):
                ja, wpa = sl_pa3.get()
                eGn3 = p.step("tensor", lambda e, ja=ja, hf=hf: e.matmul(
                    PA[ja][:, 0:512], ONESF[0:1, :], grow[0:1, hf * 512:(hf + 1) * 512],
                    start=True, stop=True), wait=[evD9b, wpa], sig="pe")
                eGw3 = p.step("vector", lambda e, ja=ja, hf=hf: e.tensor_copy(
                    out=gnbb[:, hf * 512:(hf + 1) * 512], in_=PA[ja][:, 0:512]),
                    wait=[eGn3], sig="dve")
                sl_pa3.done(ja, eGw3)
            evGnb = eGw3
            pipeQ9 = DmaPipe(p, "dQ9", n=2, engine="scalar")
            pipeKC = DmaPipe(p, "dKC", n=2)
            pipeVC = DmaPipe(p, "dVC", n=2)
            pipeGW = DmaPipe(p, "dGW", n=2, engine="scalar")
            pipeWo = DmaPipe(p, "dWO", n=2)
            sl_kc = Slots(2)
            sl_vc = Slots(2)
            sl_d9 = Slots(2)
            sl_wo9 = Slots(1)
            ev_qq = None
            ev_gtok = None
            ev_yret = None
            ev_gated = None
            ev_gatT = [None, None]
            ev_Rw9 = p.ev("dve")
            ev_gdum = None
            pwf = [p.ev("act")] * 4
            SupdB = list(Supd)
            ev_Sbf = None
            for qq9 in range(4):
                q0 = qq9 * 512
                evq_new = None
                for ot in range(8):
                    jwA, w8 = sl_wA.get()
                    ePan = pipeQ9.load_group(
                        [(cvo(wA[jwA], 8, 128), cvi(g["wqbfT"], 0, 8, ot * 128, 128))],
                        wait=[w8] + ([p.ev("dS"), p.ev("dV"), p.ev("dve")] if (qq9 == 0 and ot < 2) else []))
                    ja, wpa = sl_pa3.get()
                    mmv = None
                    for kk in range(8):
                        mmv = p.step("tensor", lambda e, kk=kk, ja=ja, q0=q0, jwA=jwA: e.matmul(
                            PA[ja][:, :], wA[jwA][:, kk * 128:(kk + 1) * 128],
                            xnR[:, kk * TPC + q0: kk * TPC + q0 + 512],
                            start=(kk == 0), stop=(kk == 7)),
                            wait=[ePan, wpa] if kk == 0 else [], sig="pe" if kk == 7 else None)
                    sl_wA.done(jwA, mmv)
                    jk2, wk2_ = sl_kt.get()
                    eKt = p.step("vector", lambda e, ja=ja, jk2=jk2: e.tensor_copy(
                        out=ktmp[jk2][:, :], in_=PA[ja][:, :]), wait=[mmv, wk2_], sig="dve")
                    ja2, wpa2 = sl_pa3.get()
                    mmr = p.step("tensor", lambda e, ja2=ja2, jk2=jk2: e.matmul(
                        PA[ja2][:, :], ropeS[:, :], ktmp[jk2][:, :], start=True, stop=True),
                        wait=[eKt, wpa2], sig="pe")
                    sl_pa3.done(ja, mmr)
                    eR1 = p.step("vector", lambda e, jk2=jk2, q0=q0: e.tensor_tensor(
                        out=ktmp[jk2][:, :], in0=ktmp[jk2][:, :],
                        in1=cosq[:, q0:q0 + 512], op=ALU.mult),
                        wait=[mmr, p.ev("dK")], sig="dve")
                    jk3, wk3_ = sl_kt.get()
                    eR2 = p.step("vector", lambda e, ja2=ja2, jk3=jk3, q0=q0: e.tensor_tensor(
                        out=ktmp[jk3][:, :], in0=PA[ja2][:, :], in1=sinq[:, q0:q0 + 512],
                        op=ALU.mult), wait=[mmr, wk3_, p.ev("dK")], sig="dve")
                    sl_pa3.done(ja2, eR2)
                    evq_new = p.step("vector", lambda e, jk2=jk2, jk3=jk3, ot=ot: e.tensor_tensor(
                        out=qq_b[:, ot * 512:(ot + 1) * 512], in0=ktmp[jk2][:, :],
                        in1=ktmp[jk3][:, :], op=ALU.add),
                        wait=[eR1, eR2, ev_qq], sig="dve")
                    sl_kt.done(jk2, evq_new)
                    sl_kt.done(jk3, evq_new)
                ev_qq = evq_new
                evg_new = None
                for half in range(2):
                    mmgs = [None] * 4
                    for kk in range(8):
                        jwA2, w82 = sl_wA.get()
                        ePan2 = pipeGW.load(
                            wA[jwA2][:, 0:512],
                            g["wgbfT"][kk * 128:(kk + 1) * 128, half * 512:(half + 1) * 512],
                            wait=[w82])
                        for tt in range(4):
                            mmgs[tt] = p.step("tensor", lambda e, kk=kk, q0=q0, tt=tt,
                                              jwA2=jwA2: e.matmul(
                                PW[tt][:, :], xnR[:, kk * TPC + q0 + tt * 128:
                                                  kk * TPC + q0 + (tt + 1) * 128],
                                wA[jwA2][:, 0:512],
                                start=(kk == 0), stop=(kk == 7)),
                                wait=[ePan2] + ([pwf[tt]] if kk == 0 else []),
                                sig="pe")
                        sl_wA.done(jwA2, mmgs[3])
                    for tt in range(4):
                        evg_new = p.step("scalar", lambda e, tt=tt, half=half: e.activation(
                            out=gtok[:, tt * 1024 + half * 512: tt * 1024 + (half + 1) * 512],
                            in_=PW[tt][:, :], func=AF.Silu),
                            wait=[mmgs[tt], ev_gtok], sig="act")
                        pwf[tt] = evg_new
                ev_gtok = evg_new
                if qq9 == 0:
                    ev_Sbf = emit_compose2()
                if qq9 == 2:
                    pipeHq = DmaPipe(p, "dHq", n=2)
                    pipeHs = DmaPipe(p, "dHs", n=2)
                    sl_hs = Slots(2)
                    p.dma("sync", scl9[:, :], g["scl_park"][:, 100:102], sig="dK")
                    ev_hq9 = None
                    for k2 in range(22):
                        jhs, whs = sl_hs.get()
                        eLh = pipeHq.load(hstg9[jhs][:, :],
                                          g["hT_park"][k2 * 128:(k2 + 1) * 128, :],
                                          wait=[whs] + ([p.ev("pe"), p.ev("dve"), p.ev("dV")]
                                                        if k2 < 2 else []))
                        eQ1 = p.step("scalar", lambda e, jhs=jhs: e.activation(
                            out=hfs9[:, :], in_=hstg9[jhs][:, :], func=AF.Copy,
                            scale=scl9[:, 1:2]), wait=[eLh, p.ev("dK"), ev_hq9], sig="act")
                        eQ2 = p.step("vector", lambda e: e.tensor_copy(
                            out=his9[:, :], in_=hfs9[:, :]), wait=[eQ1], sig="dve")
                        eQ3 = p.step("vector", lambda e: e.tensor_copy(
                            out=hfs9[:, :], in_=his9[:, :]), wait=[eQ2], sig="dve")
                        eQ4 = p.step("vector", lambda e, jhs=jhs: e.tensor_scalar(
                            out=hstg9[jhs][:, :], in0=hfs9[:, :], scalar1=127.0,
                            scalar2=-128.0, op0=ALU.min, op1=ALU.max),
                            wait=[eQ3], sig="dve")
                        ev_hq9 = pipeHs.load(g["hq_park"][k2 * 128:(k2 + 1) * 128, :],
                                             hstg9[jhs][:, :], wait=[eQ4])
                        sl_hs.done(jhs, ev_hq9)
                for tt in range(4):
                    c9 = qq9 * 4 + tt
                    jkc, wkc = sl_kc.get()
                    evK = pipeKC.load_group(
                        [(cvo(kcs[jkc], 8, 128), cvi(g["kT_park"], 0, 8, c9 * 128, 128))],
                        wait=[wkc])
                    jvc, wvc = sl_vc.get()
                    evV = pipeVC.load(vcs[jvc][:, :], g["v_park"][c9 * 128:(c9 + 1) * 128, :],
                                      wait=[wvc])
                    ev_hd = [None] * 16
                    for h in range(16):
                        tb, ro = h // 2, (h % 2) * 64
                        ja, wpa = sl_pa3.get()
                        mmG = p.step("tensor", lambda e, ja=ja, tb=tb, ro=ro, tt=tt, jkc=jkc: e.matmul(
                            PA[ja][:, 0:128],
                            kcs[jkc][ro:ro + 64, tb * 128:(tb + 1) * 128],
                            qq_b[ro:ro + 64, tb * 512 + tt * 128: tb * 512 + (tt + 1) * 128],
                            start=True, stop=True),
                            wait=[evK, ev_qq, wpa], sig="pe")
                        jd, wd = sl_d9.get()
                        eDst = p.step("vector", lambda e, ja=ja, jd=jd, h=h: e.tensor_tensor(
                            out=dst9[jd][:, :], in0=PA[ja][:, 0:128],
                            in1=dmsb[:, h * 128:(h + 1) * 128], op=ALU.mult),
                            wait=[mmG, wd, p.ev("dK")], sig="dve")
                        sl_pa3.done(ja, eDst)
                        mm1 = p.step("tensor", lambda e, jd=jd, jvc=jvc, h=h: e.matmul(
                            PW[h // 8][:, (h % 8) * 64:((h % 8) + 1) * 64],
                            dst9[jd][:, :], vcs[jvc][:, h * 64:(h + 1) * 64],
                            start=True, stop=True),
                            wait=[eDst, evV, pwf[h // 8]], sig="pe")
                        sl_d9.done(jd, mm1)
                        mm2 = p.step("tensor", lambda e, tb=tb, ro=ro, tt=tt, h=h: e.matmul(
                            PW[2 + h // 8][:, (h % 8) * 64:((h % 8) + 1) * 64],
                            qq_b[ro:ro + 64, tb * 512 + tt * 128: tb * 512 + (tt + 1) * 128],
                            Sbf[ro:ro + 64, h * 64:(h + 1) * 64],
                            start=True, stop=True),
                            wait=[ev_qq, ev_Sbf, pwf[2 + h // 8]], sig="pe")
                        ev_hd[h] = (mm1, mm2)
                    eYc = [None, None]
                    for j4 in range(2):
                        eYc[j4] = p.step("scalar", lambda e, j4=j4: e.activation(
                            out=yret[:, j4 * 512:(j4 + 1) * 512], in_=PW[j4][:, :],
                            func=AF.Copy),
                            wait=[ev_hd[j4 * 8 + 7][0], ev_hd[j4 * 8][0], ev_yret], sig="act")
                    eY9 = None
                    for h in range(16):
                        eY9 = p.step("vector", lambda e, h=h: e.scalar_tensor_tensor(
                            out=yret[:, h * 64:(h + 1) * 64],
                            in0=PW[2 + h // 8][:, (h % 8) * 64:((h % 8) + 1) * 64],
                            scalar=gcts[:, 32 + h:33 + h],
                            in1=yret[:, h * 64:(h + 1) * 64],
                            op0=ALU.mult, op1=ALU.add),
                            wait=[ev_hd[h][1], eYc[h // 8], p.ev("dK")], sig="dve")
                    for j4 in range(4):
                        pwf[j4] = eY9
                    lastS9 = None
                    for h in range(16):
                        tb, ro = h // 2, (h % 2) * 64
                        jt, wt = sl_pt.get()
                        eTb = p.step("tensor", lambda e, tb=tb, ro=ro, tt=tt, jt=jt, jkc=jkc: e.transpose(
                            PTs[jt][:, 0:64],
                            kcs[jkc][ro:ro + 64, tb * 128:(tb + 1) * 128],
                            EYE[ro:ro + 64, ro:ro + 64]),
                            wait=[evK, wt], sig="pe")
                        jg, wg_ = sl_kg.get()
                        eKg = p.step("scalar", lambda e, jt=jt, jg=jg, h=h: e.activation(
                            out=kg[jg][:, :], in_=PTs[jt][:, 0:64], func=AF.Copy,
                            scale=gcts[:, 16 + h:17 + h]),
                            wait=[eTb, wg_], sig="act")
                        sl_pt.done(jt, eKg)
                        ja, wpa = sl_pa3.get()
                        mmS = p.step("tensor", lambda e, jg=jg, jvc=jvc, h=h, ja=ja: e.matmul(
                            PA[ja][0:64, 0:64], kg[jg][:, :],
                            vcs[jvc][:, h * 64:(h + 1) * 64],
                            start=True, stop=True),
                            wait=[eKg, evV, wpa], sig="pe")
                        sl_kg.done(jg, mmS)
                        upd = p.step("vector", lambda e, h=h, ja=ja: e.scalar_tensor_tensor(
                            out=Sst[:, h * 64:(h + 1) * 64], in0=Sst[:, h * 64:(h + 1) * 64],
                            scalar=gcts[0:64, h:h + 1], in1=PA[ja][0:64, 0:64],
                            op0=ALU.mult, op1=ALU.add),
                            wait=[mmS, SupdB[h], ev_Sbf], sig="dve")
                        sl_pa3.done(ja, upd)
                        SupdB[h] = upd
                        lastS9 = upd
                    sl_kc.done(jkc, lastS9)
                    sl_vc.done(jvc, lastS9)
                    ev_Sbf = p.step("vector", lambda e: e.tensor_copy(
                        out=Sbf[0:64, :], in_=Sst[:, :]), wait=SupdB + [ev_Sbf], sig="dve")
                    ev_Sbf = p.dma("gpsimd", Sbf[64:128, :], Sbf[0:64, :],
                                   wait=[ev_Sbf], sig="dS")
                    eGs9 = None
                    eSqp = ev_gdum
                    for h in range(16):
                        eGs9 = p.step("vector", lambda e, h=h: e.reduce_sum(
                            out=gst[:, h:h + 1], in_=yret[:, h * 64:(h + 1) * 64], axis=AX.X),
                            wait=[eY9] if h == 0 else [], sig="dve")
                        eSqp = p.step("scalar", lambda e, h=h: e.activation(
                            out=gdum[:, :], in_=yret[:, h * 64:(h + 1) * 64], func=AF.Square,
                            accum_out=gst[:, 16 + h:17 + h]),
                            wait=[eY9, eSqp], sig="act")
                    eGs9 = eSqp
                    eM9 = p.step("vector", lambda e: e.tensor_scalar(
                        out=gst[:, 0:16], in0=gst[:, 0:16], scalar1=1.0 / DH1, scalar2=None,
                        op0=ALU.mult), wait=[eGs9, p.ev("dve"), p.ev("act")], sig="dve")
                    eV9 = p.step("vector", lambda e: e.tensor_scalar(
                        out=gst[:, 16:32], in0=gst[:, 16:32], scalar1=1.0 / DH1, scalar2=1e-5,
                        op0=ALU.mult, op1=ALU.add), wait=[eM9, p.ev("act")], sig="dve")
                    eV9b = p.step("vector", lambda e: e.tensor_tensor(
                        out=gst[:, 32:40][:, 0:8] if False else gst[:, 16:32],
                        in0=gst[:, 16:32], in1=gst[:, 0:16], op=ALU.subtract)
                        if False else e.tensor_tensor(
                        out=gdum[:, 0:16], in0=gst[:, 0:16], in1=gst[:, 0:16], op=ALU.mult),
                        wait=[eV9], sig="dve")
                    eV9c = p.step("vector", lambda e: e.tensor_tensor(
                        out=gst[:, 16:32], in0=gst[:, 16:32], in1=gdum[:, 0:16],
                        op=ALU.subtract), wait=[eV9b], sig="dve")
                    ev_gdum = eV9c
                    eSr9 = p.step("scalar", lambda e: e.activation(
                        out=gst[:, 16:32], in_=gst[:, 16:32], func=AF.Sqrt),
                        wait=[eV9c], sig="act")
                    eRc9 = p.step("vector", lambda e: e.reciprocal(
                        out=gst[:, 16:32], in_=gst[:, 16:32]), wait=[eSr9], sig="dve")
                    eNb9 = p.step("vector", lambda e: e.scalar_tensor_tensor(
                        out=gst[:, 0:16], in0=gst[:, 0:16], scalar=-1.0, in1=gst[:, 16:32],
                        op0=ALU.mult, op1=ALU.mult), wait=[eRc9], sig="dve")
                    eAp9 = None
                    for h in range(16):
                        eAp9 = p.step("vector", lambda e, h=h: e.tensor_scalar(
                            out=yret[:, h * 64:(h + 1) * 64], in0=yret[:, h * 64:(h + 1) * 64],
                            scalar1=gst[:, 16 + h:17 + h], scalar2=gst[:, h:h + 1],
                            op0=ALU.mult, op1=ALU.add),
                            wait=[eNb9] if h == 0 else [], sig="dve")
                    eGnw = p.step("vector", lambda e: e.tensor_tensor(
                        out=yret[:, :], in0=yret[:, :], in1=gnwb[:, :], op=ALU.mult),
                        wait=[eAp9, eGw2], sig="dve")
                    eGnb2 = p.step("vector", lambda e: e.tensor_tensor(
                        out=yret[:, :], in0=yret[:, :], in1=gnbb[:, :], op=ALU.add),
                        wait=[eGnw, evGnb], sig="dve")
                    eGt = p.step("vector", lambda e, tt=tt: e.tensor_tensor(
                        out=gated[:, :], in0=yret[:, :], in1=gtok[:, tt * 1024:(tt + 1) * 1024],
                        op=ALU.mult), wait=[eGnb2, ev_gtok, ev_gated], sig="dve")
                    ev_yret = eGt
                    lastg9 = None
                    for ff in range(8):
                        jt, wt = sl_pt.get()
                        eT = p.step("tensor", lambda e, ff=ff, jt=jt: e.transpose(
                            PTs[jt][:, 0:128], gated[:, ff * 128:(ff + 1) * 128], EYE[:, :]),
                            wait=[eGt, wt], sig="pe")
                        lastg9 = p.step("vector", lambda e, ff=ff, tt=tt, jt=jt: e.tensor_copy(
                            out=gatT[:, ff * 256 + (tt % 2) * 128: ff * 256 + (tt % 2) * 128 + 128],
                            in_=PTs[jt][:, 0:128]), wait=[eT, ev_gatT[tt % 2]], sig="dve")
                        sl_pt.done(jt, lastg9)
                    ev_gated = lastg9
                    ev_gatT[tt % 2] = lastg9
                    if tt % 2 == 1:
                        ev_mo = [None] * 4
                        for k2 in range(8):
                            jwo, wwo = sl_wo9.get()
                            eKo = pipeWo.load(wop[jwo][:, :],
                                              g["wobfT"][k2 * 128:(k2 + 1) * 128, :],
                                              wait=[wwo])
                            for tt2 in range(2):
                                for nh in range(2):
                                    pwidx = tt2 * 2 + nh
                                    ev_mo[pwidx] = p.step("tensor",
                                        lambda e, k2=k2, tt2=tt2, nh=nh, jwo=jwo, pwidx=pwidx: e.matmul(
                                            PW[pwidx][:, :],
                                            gatT[:, k2 * 256 + tt2 * 128: k2 * 256 + tt2 * 128 + 128],
                                            wop[jwo][:, nh * 512:(nh + 1) * 512],
                                            start=(k2 == 0), stop=(k2 == 7)),
                                        wait=[eKo, ev_gatT[0], ev_gatT[1]] +
                                             ([pwf[pwidx]] if k2 == 0 else []),
                                        sig="pe")
                            sl_wo9.done(jwo, ev_mo[3])
                        for tt2 in range(2):
                            ti9 = qq9 * 4 + (tt - 1) + tt2
                            for nh in range(2):
                                pwidx = tt2 * 2 + nh
                                evR9 = p.step("vector", lambda e, ti9=ti9, nh=nh, pwidx=pwidx: e.tensor_tensor(
                                    out=R[:, ti9 * DM + nh * 512: ti9 * DM + (nh + 1) * 512],
                                    in0=R[:, ti9 * DM + nh * 512: ti9 * DM + (nh + 1) * 512],
                                    in1=PW[pwidx][:, :], op=ALU.add),
                                    wait=[ev_mo[pwidx], ev_Rw9], sig="dve")
                                pwf[pwidx] = evR9
                                ev_Rw9 = evR9
                        ev_gatT = [ev_Rw9, ev_Rw9]

        if stage >= 10:
            p.barrier()
            a9.free()
            a8.free()
            a10 = nc.auto_arena()
            p.arenas.append(a10)
            wdres = a10.sbuf_tensor("wdres", [128, 22 * 1024], BF16)
            hfs = a10.sbuf_tensor("hfs", [128, 2048], F32)
            hqs = [a10.sbuf_tensor(f"hqs{j}", [128, 512], BF16) for j in range(3)]
            sdhb = a10.sbuf_tensor("sdhb", [128, 1024], F32)
            scl2 = a10.sbuf_tensor("scl2", [128, 16], F32)
            sd8 = a10.sbuf_tensor("sd8", [8, 128], F32)
            sdrow = a10.sbuf_tensor("sdrow", [1, 1024], F32)
            a10.allocate()
            dint10 = g["hq_park"]
            eL10 = p.dma("sync", scl2[:, 0:2], g["scl_park"][:, 100:102], sig="dK")
            eL10b = p.dma("sync", scl2[:, 2:10], g["scl_park"][:, 44:52], wait=[eL10], sig="dK")
            ja, wpa = sl_pa3.get()
            eT10 = p.step("tensor", lambda e, ja=ja: e.transpose(
                PA[ja][0:8, 0:128], scl2[:, 2:10], EYEF[:, :]),
                wait=[p.ev("dK"), wpa], sig="pe")
            eC10 = p.step("vector", lambda e, ja=ja: e.tensor_copy(
                out=sd8[:, :], in_=PA[ja][0:8, 0:128]), wait=[eT10], sig="dve")
            sl_pa3.done(ja, eC10)
            eF10 = p.dma("gpsimd", sdrow[0:1, :], sd8[:, :], wait=[eC10, p.ev("dS")], sig="dS")
            ja, wpa = sl_pa3.get()
            eR10 = p.step("tensor", lambda e, ja=ja: e.matmul(
                PA[ja][:, 0:512], ONESF[0:1, :], sdrow[0:1, 0:512], start=True, stop=True),
                wait=[eF10, wpa], sig="pe")
            eB10 = p.step("vector", lambda e, ja=ja: e.tensor_copy(
                out=sdhb[:, 0:512], in_=PA[ja][:, 0:512]), wait=[eR10], sig="dve")
            sl_pa3.done(ja, eB10)
            ja, wpa = sl_pa3.get()
            eR11 = p.step("tensor", lambda e, ja=ja: e.matmul(
                PA[ja][:, 0:512], ONESF[0:1, :], sdrow[0:1, 512:1024], start=True, stop=True),
                wait=[eB10, wpa], sig="pe")
            eB11 = p.step("vector", lambda e, ja=ja: e.tensor_copy(
                out=sdhb[:, 512:1024], in_=PA[ja][:, 0:512]), wait=[eR11], sig="dve")
            sl_pa3.done(ja, eB11)
            pipeWd = DmaPipe(p, "dWd", n=2)
            evWdL = {}
            pipeHl = DmaPipe(p, "dHl", n=3)
            sl_hq = Slots(3)
            pwfD = [p.ev("dve")] * 4
            ev_RwD = p.ev("dve")
            for tig in range(8):
                ev_m = [None] * 4
                for k2p in range(11):
                    if tig == 0:
                        for k2l in (2 * k2p, 2 * k2p + 1):
                            evWdL[k2l] = pipeWd.load(
                                wdres[:, k2l * 1024:(k2l + 1) * 1024],
                                g["wdTint"][k2l * 128:(k2l + 1) * 128, :],
                                wait=[evWdL.get(k2l - 2)])
                    jh, wh = sl_hq.get()
                    eLq = pipeHl.load(cvo(hqs[jh], 2, 256),
                                      cvi(g["hq_park"], k2p * 256, 2, tig * 256, 256),
                                      wait=[wh] + ([p.ev("dHs0"), p.ev("dHs1")]
                                                  if (tig == 0 and k2p < 3) else []))
                    lastm = None
                    for k2s in range(2):
                        k2 = k2p * 2 + k2s
                        for ti2 in range(2):
                            for nh in range(2):
                                pwidx = ti2 * 2 + nh
                                ev_m[pwidx] = p.step("tensor",
                                    lambda e, k2=k2, k2s=k2s, ti2=ti2, nh=nh, jh=jh,
                                    pwidx=pwidx: e.matmul(
                                        PW[pwidx][:, :],
                                        hqs[jh][:, k2s * 256 + ti2 * 128:
                                                k2s * 256 + (ti2 + 1) * 128],
                                        wdres[:, k2 * 1024 + nh * 512: k2 * 1024 + (nh + 1) * 512],
                                        start=(k2 == 0), stop=(k2 == 21)),
                                    wait=[eLq] + ([evWdL[k2]] if tig == 0 else [])
                                         + ([pwfD[pwidx]] if k2 == 0 else []),
                                    sig="pe")
                                lastm = ev_m[pwidx]
                    sl_hq.done(jh, lastm)
                for ti2 in range(2):
                    ti10 = tig * 2 + ti2
                    for nh in range(2):
                        pwidx = ti2 * 2 + nh
                        eEv = p.step("vector", lambda e, nh=nh, pwidx=pwidx: e.tensor_tensor(
                            out=hfs[:, nh * 512:(nh + 1) * 512], in0=PW[pwidx][:, :],
                            in1=sdhb[:, nh * 512:(nh + 1) * 512], op=ALU.mult),
                            wait=[ev_m[pwidx], eB11, p.ev("dve")], sig="dve")
                        evR10 = p.step("vector", lambda e, ti10=ti10, nh=nh: e.scalar_tensor_tensor(
                            out=R[:, ti10 * DM + nh * 512: ti10 * DM + (nh + 1) * 512],
                            in0=hfs[:, nh * 512:(nh + 1) * 512], scalar=scl2[:, 0:1],
                            in1=R[:, ti10 * DM + nh * 512: ti10 * DM + (nh + 1) * 512],
                            op0=ALU.mult, op1=ALU.add),
                            wait=[eEv, ev_RwD], sig="dve")
                        pwfD[pwidx] = evR10
                        ev_RwD = evR10

    p.barrier()

    if "ssdA" in debug:
        evDS = p.ev("dS")
        dout("dbg_hall", (64, 2048))
        dout("dbg_lcsg", (8, TPC))
        dout("dbg_BT", (128, 4 * TPC), BF16)
        dout("dbg_xtok", (TPC, DI), BF16)
        p.dma("gpsimd", g["dbg_hall"][:, :], g["hall"][:, :], wait=[evDS], sig="dO")
        p.dma("gpsimd", g["dbg_lcsg"][:, :], g["lcsg"][:, :], sig="dO")
        p.dma("gpsimd", g["dbg_BT"][:, :], g["BT"][:, :], sig="dO")
        for i in range(NTT):
            p.dma("gpsimd", g["dbg_xtok"][i * 128:(i + 1) * 128, :],
                  g["xtok_park"][i * 128:(i + 1) * 128, :], sig="dO")

    if "xq1" in debug:
        dout("dbg_xq", (DM, TPC), BF16)
        dout("dbg_xn1", (DM, TPC), BF16)
        for f in range(8):
            p.dma("gpsimd", g["dbg_xq"][f * 128:(f + 1) * 128, :],
                  g["xqT"][:, f * TPC:(f + 1) * TPC], sig="dO")
            p.dma("gpsimd", g["dbg_xn1"][f * 128:(f + 1) * 128, :],
                  g["xnT1"][:, f * TPC:(f + 1) * TPC], sig="dO")

    if "R2" in debug:
        dout("dbg_R2", (TPC, DM))
        for i in range(NTT):
            p.dma("gpsimd", g["dbg_R2"][i * 128:(i + 1) * 128, :],
                  R[:, i * DM:(i + 1) * DM], sig="dO")

    for i0 in range(0, NTT, 4):
        p.dma("gpsimd", cvrows(g["out"], i0 * 128, 4, 0, DM),
              cvcols(R, 4, DM, c0=i0 * DM), sig="dO")
    p.step("gpsimd", lambda e: e.nop(), wait=[p.ev("dO")])

    with nc.Block() as block:
        p.finalize(block)
    for _ar in reversed(getattr(p, "arenas", [])):
        if _ar.handle is not None:
            _ar.free()
    stack.close()
    return nc


def host_constants(core_id):
    c = {}
    c["consts"] = np.tile(np.array([256.0, 0.25, 1.0, 0.5, 4.0, 64.0, 127.0, -128.0],
                                    np.float32), (128, 1))
    c["eye_f32"] = np.eye(128, dtype=np.float32)
    c["ones_f32"] = np.ones((1, 128), dtype=np.float32)
    c["eye_bf"] = np.eye(128, dtype=bfloat16)
    c["onesK"] = np.ones((1, 512), dtype=bfloat16)
    c["trilT"] = np.triu(np.ones((128, 128), dtype=np.float32))
    mmv = np.zeros((DI, NH0), dtype=np.float64)
    for h in range(NH0):
        mmv[h * P0H:(h + 1) * P0H, h] = 1.0 / P0H
    c["dtmean"] = mmv.astype(bfloat16)
    cq = core_id % 4
    pm = np.zeros((8, 8), np.float32)
    for s in range((core_id // 4) * 4, (core_id // 4) * 4 + cq):
        pm[:, s] = 1.0
    c["ssd_premaskT"] = pm[0:8, :]
    c["ssd_maskinvT"] = 1.0 - pm[0:8, :]
    s0 = cq * TPC
    pos = np.arange(s0, s0 + TPC, dtype=np.float64)
    inv_freq = 1.0 / 10000.0 ** (np.arange(0, DH1, 2, dtype=np.float64) / DH1)
    fr = pos[None, :] * inv_freq[:, None]
    emb = np.concatenate([fr, fr], axis=0)
    cos = np.cos(emb); sin = np.sin(emb)
    SC1 = DH1 ** -0.5
    c["cosq_bf"] = np.tile(cos * SC1, (2, 1)).astype(bfloat16)
    c["sinq_bf"] = np.tile(sin * SC1, (2, 1)).astype(bfloat16)
    c["cosk_bf"] = np.tile(cos, (2, 1)).astype(bfloat16)
    c["sink_bf"] = np.tile(sin, (2, 1)).astype(bfloat16)
    P = np.zeros((128, 128), dtype=np.float64)
    for o in range(128):
        d = o % 64
        base = o - d
        if d < 32:
            P[base + d + 32, o] = -1.0
        else:
            P[base + d - 32, o] = 1.0
    c["ropeP"] = P.astype(bfloat16)
    gam = 1.0 - 2.0 ** (-5.0 - np.arange(NH1, dtype=np.float64))
    l = np.arange(CH, dtype=np.float64)
    dm_ = np.zeros((CH, NH1 * CH), dtype=np.float64)
    for h in range(NH1):
        dm_[:, h * CH:(h + 1) * CH] = np.where(l[None, :] >= l[:, None],
                                               gam[h] ** (l[None, :] - l[:, None]), 0.0)
    c["dmaskT"] = dm_.astype(bfloat16)
    c["gfacT"] = (gam[None, :] ** (l[:, None] + 1.0)).astype(np.float32)
    c["gkT"] = (gam[None, :] ** (CH - 1.0 - l[:, None])).astype(np.float32)
    c["gCT"] = np.tile((gam ** CH)[None, :], (64, 1)).astype(np.float32)
    coef = np.zeros((8, NH1), dtype=np.float64)
    for s in range((core_id // 4) * 4, (core_id // 4) * 4 + cq):
        coef[s, :] = gam ** (2048.0 * (cq - 1 - (s % 4)))
    c["ret_coef"] = np.tile(coef.reshape(1, -1), (64, 1)).astype(np.float32)
    return c


def make_inmaps(ins):
    hid = np.ascontiguousarray(np.asarray(ins["L0_attn_hidden"], dtype=np.float32))
    maps = []
    for core in range(8):
        b, q = core // 4, core % 4
        m = {}
        m["hid"] = np.ascontiguousarray(hid[b, q * TPC:(q + 1) * TPC, :])
        if q == 0:
            m["hid_halo"] = np.zeros((3, DM), np.float32)
        else:
            m["hid_halo"] = np.ascontiguousarray(hid[b, q * TPC - 3:q * TPC, :])
        m["w_norm_in"] = np.asarray(ins["L0_attn_w_norm_in"], np.float32).reshape(1, DM)
        m["w_norm_in_t"] = np.asarray(ins["L0_attn_w_norm_in"], np.float32).reshape(8, 128).T.copy()
        m["w_in_proj"] = np.asarray(ins["L0_attn_w_in_proj"], np.float32)
        cw = np.asarray(ins["L0_attn_conv_w"], np.float32).reshape(DI, 4)
        cwr = np.zeros((128, 64), np.float32)
        for t in range(4):
            cwr[:, t * 16:(t + 1) * 16] = cw[:, t].reshape(16, 128).T
        m["conv_w"] = cwr
        m["conv_b"] = np.asarray(ins["L0_attn_conv_b"], np.float32).reshape(16, 128).T.copy()
        m["w_dt"] = np.asarray(ins["L0_attn_w_dt"], np.float32)
        m["b_dt"] = np.asarray(ins["L0_attn_b_dt"], np.float32).reshape(16, 128).T.copy()
        m["A_log"] = np.asarray(ins["L0_attn_A_log"], np.float32).reshape(8, 1)
        m["Dp"] = np.asarray(ins["L0_attn_D"], np.float32).reshape(8, 1)
        m["Dp_row"] = np.asarray(ins["L0_attn_D"], np.float32).reshape(1, 8)
        m["w_norm_out"] = np.asarray(ins["L0_attn_w_norm_out"], np.float32).reshape(1, DI)
        m["w_norm_out_t"] = np.asarray(ins["L0_attn_w_norm_out"], np.float32).reshape(16, 128).T.copy()
        m["w_out_proj"] = np.asarray(ins["L0_attn_w_out_proj"], np.float32)
        m["w1"] = np.asarray(ins["L0_ffn_w1"], np.float32)
        m["b1"] = np.asarray(ins["L0_ffn_b1"], np.float32).reshape(32, 128).T.copy()
        m["b1row"] = np.asarray(ins["L0_ffn_b1"], np.float32).reshape(1, DF)
        m["w2"] = np.asarray(ins["L0_ffn_w2"], np.float32)
        m["b2"] = np.asarray(ins["L0_ffn_b2"], np.float32).reshape(1, DM)
        m["Wq"] = np.asarray(ins["L1_attn_Wq"], np.float32)
        m["Wk"] = np.asarray(ins["L1_attn_Wk"], np.float32)
        m["Wv"] = np.asarray(ins["L1_attn_Wv"], np.float32)
        m["Wg"] = np.asarray(ins["L1_attn_Wg"], np.float32)
        m["Wo"] = np.asarray(ins["L1_attn_Wo"], np.float32)
        m["gn_w"] = np.asarray(ins["L1_attn_gn_w"], np.float32).reshape(1, DM)
        m["gn_b"] = np.asarray(ins["L1_attn_gn_b"], np.float32).reshape(1, DM)
        m["clip_x"] = np.asarray(ins["L1_ffn_clip_x"], np.float32).reshape(1, 1)
        m["clip_down"] = np.asarray(ins["L1_ffn_clip_down"], np.float32).reshape(1, 1)
        m["w_gate"] = np.asarray(ins["L1_ffn_w_gate"], np.float32)
        m["w_up"] = np.asarray(ins["L1_ffn_w_up"], np.float32)
        m["w_down"] = np.asarray(ins["L1_ffn_w_down"], np.float32)
        m.update(host_constants(core))
        maps.append(m)
    return maps


def kernel(debug=(), stage=99, lnsub=9, lntiles=99, **inputs):
    nc = build_program(debug=debug, stage=stage, lnsub=lnsub, lntiles=lntiles)
    maps = make_inmaps(inputs)
    res = run_bass_kernel_spmd(nc, maps, core_ids=list(range(8)))
    outs = [res.results[c]["out"] for c in range(8)]
    full = np.stack([np.concatenate(outs[0:4], axis=0),
                     np.concatenate(outs[4:8], axis=0)])
    if debug:
        dbg = [{k[4:]: v for k, v in res.results[c].items() if k.startswith("dbg_")}
               for c in range(8)]
        return full, dbg
    return full
